# version 03
# baseline (speedup 1.0000x reference)
import math
from contextlib import ExitStack

import numpy as np
import ml_dtypes
import concourse.bass as bass
import concourse.mybir as mybir
from concourse.bass import ts
from concourse.bass_utils import run_bass_kernel_spmd

F32 = mybir.dt.float32
BF16 = mybir.dt.bfloat16
U32 = mybir.dt.uint32
AF = mybir.ActivationFunctionType
ALU = mybir.AluOpType
AX = mybir.AxisListType.X

T = 2048
D = 2048
E = 16
F = 768
NH = 16
NKV = 4
HD = 128
TH = 1024
NTC = 16
SC = 1.0 / math.sqrt(HD)
MASKNEG = -3.0e38
EPS = 1e-5



def _host_consts(core_parity):
    theta = 10000.0
    i = np.arange(64, dtype=np.float64)
    freqs = theta ** (-i / 64.0)
    t = np.arange(T, dtype=np.float64)
    pos = 2.0 * t + core_parity
    ang = pos[None, :] * freqs[:, None]
    cos = np.cos(ang)
    sin = np.sin(ang)
    ropec = np.concatenate([cos, cos], axis=0).astype(ml_dtypes.bfloat16)
    ropes = np.concatenate([sin, sin], axis=0).astype(ml_dtypes.bfloat16)

    cm = np.full((128, 8 * 512), MASKNEG, dtype=np.float32)
    p = np.arange(128)[:, None]
    f = np.arange(512)[None, :]
    for pk in range(2):
        strict = 1 if (core_parity == 0 and pk == 1) else 0
        for jm in range(4):
            keep = f <= (128 * jm + p - strict)
            blk = np.where(keep, np.float32(0.0), np.float32(MASKNEG))
            cm[:, (pk * 4 + jm) * 512:(pk * 4 + jm + 1) * 512] = blk
    cmask = cm.astype(ml_dtypes.bfloat16)

    ident = np.eye(128, dtype=ml_dtypes.bfloat16)
    return ropec, ropes, cmask, ident




class Sched:

    ENGS = ("t", "v", "a", "g", "s")

    def __init__(self):
        self.ops = {e: [] for e in self.ENGS}
        self.cnt = {}

    paranoid = False

    def emit(self, eng, fn):
        if self.paranoid and eng in ("t", "v", "a"):
            cp, cv, ca = self.val("cP"), self.val("cV"), self.val("cA")

            def wrapped(E, _fn=fn, _eng=eng, _cp=cp, _cv=cv, _ca=ca):
                if _cp: E[_eng].wait_ge(E["sem"]["cP"], _cp)
                if _cv: E[_eng].wait_ge(E["sem"]["cV"], _cv)
                if _ca: E[_eng].wait_ge(E["sem"]["cA"], _ca)
                _fn(E)
            self.ops[eng].append(wrapped)
        else:
            self.ops[eng].append(fn)

    def inc(self, name, n=1):
        self.cnt[name] = self.cnt.get(name, 0) + n
        return self.cnt[name]

    def val(self, name):
        return self.cnt.get(name, 0)


def _wait(S, eng, sem_name, value):
    if value <= 0:
        return
    def f(E, _sn=sem_name, _v=value):
        E[eng].wait_ge(E["sem"][_sn], _v)
    S.emit(eng, f)




def build_nc(dbg=False, n_exp=E, stop_after=None, n_qt=16, no_mask=False, no_resc=False, fl_lvl=8, paranoid=0, evac_act=True):
    EVAC_ACT = evac_act
    nc = bass.Bass("TRN2", target_bir_lowering=False)

    x0 = nc.dram_tensor("x0", [T, D], F32, kind="ExternalInput").ap()
    wqkv = nc.dram_tensor("wqkv", [D, 3072], F32, kind="ExternalInput").ap()
    wo = nc.dram_tensor("wo", [D, D], F32, kind="ExternalInput").ap()
    rw = nc.dram_tensor("rw", [D, E], F32, kind="ExternalInput").ap()
    gw = nc.dram_tensor("gw", [E, D, F], F32, kind="ExternalInput").ap()
    uw = nc.dram_tensor("uw", [E, D, F], F32, kind="ExternalInput").ap()
    dw = nc.dram_tensor("dw", [E, F, D], F32, kind="ExternalInput").ap()
    ropec_in = nc.dram_tensor("ropec", [128, T], BF16, kind="ExternalInput").ap()
    ropes_in = nc.dram_tensor("ropes", [128, T], BF16, kind="ExternalInput").ap()
    cmask_in = nc.dram_tensor("cmask", [128, 4096], BF16, kind="ExternalInput").ap()
    ident_in = nc.dram_tensor("ident", [128, 128], BF16, kind="ExternalInput").ap()
    nonce_in = nc.dram_tensor("nonce", [1, 16], U32, kind="ExternalInput").ap()

    out = nc.dram_tensor("out", [T, D], F32, kind="ExternalOutput").ap()

    qkv_kind = "ExternalOutput" if stop_after == "qkv" else "Internal"
    q_dram = nc.dram_tensor("q_dram", [D, T], BF16, kind=qkv_kind).ap()
    k_dram = nc.dram_tensor("k_dram", [512, T], BF16, kind=qkv_kind).ap()
    v_dram = nc.dram_tensor("v_dram", [T, 512], BF16, kind=qkv_kind).ap()
    x1_kind = "ExternalOutput" if dbg else "Internal"
    x1_dram = nc.dram_tensor("x1_dram", [T, D], F32, kind=x1_kind).ap()
    sh_k = nc.dram_tensor("sh_k", [1024, T], BF16, kind="Internal",
                          addr_space="Shared").ap()
    sh_v = nc.dram_tensor("sh_v", [2 * T, 512], BF16, kind="Internal",
                          addr_space="Shared").ap()
    sh_flag = nc.dram_tensor("sh_flag", [2, 16], U32, kind="Internal",
                             addr_space="Shared").ap()
    if dbg:
        dbg_x1 = nc.dram_tensor("dbg_x1", [T, D], F32, kind="ExternalOutput").ap()
    if stop_after == "ffn1":
        dbg_sc = nc.dram_tensor("dbg_sc", [128, 512], F32, kind="ExternalOutput").ap()
        dbg_xn = nc.dram_tensor("dbg_xn", [128, 2048], BF16, kind="ExternalOutput").ap()
        dbg_cn = nc.dram_tensor("dbg_cn", [128, 1024], F32, kind="ExternalOutput").ap()

    SBASE = 16928

    def A(name, shape, dtype, offset):
        return nc.alloc_sbuf_tensor_at(name, shape, dtype,
                                       offset=SBASE + offset)
    x_sb = A("x_sb", [128, NTC * 2048], F32, offset=0)
    wo_sb = A("wo_sb", [128, 16 * 2048], BF16, offset=0)
    cmask_sb = A("cmask_sb", [128, 4096], BF16, offset=65536)
    qbuf = [A(f"qbuf{i}", [128, 2048], BF16, offset=73728 + 4096 * i)
            for i in range(2)]
    kbuf = [A(f"kbuf{i}", [128, 512], BF16, offset=81920 + 1024 * i)
            for i in range(2)]
    vbuf = [A(f"vbuf{i}", [128, 512], BF16, offset=83968 + 1024 * i)
            for i in range(2)]
    psb = [A(f"psb{i}", [128, 512], BF16, offset=86016 + 1024 * i)
           for i in range(2)]
    ptsb = [A(f"ptsb{i}", [128, 512], BF16, offset=88064 + 1024 * i)
            for i in range(2)]
    otok = A("otok", [128, 2048], BF16, offset=90112)
    ofm = A("ofm", [128, 2048], BF16, offset=94208)
    x1t = [A(f"x1t{i}", [128, 2048], F32, offset=98304 + 8192 * i)
           for i in range(2)]

    xn_sb = A("xn_sb", [128, 16 * TH], BF16, offset=131072)

    CA = 163840
    gu = [A(f"gu{i}", [128, 4096], BF16, offset=CA + 8192 * i)
          for i in range(2)]
    dwq = [A(f"dwq{i}", [128, 3072], BF16, offset=CA + 16384 + 6144 * i)
           for i in range(2)]
    h_all = A("h_all", [128, 6 * TH], BF16, offset=CA + 28672)
    xn_tok_f = A("xn_tok_f", [128, 2048], BF16, offset=CA + 28672)
    xn_tok_q = A("xn_tok_q", [128, 2048], BF16, offset=CA + 0)
    silu_sb = A("silu_sb", [128, 512], BF16, offset=211392)
    tmp_ev = A("tmp_ev", [128, 512], F32, offset=CA + 40960)
    wqb = [A(f"wqb{i}", [128, 2048], BF16, offset=CA + 4096 * i)
           for i in range(2)]
    wv_sb = A("wv_sb", [128, 16 * 512], BF16, offset=CA + 8192)
    ropec_sb = A("ropec_sb", [128, T], BF16, offset=CA + 24576)
    ropes_sb = A("ropes_sb", [128, T], BF16, offset=CA + 28672)
    stq = [A(f"stq{i}", [128, 512], BF16, offset=CA + 32768 + 1024 * i)
           for i in range(2)]
    stv = [A(f"stv{i}", [128, 512], BF16, offset=CA + 34816 + 1024 * i)
           for i in range(2)]
    rta = A("rta", [128, 512], F32, offset=CA + 36864)
    rtb = A("rtb", [128, 512], F32, offset=CA + 38912)
    rth = A("rth", [128, 512], F32, offset=CA + 40960)

    sm = [206848]

    def SA(name, shape, dtype):
        t = A(name, shape, dtype, offset=sm[0])
        nbytes = shape[1] * (4 if dtype in (F32, U32) else 2)
        sm[0] += (nbytes + 31) // 32 * 32
        return t

    ident_sb = SA("ident_sb", [128, 128], BF16)
    rw_sb = SA("rw_sb", [128, 256], BF16)
    scores_sb = SA("scores_sb", [128, 256], F32)
    c_tok = SA("c_tok", [128, 256], F32)
    ln_sum = SA("ln_sum", [128, 1], F32)
    ln_negm = SA("ln_negm", [128, 1], F32)
    ln_var = SA("ln_var", [128, 1], F32)
    ln_r = SA("ln_r", [128, 1], F32)
    ln_mr = SA("ln_mr", [128, 1], F32)
    r2_m1 = SA("r2_m1", [128, 1], F32)
    r2_m2 = SA("r2_m2", [128, 1], F32)
    r2_t = SA("r2_t", [128, 16], F32)
    r2_s2 = SA("r2_s2", [128, 16], F32)
    r2_c1 = SA("r2_c1", [128, 16], F32)
    r2_e = SA("r2_e", [128, 1], F32)
    r2_w2 = SA("r2_w2", [128, 1], F32)
    r2_w1 = SA("r2_w1", [128, 1], F32)
    fl_m = [SA(f"fl_m{i}", [128, 1], F32) for i in range(4)]
    fl_l = [SA(f"fl_l{i}", [128, 1], F32) for i in range(4)]
    fl_mn = [SA(f"fl_mn{i}", [128, 1], F32) for i in range(2)]
    fl_dm = [SA(f"fl_dm{i}", [128, 1], F32) for i in range(2)]
    fl_al = [SA(f"fl_al{i}", [128, 1], F32) for i in range(2)]
    fl_bias = [SA(f"fl_bias{i}", [128, 1], F32) for i in range(2)]
    fl_lp = [SA(f"fl_lp{i}", [128, 1], F32) for i in range(2)]
    fl_rl = [SA(f"fl_rl{i}", [128, 1], F32) for i in range(4)]
    vdmy = SA("vdmy", [128, 1], F32)
    admy = SA("admy", [128, 1], F32)
    mx8 = SA("mx8", [128, 8], F32)
    r2_a1 = SA("r2_a1", [128, 16], F32)
    r2_a3 = SA("r2_a3", [128, 16], F32)
    r2_sg1 = SA("r2_sg1", [128, 16], F32)
    r2_sg3 = SA("r2_sg3", [128, 16], F32)
    nonce_sb = SA("nonce_sb", [1, 16], U32)
    flag_sb = SA("flag_sb", [1, 16], U32)
    assert sm[0] <= 211392 and SBASE + 211392 + 1024 <= 229344, sm[0]

    S = Sched()
    if paranoid >= 3:
        S.paranoid = True
    state = {}
    S_KV_PUB = [0]

    psum_stack = ExitStack()

    def psum_alloc(name, shape, dtype=F32):
        return psum_stack.enter_context(nc.psum_tensor(name, shape, dtype))


    def phase0():
        for tc in range(NTC):
            def f(Ee, tc=tc):
                Ee["s"].dma_start(
                    out=x_sb[:, tc * 2048:(tc + 1) * 2048],
                    in_=x0[tc * 128:(tc + 1) * 128, :],
                ).then_inc(Ee["sem"]["dS"], 16)
            S.emit("s", f)
            S.inc("dS", 16)
        for name, dst, src in (
            ("ident", ident_sb, ident_in),
            ("nonce", nonce_sb, nonce_in),
        ):
            def f(Ee, dst=dst, src=src):
                Ee["s"].dma_start(out=dst[:], in_=src).then_inc(Ee["sem"]["dS"], 16)
            S.emit("s", f)
            S.inc("dS", 16)
        def f(Ee):
            for dc in range(16):
                Ee["g"].dma_start(
                    out=rw_sb[:, dc * 16:(dc + 1) * 16],
                    in_=rw[dc * 128:(dc + 1) * 128, :],
                ).then_inc(Ee["sem"]["dG"], 16)
        S.emit("g", f)
        S.inc("dG", 256)
        def finit_v(Ee):
            Ee["v"].memset(vdmy[:], 0.0)
        S.emit("v", finit_v)

        def finit_a(Ee):
            Ee["a"].memzero(admy[:])
        S.emit("a", finit_a)
        state["x_loaded_dS"] = S.val("dS")
        state["consts_dG"] = S.val("dG")

    def load_gu(e, fc, slot, wait_cP):
        _wait(S, "g", "cP", wait_cP)
        def f(Ee, e=e, fc=fc, slot=slot):
            for w_, base in ((gw, 0), (uw, 2048)):
                wap = w_[e, :, fc * 128:(fc + 1) * 128].rearrange(
                    "(dc p) f -> p dc f", p=128)
                for hh in range(2):
                    Ee["g"].dma_start(
                        out=gu[slot][:, base + hh * 1024:base + (hh + 1) * 1024],
                        in_=wap[:, hh * 8:(hh + 1) * 8, :],
                    ).then_inc(Ee["sem"]["dG"], 16)
        S.emit("g", f)
        return S.inc("dG", 64)

    def load_dwq(e, dq, slot, wait_cP):
        _wait(S, "g", "cP", wait_cP)
        def f(Ee, e=e, dq=dq, slot=slot):
            Ee["g"].dma_start(
                out=dwq[slot][:],
                in_=dw[e][:, dq * 512:(dq + 1) * 512]
                .rearrange("(fc p) d -> p fc d", p=128),
            ).then_inc(Ee["sem"]["dG"], 16)
        S.emit("g", f)
        return S.inc("dG", 16)

    def emit_ln(h, ps_tr, xn_tok):
        _wait(S, "v", "dS", state["x_loaded_dS"])
        entry_cp = S.val("cP")
        _wait(S, "a", "cP", entry_cp)
        _wait(S, "v", "cP", entry_cp)
        for tl in range(8):
            tcg = h * 8 + tl
            xt = x_sb[:, tcg * 2048:(tcg + 1) * 2048]

            def fv1(Ee, xt=xt):
                E = Ee
                E["v"].reduce_sum(ln_sum[:], xt, AX)
                E["v"].tensor_copy(vdmy[:], vdmy[:])
                E["v"].tensor_scalar(ln_negm[:], ln_sum[:], -1.0 / D, None,
                                     ALU.mult)
                E["v"].tensor_copy(vdmy[:], vdmy[:]).then_inc(E["sem"]["cV"], 1)
            S.emit("v", fv1)
            cv1 = S.inc("cV")
            _wait(S, "a", "cV", cv1)

            def fa1(Ee, xt=xt, xn_tok=xn_tok):
                Ee["a"].activation(xn_tok[:], xt, AF.Square,
                                   bias=ln_negm[:], scale=1.0)
                Ee["a"].activation(admy[:], admy[:], AF.Identity
                                   ).then_inc(Ee["sem"]["cA"], 1)
            S.emit("a", fa1)
            ca1 = S.inc("cA")
            _wait(S, "v", "cA", ca1)

            def fv2(Ee, xn_tok=xn_tok):
                E = Ee
                E["v"].reduce_sum(ln_var[:], xn_tok[:], AX)
                E["v"].tensor_copy(vdmy[:], vdmy[:])
                E["v"].tensor_scalar(ln_var[:], ln_var[:], 1.0 / D, EPS,
                                     ALU.mult, ALU.add)
                E["v"].tensor_copy(vdmy[:], vdmy[:]).then_inc(E["sem"]["cV"], 1)
            S.emit("v", fv2)
            cv2 = S.inc("cV")
            _wait(S, "a", "cV", cv2)

            def fa2(Ee):
                E = Ee
                E["a"].activation(ln_r[:], ln_var[:], AF.Ln)
                E["a"].activation(ln_r[:], ln_r[:], AF.Exp, scale=-0.5)
                E["a"].activation(admy[:], admy[:], AF.Identity
                                  ).then_inc(E["sem"]["cA"], 1)
            S.emit("a", fa2)
            ca2 = S.inc("cA")
            _wait(S, "v", "cA", ca2)

            def fv3(Ee):
                Ee["v"].tensor_mul(ln_mr[:], ln_negm[:], ln_r[:])
                Ee["v"].tensor_copy(vdmy[:], vdmy[:]).then_inc(Ee["sem"]["cV"], 1)
            S.emit("v", fv3)
            cv3 = S.inc("cV")
            _wait(S, "a", "cV", cv3)

            def fa3(Ee, xt=xt, xn_tok=xn_tok):
                Ee["a"].activation(xn_tok[:], xt, AF.Identity,
                                   bias=ln_mr[:], scale=ln_r[:])
                Ee["a"].activation(admy[:], admy[:], AF.Identity
                                   ).then_inc(Ee["sem"]["cA"], 1)
            S.emit("a", fa3)
            ca3 = S.inc("cA")

            for g4 in range(4):
                _wait(S, "t", "cA", ca3)
                _wait(S, "t", "dS", state["x_loaded_dS"])
                _wait(S, "t", "cV", state.get("tr_evac_cV", 0))

                def ft(Ee, g4=g4, ps_tr=ps_tr, xn_tok=xn_tok):
                    E = Ee
                    for j in range(4):
                        dc = g4 * 4 + j
                        ins = E["t"].transpose(
                            ps_tr[:, j * 128:(j + 1) * 128],
                            xn_tok[:, dc * 128:(dc + 1) * 128],
                            ident_sb[:],
                        )
                        if j == 3:
                            ins.then_inc(E["sem"]["cP"], 1)
                S.emit("t", ft)
                cp = S.inc("cP")
                _wait(S, "v", "cP", cp)

                def fv4(Ee, g4=g4, tl=tl, ps_tr=ps_tr):
                    E = Ee
                    dst = xn_sb.ap().rearrange(
                        "p (dc t) -> p dc t", t=TH
                    )[:, 4 * g4:4 * g4 + 4, tl * 128:(tl + 1) * 128]
                    src = ps_tr.ap().rearrange("p (j t) -> p j t", t=128)
                    E["v"].tensor_copy(dst, src)
                    E["v"].tensor_copy(vdmy[:], vdmy[:]).then_inc(E["sem"]["cV"], 1)
                S.emit("v", fv4)
                state["tr_evac_cV"] = S.inc("cV")
        state["xn_cV"] = S.val("cV")

    def emit_router(h, ps_sc):
        _wait(S, "t", "dG", state["consts_dG"])
        _wait(S, "t", "cV", state["xn_cV"])
        for tl in range(8):
            tcg = h * 8 + tl

            def ft(Ee, tl=tl):
                E = Ee
                for dc in range(16):
                    ins = E["t"].matmul(
                        ps_sc[:, tl * 16:(tl + 1) * 16],
                        xn_sb[:, dc * TH + tl * 128:dc * TH + (tl + 1) * 128],
                        rw_sb[:, dc * 16:(dc + 1) * 16],
                        start=(dc == 0), stop=(dc == 15),
                    )
                    if dc == 15:
                        ins.then_inc(E["sem"]["cP"], 1)
            S.emit("t", ft)
            S.inc("cP")
        cp = S.val("cP")
        _wait(S, "v", "cP", cp)

        def fv(Ee, h=h):
            Ee["v"].tensor_copy(scores_sb[:, h * 128:(h + 1) * 128],
                                ps_sc[:, 0:128])
            Ee["v"].tensor_copy(vdmy[:], vdmy[:]).then_inc(Ee["sem"]["cV"], 1)
        S.emit("v", fv)
        S.inc("cV")
        for tl in range(8):
            tcg = h * 8 + tl
            s_sl = scores_sb[:, tcg * 16:(tcg + 1) * 16]

            def fv1(Ee, s_sl=s_sl):
                E = Ee
                v = E["v"]
                v.tensor_copy(vdmy[:], vdmy[:])
                v.max(mx8[:], s_sl)
                v.tensor_copy(vdmy[:], vdmy[:])
                v.tensor_scalar(r2_a1[:], s_sl, mx8[:, 1:2], None, ALU.subtract)
                v.tensor_scalar(r2_a3[:], s_sl, mx8[:, 2:3], None, ALU.subtract)
                v.tensor_sub(r2_e[:], mx8[:, 0:1], mx8[:, 1:2])
                v.tensor_copy(vdmy[:], vdmy[:]).then_inc(E["sem"]["cV"], 1)
            S.emit("v", fv1)
            cv = S.inc("cV")
            _wait(S, "a", "cV", cv)

            def fa(Ee):
                E = Ee
                E["a"].activation(r2_sg1[:], r2_a1[:], AF.Sign)
                E["a"].activation(r2_sg3[:], r2_a3[:], AF.Sign)
                E["a"].activation(r2_e[:], r2_e[:], AF.Exp)
                E["a"].activation(admy[:], admy[:], AF.Identity
                                  ).then_inc(E["sem"]["cA"], 1)
            S.emit("a", fa)
            ca = S.inc("cA")
            _wait(S, "v", "cA", ca)

            def fv2(Ee, s_sl=s_sl, tcg=tcg):
                E = Ee
                v = E["v"]
                v.tensor_scalar(r2_e[:], r2_e[:], 1.0, None, ALU.add)
                v.tensor_copy(vdmy[:], vdmy[:])
                v.reciprocal(r2_w2[:], r2_e[:])
                v.tensor_copy(vdmy[:], vdmy[:])
                v.tensor_scalar(r2_w1[:], r2_w2[:], -1.0, 0.5, ALU.mult,
                                ALU.add)
                v.tensor_scalar(r2_w2[:], r2_w2[:], 0.5, None, ALU.mult)
                v.tensor_copy(vdmy[:], vdmy[:])
                v.tensor_scalar(r2_c1[:], r2_sg1[:], 0.0, r2_w1[:],
                                ALU.max, ALU.mult)
                v.tensor_scalar(r2_t[:], r2_sg3[:], 0.0, r2_w2[:],
                                ALU.max, ALU.mult)
                v.tensor_copy(vdmy[:], vdmy[:])
                v.tensor_add(c_tok[:, tcg * 16:(tcg + 1) * 16],
                             r2_c1[:], r2_t[:])
                v.tensor_copy(vdmy[:], vdmy[:]).then_inc(E["sem"]["cV"], 1)
            S.emit("v", fv2)
            S.inc("cV")
        state["ctok_cV"] = S.val("cV")

    def emit_experts(h, ps_gs, ps_u, daccs, tmp_evs):
        gu_last_use = [0, 0]
        gu_loaded = [0, 0]
        silu_ca = [0, 0]
        h_cv = 0
        dacc_free = [0, 0]
        tmp_free = [0, 0]
        dw_last_use = [state.get("arena_free_cP", 0)] * 2
        dw_loaded = [0, 0]
        gseq = 0
        dwseq = 0
        it = 0
        entry = state.get("arena_free_cP", 0)
        gu_loaded[0] = load_gu(0, 0, 0, entry)
        gu_loaded[1] = load_gu(0, 1, 1, entry)
        dw_loaded[0] = load_dwq(0, 0, 0, entry)
        dw_loaded[1] = load_dwq(0, 1, 1, entry)
        for e in range(n_exp):
            for fc in range(6):
                slot = gseq % 2
                _wait(S, "t", "dG", gu_loaded[slot])
                _wait(S, "t", "cV", state["xn_cV"])
                for tb in range(2):
                    gslot = it % 2
                    it += 1
                    pg = ps_gs[gslot]
                    _wait(S, "t", "cA", silu_ca[gslot])

                    def ftg(Ee, slot=slot, tb=tb, pg=pg):
                        E = Ee
                        for dc in range(16):
                            ins = E["t"].matmul(
                                pg[:],
                                gu[slot][:, dc * 128:(dc + 1) * 128],
                                xn_sb[:, dc * TH + tb * 512:dc * TH + tb * 512 + 512],
                                start=(dc == 0), stop=(dc == 15),
                            )
                            if dc == 15:
                                ins.then_inc(E["sem"]["cP"], 1)
                    S.emit("t", ftg)
                    cp_g = S.inc("cP")
                    _wait(S, "t", "cV", h_cv)

                    def ftu(Ee, slot=slot, tb=tb, ps_u=ps_u):
                        E = Ee
                        for dc in range(16):
                            ins = E["t"].matmul(
                                ps_u[:],
                                gu[slot][:, 2048 + dc * 128:2048 + (dc + 1) * 128],
                                xn_sb[:, dc * TH + tb * 512:dc * TH + tb * 512 + 512],
                                start=(dc == 0), stop=(dc == 15),
                            )
                            if dc == 15:
                                ins.then_inc(E["sem"]["cP"], 1)
                    S.emit("t", ftu)
                    cp_u = S.inc("cP")
                    _wait(S, "a", "cP", cp_g)
                    _wait(S, "a", "cV", h_cv)

                    def fa(Ee, pg=pg):
                        Ee["a"].activation(silu_sb[:], pg[:], AF.Silu)
                        Ee["a"].activation(admy[:], admy[:], AF.Identity
                                           ).then_inc(Ee["sem"]["cA"], 1)
                    S.emit("a", fa)
                    silu_ca[gslot] = S.inc("cA")
                    _wait(S, "v", "cA", silu_ca[gslot])
                    _wait(S, "v", "cP", cp_u)

                    def fv(Ee, fc=fc, tb=tb, ps_u=ps_u):
                        Ee["v"].tensor_mul(
                            h_all[:, fc * TH + tb * 512:fc * TH + tb * 512 + 512],
                            ps_u[:], silu_sb[:])
                        Ee["v"].tensor_copy(vdmy[:], vdmy[:]
                                            ).then_inc(Ee["sem"]["cV"], 1)
                    S.emit("v", fv)
                    h_cv = S.inc("cV")
                gu_last_use[slot] = S.val("cP")
                ne, nfc = (e, fc + 2) if fc < 4 else (e + 1, fc - 4)
                if ne < n_exp:
                    gu_loaded[slot] = load_gu(ne, nfc, slot, gu_last_use[slot])
                gseq += 1
            _wait(S, "t", "cV", h_cv)
            for dq in range(4):
                slot = dwseq % 2
                dwseq += 1
                _wait(S, "t", "dG", dw_loaded[slot])
                for tl in range(8):
                    tcg = h * 8 + tl
                    dslot = tl % 2
                    dacc = daccs[dslot]
                    _wait(S, "t", "cA" if EVAC_ACT else "cV",
                          dacc_free[dslot])

                    def ft(Ee, tl=tl, slot=slot, dacc=dacc):
                        E = Ee
                        for fc in range(6):
                            ins = E["t"].matmul(
                                dacc[:],
                                h_all[:, fc * TH + tl * 128:fc * TH + (tl + 1) * 128],
                                dwq[slot][:, fc * 512:(fc + 1) * 512],
                                start=(fc == 0), stop=(fc == 5),
                            )
                            if fc == 5:
                                ins.then_inc(E["sem"]["cP"], 1)
                    S.emit("t", ft)
                    cp_d = S.inc("cP")
                    if EVAC_ACT:
                        _wait(S, "a", "cP", cp_d)
                        _wait(S, "a", "cV", state["ctok_cV"])
                        _wait(S, "a", "cV", tmp_free[dslot])

                        def fa(Ee, e=e, tcg=tcg, dacc=dacc,
                               tmp=tmp_evs[dslot]):
                            Ee["a"].activation(
                                tmp[:], dacc[:], AF.Identity,
                                scale=c_tok[:, tcg * 16 + e:tcg * 16 + e + 1])
                            Ee["a"].activation(admy[:], admy[:], AF.Identity
                                               ).then_inc(Ee["sem"]["cA"], 1)
                        S.emit("a", fa)
                        dacc_free[dslot] = S.inc("cA")
                        _wait(S, "v", "cA", dacc_free[dslot])

                        def fv(Ee, tcg=tcg, dq=dq, tmp=tmp_evs[dslot]):
                            E = Ee
                            xs = x_sb[:, tcg * 2048 + dq * 512:
                                      tcg * 2048 + (dq + 1) * 512]
                            E["v"].tensor_add(xs, tmp[:], xs)
                            E["v"].tensor_copy(vdmy[:], vdmy[:]).then_inc(
                                E["sem"]["cV"], 1)
                        S.emit("v", fv)
                        tmp_free[dslot] = S.inc("cV")
                    else:
                        _wait(S, "v", "cP", cp_d)
                        _wait(S, "v", "cV", state["ctok_cV"])

                        def fv(Ee, e=e, tcg=tcg, dq=dq, dacc=dacc):
                            E = Ee
                            E["v"].tensor_scalar(
                                tmp_ev[:], dacc[:],
                                c_tok[:, tcg * 16 + e:tcg * 16 + e + 1], None,
                                ALU.mult)
                            E["v"].tensor_copy(vdmy[:], vdmy[:])
                            xs = x_sb[:, tcg * 2048 + dq * 512:
                                      tcg * 2048 + (dq + 1) * 512]
                            E["v"].tensor_add(xs, tmp_ev[:], xs)
                            E["v"].tensor_copy(vdmy[:], vdmy[:]).then_inc(
                                E["sem"]["cV"], 1)
                        S.emit("v", fv)
                        dacc_free[dslot] = S.inc("cV")
                        tmp_free[dslot] = dacc_free[dslot]
                dw_last_use[slot] = S.val("cP")
                nde, ndq = (e, dq + 2) if dq < 2 else (e + 1, dq - 2)
                if nde < n_exp:
                    dw_loaded[slot] = load_dwq(nde, ndq, slot,
                                               dw_last_use[slot])
        state["ffn_done_cV"] = S.val("cV")
        state["ffn_done_cP"] = S.val("cP")
        state["ffn_done_cA"] = S.val("cA")

    def emit_qkv(h, ps_qk, ps_v, par_sv):
        entry_cp = S.val("cP")
        wq_loaded = [0, 0]
        wq_last_use = [entry_cp, entry_cp]
        rope_cv = 0
        stq_stored = [0, 0]
        stv_stored = [0, 0]
        for mi in range(20):
            slot = mi % 2
            _wait(S, "g", "cP", wq_last_use[slot])
            def fg(Ee, mi=mi, slot=slot):
                wap = wqkv[:, mi * 128:(mi + 1) * 128].rearrange(
                    "(dc p) f -> p dc f", p=128)
                for hh in range(2):
                    Ee["g"].dma_start(
                        out=wqb[slot][:, hh * 1024:(hh + 1) * 1024],
                        in_=wap[:, hh * 8:(hh + 1) * 8, :],
                    ).then_inc(Ee["sem"]["dG"], 16)
            S.emit("g", fg)
            wq_loaded[slot] = S.inc("dG", 32)
            for tb in range(2):
                pp = (mi * 2 + tb) % 2
                _wait(S, "t", "dG", wq_loaded[slot])
                _wait(S, "t", "cV", rope_cv if (mi * 2 + tb) >= 2 else state.get("qkv_start_cV", 0))
                def ft(Ee, slot=slot, tb=tb, pp=pp, ps_qk=ps_qk):
                    E = Ee
                    for dc in range(16):
                        ins = E["t"].matmul(
                            ps_qk[pp][:],
                            wqb[slot][:, dc * 128:(dc + 1) * 128],
                            xn_sb[:, dc * TH + tb * 512:dc * TH + tb * 512 + 512],
                            start=(dc == 0), stop=(dc == 15),
                        )
                        if dc == 15:
                            ins.then_inc(E["sem"]["cP"], 1)
                S.emit("t", ft)
                wq_last_use[slot] = S.inc("cP")
                cp = wq_last_use[slot]
                _wait(S, "v", "cP", cp)
                _wait(S, "v", "dS", state["rope_dS"])
                _wait(S, "v", "dS", stq_stored[pp])
                toff = h * TH + tb * 512

                def fv(Ee, pp=pp, toff=toff, ps_qk=ps_qk):
                    E = Ee
                    v = E["v"]
                    v.tensor_mul(rtb[:], ps_qk[pp][:],
                                 ropes_sb[:, toff:toff + 512])
                    v.tensor_mul(rta[:], ps_qk[pp][:],
                                 ropec_sb[:, toff:toff + 512])
                    v.tensor_copy(rth[0:64, :], rtb[64:128, :])
                    v.tensor_copy(vdmy[:], vdmy[:])
                    v.tensor_sub(stq[pp][0:64, :], rta[0:64, :], rth[0:64, :])
                    v.tensor_copy(rth[0:64, :], rta[64:128, :])
                    v.tensor_copy(vdmy[:], vdmy[:])
                    v.tensor_add(stq[pp][64:128, :], rtb[0:64, :],
                                 rth[0:64, :])
                    v.tensor_copy(vdmy[:], vdmy[:]).then_inc(E["sem"]["cV"], 1)
                S.emit("v", fv)
                rope_cv = S.inc("cV")
                _wait(S, "s", "cV", rope_cv)

                def fs(Ee, mi=mi, pp=pp, toff=toff, par_sv_key="par"):
                    E = Ee
                    if mi < 16:
                        E["s"].dma_start(
                            out=q_dram[mi * 128:(mi + 1) * 128,
                                       toff:toff + 512],
                            in_=stq[pp][:],
                        ).then_inc(E["sem"]["dS"], 16)
                    else:
                        hg = mi - 16
                        E["s"].dma_start(
                            out=k_dram[hg * 128:(hg + 1) * 128,
                                       toff:toff + 512],
                            in_=stq[pp][:],
                        ).then_inc(E["sem"]["dS"], 16)
                S.emit("s", fs)
                stq_stored[pp] = S.inc("dS", 16)
        _wait(S, "g", "cP", max(wq_last_use))
        def fgv(Ee):
            wap = wqkv[:, 2560:3072].rearrange("(dc p) f -> p dc f", p=128)
            for hh in range(2):
                Ee["g"].dma_start(
                    out=wv_sb[:, hh * 4096:(hh + 1) * 4096],
                    in_=wap[:, hh * 8:(hh + 1) * 8, :],
                ).then_inc(Ee["sem"]["dG"], 16)
        S.emit("g", fgv)
        wv_loaded = S.inc("dG", 32)
        v_evac_cv = 0
        for tl in range(8):
            tcg = h * 8 + tl
            pp = tl % 2
            _wait(S, "t", "dG", wv_loaded)
            _wait(S, "t", "cV", v_evac_cv if tl >= 2 else state["qkv_start_cV"])
            def ft(Ee, tl=tl, pp=pp, ps_v=ps_v):
                E = Ee
                for dc in range(16):
                    ins = E["t"].matmul(
                        ps_v[pp][:],
                        xn_sb[:, dc * TH + tl * 128:dc * TH + (tl + 1) * 128],
                        wv_sb[:, dc * 512:(dc + 1) * 512],
                        start=(dc == 0), stop=(dc == 15),
                    )
                    if dc == 15:
                        ins.then_inc(E["sem"]["cP"], 1)
            S.emit("t", ft)
            cp = S.inc("cP")
            _wait(S, "v", "cP", cp)
            _wait(S, "v", "dS", stv_stored[pp])

            def fv(Ee, pp=pp, ps_v=ps_v):
                Ee["v"].tensor_copy(stv[pp][:], ps_v[pp][:])
                Ee["v"].tensor_copy(vdmy[:], vdmy[:]).then_inc(
                    Ee["sem"]["cV"], 1)
            S.emit("v", fv)
            v_evac_cv = S.inc("cV")
            _wait(S, "s", "cV", v_evac_cv)

            def fs(Ee, tcg=tcg, pp=pp):
                Ee["s"].dma_start(
                    out=v_dram[tcg * 128:(tcg + 1) * 128, :],
                    in_=stv[pp][:],
                ).then_inc(Ee["sem"]["dS"], 16)
            S.emit("s", fs)
            stv_stored[pp] = S.inc("dS", 16)
        state["kv_stored_dS"] = S.val("dS")

    def emit_x1_spill(h, to_dbg=False):
        for tl in range(8):
            tcg = h * 8 + tl
            def fs(Ee, tcg=tcg, to_dbg=to_dbg):
                E = Ee
                E["s"].dma_start(
                    out=x1_dram[tcg * 128:(tcg + 1) * 128, :],
                    in_=x_sb[:, tcg * 2048:(tcg + 1) * 2048],
                ).then_inc(E["sem"]["dS"], 16)
                if to_dbg and dbg:
                    E["s"].dma_start(
                        out=dbg_x1[tcg * 128:(tcg + 1) * 128, :],
                        in_=x_sb[:, tcg * 2048:(tcg + 1) * 2048],
                    ).then_inc(E["sem"]["dS"], 16)
            S.emit("s", fs)
            S.inc("dS", 32 if (to_dbg and dbg) else 16)
        state[f"x1_spill_{h}_dS"] = S.val("dS")

    def emit_flag_sync():
        _wait(S, "s", "dS", state["kv_stored_dS"])

        def fs(Ee):
            E = Ee
            s = E["s"]
            s.dma_start(out=sh_k[ts(state["par"], 512), :], in_=k_dram,
                        ).then_inc(E["sem"]["dS"], 16)
            s.dma_start(out=sh_v[ts(state["par"], T), :], in_=v_dram,
                        ).then_inc(E["sem"]["dS"], 16)
            s.wait_ge(E["sem"]["dS"], S_KV_PUB[0])
            s.dma_start(out=sh_flag[ts(state["par"], 1), :],
                        in_=nonce_sb[:]).then_inc(E["sem"]["dS"], 16)
            rn = s.alloc_register("rn")
            rf = s.alloc_register("rf")
            rd = s.alloc_register("rd")
            rc = s.alloc_register("rc")
            s.load(rn, nonce_sb[0:1, 0:1])
            s.reg_mov(rd, 1)
            s.reg_mov(rc, 0)
            with s.While(rd):
                s.reg_add(rc, rc, 16)
                s.dma_start(out=flag_sb[:],
                            in_=sh_flag[ts(state["parx"], 1), :]
                            ).then_inc(E["sem"]["dP"], 16)
                s.wait_ge(E["sem"]["dP"], rc)
                s.load(rf, flag_sb[0:1, 0:1])
                s.reg_alu(rd, rf, rn, ALU.bitwise_xor)
            s.sem_inc(E["sem"]["dS"], 16)
        S_KV_PUB[0] = S.val("dS") + 32
        S.emit("s", fs)
        S.inc("dS", 64)
        state["kv_ready_dS"] = S.val("dS")

    def emit_attn_overlay_loads():
        _wait(S, "g", "dS", state["x1_spill_0_dS"])
        _wait(S, "g", "cA", state["qkv_end_cA"])
        _wait(S, "g", "cV", state["qkv_end_cV"])
        def fg(Ee):
            E = Ee
            for q4 in range(4):
                E["g"].dma_start(
                    out=wo_sb[:, q4 * 8192:(q4 + 1) * 8192],
                    in_=wo[q4 * 512:(q4 + 1) * 512, :]
                    .rearrange("(hc p) d -> p hc d", p=128),
                ).then_inc(E["sem"]["dG"], 16)
        S.emit("g", fg)
        state["wo_loaded_dG"] = S.inc("dG", 64)
        _wait(S, "s", "dS", state["x1_spill_1_dS"])
        _wait(S, "s", "cA", state["qkv_end_cA"])
        _wait(S, "s", "cV", state["qkv_end_cV"])

        def fs(Ee):
            Ee["s"].dma_start(out=cmask_sb[:], in_=cmask_in).then_inc(
                Ee["sem"]["dS"], 16)
        S.emit("s", fs)
        state["cmask_dS"] = S.inc("dS", 16)

    def emit_attention(ps_s, ps_pt, ps_o, ps_ofm, ps_wo):
        _wait(S, "t", "cV", S.val("cV"))
        _wait(S, "t", "cA", S.val("cA"))
        q_loaded = [0, 0]
        q_last_use = [0, 0]
        kv_loaded = [0, 0]
        kv_last_use = [0, 0]
        x1_loaded = [0, 0]
        x1_free = [0, 0]
        s_free = [0, 0]
        psb_free = [0, 0]
        lp_free = [0, 0]
        pt_free = [0, 0]
        o_free = 0
        ofm_cv = 0
        wo_free = 0
        sseq = 0
        tseq = 0
        kvseq = 0
        for qt in range(n_qt):
            qslot = qt % 2
            _wait(S, "s", "dS", state["kv_ready_dS"])
            _wait(S, "s", "cP", q_last_use[qslot])

            def fsq(Ee, qt=qt, qslot=qslot):
                qap = q_dram[:, qt * 128:(qt + 1) * 128].rearrange(
                    "(hc p) q -> p hc q", p=128)
                for hh in range(2):
                    Ee["s"].dma_start(
                        out=qbuf[qslot][:, hh * 1024:(hh + 1) * 1024],
                        in_=qap[:, hh * 8:(hh + 1) * 8, :],
                    ).then_inc(Ee["sem"]["dS"], 16)
            S.emit("s", fsq)
            q_loaded[qslot] = S.inc("dS", 32)
            xslot = qt % 2
            _wait(S, "s", "cV", x1_free[xslot])

            def fsx(Ee, qt=qt, xslot=xslot):
                Ee["s"].dma_start(
                    out=x1t[xslot][:],
                    in_=x1_dram[qt * 128:(qt + 1) * 128, :],
                ).then_inc(Ee["sem"]["dS"], 16)
            S.emit("s", fsx)
            x1_loaded[xslot] = S.inc("dS", 16)

            nkb = qt // 4 + 1
            for hg in range(4):
                o_started = [False] * 4
                for pk in range(2):
                    for kb in range(nkb):
                        kvslot = kvseq % 2
                        kvseq += 1
                        _wait(S, "s", "cP", kv_last_use[kvslot])

                        def fskv(Ee, pk=pk, hg=hg, kb=kb, kvslot=kvslot):
                            E = Ee
                            E["s"].dma_start(
                                out=kbuf[kvslot][:],
                                in_=sh_k[pk * 512 + hg * 128:
                                         pk * 512 + (hg + 1) * 128,
                                         kb * 512:(kb + 1) * 512],
                            ).then_inc(E["sem"]["dS"], 16)
                            E["s"].dma_start(
                                out=vbuf[kvslot][:],
                                in_=sh_v[pk * T + kb * 512:pk * T + (kb + 1) * 512,
                                         hg * 128:(hg + 1) * 128]
                                .rearrange("(i p) d -> p i d", p=128),
                            ).then_inc(E["sem"]["dS"], 16)
                        S.emit("s", fskv)
                        kv_loaded[kvslot] = S.inc("dS", 32)
                        diag = (kb == nkb - 1)
                        isfirst = (pk == 0 and kb == 0)
                        for hq in range(4):
                            sslot = sseq % 2
                            sseq += 1
                            if paranoid:
                                for _e in ("t", "v", "a"):
                                    _wait(S, _e, "cP", S.val("cP"))
                                    _wait(S, _e, "cV", S.val("cV"))
                                    _wait(S, _e, "cA", S.val("cA"))
                            _wait(S, "t", "dS", kv_loaded[kvslot])
                            _wait(S, "t", "dS", q_loaded[qslot])
                            _wait(S, "t", "cA", s_free[sslot])

                            def ftS(Ee, hg=hg, hq=hq, sslot=sslot,
                                    kvslot=kvslot, qslot=qslot):
                                Ee["t"].matmul(
                                    ps_s[sslot][:],
                                    qbuf[qslot][:, (hg * 4 + hq) * 128:
                                                (hg * 4 + hq + 1) * 128],
                                    kbuf[kvslot][:],
                                    start=True, stop=True,
                                ).then_inc(Ee["sem"]["cP"], 1)
                            S.emit("t", ftS)
                            cpS = S.inc("cP")
                            if fl_lvl < 2:
                                continue
                            _wait(S, "v", "cP", cpS)
                            _wait(S, "v", "cA", s_free[sslot])
                            if diag:
                                _wait(S, "v", "dS", state["cmask_dS"])

                            def fvm(Ee, sslot=sslot, diag=diag, pk=pk, qt=qt,
                                    hq=hq, isfirst=isfirst):
                                E = Ee
                                v = E["v"]
                                if diag and not no_mask:
                                    moff = (pk * 4 + qt % 4) * 512
                                    v.tensor_add(ps_s[sslot][:], ps_s[sslot][:],
                                                 cmask_sb[:, moff:moff + 512])
                                    v.tensor_copy(vdmy[:], vdmy[:])
                                if isfirst:
                                    v.reduce_max(fl_m[hq][:], ps_s[sslot][:], AX)
                                    v.tensor_copy(vdmy[:], vdmy[:])
                                    v.tensor_scalar(fl_bias[sslot][:],
                                                    fl_m[hq][:],
                                                    -SC, None, ALU.mult)
                                    v.tensor_copy(vdmy[:], vdmy[:]
                                                  ).then_inc(E["sem"]["cV"], 1)
                                else:
                                    v.reduce_max(fl_mn[sslot][:],
                                                 ps_s[sslot][:], AX)
                                    v.tensor_copy(vdmy[:], vdmy[:])
                                    v.tensor_max(fl_mn[sslot][:], fl_mn[sslot][:],
                                                 fl_m[hq][:])
                                    v.tensor_copy(vdmy[:], vdmy[:])
                                    v.tensor_sub(fl_dm[sslot][:], fl_m[hq][:],
                                                 fl_mn[sslot][:])
                                    v.tensor_copy(fl_m[hq][:], fl_mn[sslot][:])
                                    v.tensor_copy(vdmy[:], vdmy[:])
                                    v.tensor_scalar(fl_bias[sslot][:],
                                                    fl_mn[sslot][:],
                                                    -SC, None, ALU.mult)
                                    v.tensor_copy(vdmy[:], vdmy[:]
                                                  ).then_inc(E["sem"]["cV"], 1)
                            S.emit("v", fvm)
                            cvm = S.inc("cV")
                            if fl_lvl < 3:
                                continue
                            _wait(S, "a", "cV", cvm)
                            _wait(S, "a", "cV", lp_free[sslot])
                            _wait(S, "a", "cP", psb_free[sslot])

                            def faP(Ee, sslot=sslot, isfirst=isfirst):
                                E = Ee
                                a = E["a"]
                                if not isfirst:
                                    a.activation(fl_al[sslot][:], fl_dm[sslot][:],
                                                 AF.Exp, scale=SC)
                                a.activation(psb[sslot][:], ps_s[sslot][:],
                                             AF.Exp, bias=fl_bias[sslot][:],
                                             scale=SC)
                                a.activation(admy[:], admy[:], AF.Identity
                                             ).then_inc(E["sem"]["cA"], 1)
                            S.emit("a", faP)
                            s_free[sslot] = S.inc("cA")
                            caP = s_free[sslot]
                            if fl_lvl < 4:
                                continue
                            _wait(S, "v", "cA", caP)
                            _wait(S, "v", "cP", state.get("pv_cp_%d" % hq, 0))

                            def fvl(Ee, hq=hq, sslot=sslot, isfirst=isfirst):
                                E = Ee
                                v = E["v"]
                                if isfirst:
                                    v.reduce_sum(fl_lp[sslot][:], psb[sslot][:],
                                                 AX)
                                    v.tensor_copy(vdmy[:], vdmy[:])
                                    v.tensor_copy(fl_l[hq][:], fl_lp[sslot][:])
                                    v.tensor_copy(vdmy[:], vdmy[:]
                                                  ).then_inc(E["sem"]["cV"], 1)
                                else:
                                    v.reduce_sum(fl_lp[sslot][:], psb[sslot][:],
                                                 AX)
                                    v.tensor_scalar(fl_l[hq][:], fl_l[hq][:],
                                                    fl_al[sslot][:], None,
                                                    ALU.mult)
                                    if not no_resc:
                                        v.tensor_scalar(
                                            ps_o[:, hq * 128:(hq + 1) * 128],
                                            ps_o[:, hq * 128:(hq + 1) * 128],
                                            fl_al[sslot][:], None, ALU.mult)
                                    else:
                                        v.tensor_copy(vdmy[:], vdmy[:])
                                    v.tensor_add(fl_l[hq][:], fl_l[hq][:],
                                                 fl_lp[sslot][:])
                                    v.tensor_copy(vdmy[:], vdmy[:]
                                                  ).then_inc(E["sem"]["cV"], 1)
                            S.emit("v", fvl)
                            cvl = S.inc("cV")
                            lp_free[sslot] = cvl
                            if fl_lvl < 5:
                                continue
                            tslot = tseq % 2
                            tseq += 1
                            _wait(S, "t", "cA", caP)
                            _wait(S, "t", "cV", pt_free[tslot])

                            def ftT(Ee, sslot=sslot, tslot=tslot):
                                E = Ee
                                for i in range(4):
                                    ins = E["t"].transpose(
                                        ps_pt[tslot][:, i * 128:(i + 1) * 128],
                                        psb[sslot][:, i * 128:(i + 1) * 128],
                                        ident_sb[:],
                                    )
                                    if i == 3:
                                        ins.then_inc(E["sem"]["cP"], 1)
                            S.emit("t", ftT)
                            cpT = S.inc("cP")
                            psb_free[sslot] = cpT
                            _wait(S, "v", "cP", cpT)

                            def fvpt(Ee, tslot=tslot):
                                Ee["v"].tensor_copy(
                                    ptsb[tslot][:], ps_pt[tslot][:])
                                Ee["v"].tensor_copy(vdmy[:], vdmy[:]
                                                    ).then_inc(Ee["sem"]["cV"], 1)
                            S.emit("v", fvpt)
                            pt_free[tslot] = S.inc("cV")
                            if fl_lvl < 6:
                                continue
                            _wait(S, "t", "cV", max(pt_free[tslot], cvl))
                            _wait(S, "t", "cA", state.get("o_free_a", 0))

                            st_bank = not any(o_started)

                            def ftPV(Ee, tslot=tslot, kvslot=kvslot, hq=hq,
                                     st=st_bank):
                                E = Ee
                                for i in range(4):
                                    ins = E["t"].matmul(
                                        ps_o[:, hq * 128:(hq + 1) * 128],
                                        ptsb[tslot][:, i * 128:(i + 1) * 128],
                                        vbuf[kvslot][:, i * 128:(i + 1) * 128],
                                        start=(st and i == 0), stop=False,
                                        skip_group_check=True,
                                    )
                                    if i == 3:
                                        ins.then_inc(E["sem"]["cP"], 1)
                            S.emit("t", ftPV)
                            state["pv_cp_%d" % hq] = S.inc("cP")
                            o_started[hq] = True
                        kv_last_use[kvslot] = S.val("cP")
                if fl_lvl >= 7:
                    for hq in range(4):
                        _wait(S, "v", "cP", state.get("pv_cp_%d" % hq, 0))

                    def fvr(Ee):
                        E = Ee
                        v = E["v"]
                        for hq in range(4):
                            v.reciprocal(fl_rl[hq][:], fl_l[hq][:])
                        v.tensor_copy(vdmy[:], vdmy[:]
                                      ).then_inc(E["sem"]["cV"], 1)
                    S.emit("v", fvr)
                    cvr = S.inc("cV")
                    _wait(S, "a", "cV", cvr)

                    _wait(S, "a", "dS", state.get("otdump_dS", 0))

                    def fao(Ee, hg=hg):
                        E = Ee
                        for hq in range(4):
                            E["a"].activation(
                                otok[:, hg * 512 + hq * 128:
                                     hg * 512 + (hq + 1) * 128],
                                ps_o[:, hq * 128:(hq + 1) * 128],
                                AF.Identity, scale=fl_rl[hq][:])
                        E["a"].activation(admy[:], admy[:], AF.Identity
                                          ).then_inc(E["sem"]["cA"], 1)
                    S.emit("a", fao)
                    o_free_a = S.inc("cA")
                    state["o_free_a"] = o_free_a
            q_last_use[qslot] = S.val("cP")
            for g4 in range(4 if fl_lvl >= 8 else 0):
                _wait(S, "t", "cA", state.get("o_free_a", 0))
                _wait(S, "t", "cV", ofm_cv)

                def ftO(Ee, g4=g4):
                    E = Ee
                    for j in range(4):
                        hc = g4 * 4 + j
                        ins = E["t"].transpose(
                            ps_ofm[:, j * 128:(j + 1) * 128],
                            otok[:, hc * 128:(hc + 1) * 128],
                            ident_sb[:],
                        )
                        if j == 3:
                            ins.then_inc(E["sem"]["cP"], 1)
                S.emit("t", ftO)
                cpO = S.inc("cP")
                _wait(S, "v", "cP", cpO)

                def fvO(Ee, g4=g4):
                    E = Ee
                    dst = ofm.ap().rearrange("p (hc q) -> p hc q", q=128)[
                        :, 4 * g4:4 * g4 + 4, :]
                    src = ps_ofm.ap().rearrange("p (j q) -> p j q", q=128)
                    E["v"].tensor_copy(dst, src)
                    E["v"].tensor_copy(vdmy[:], vdmy[:]).then_inc(E["sem"]["cV"], 1)
                S.emit("v", fvO)
                ofm_cv = S.inc("cV")
            for dh in range(2 if fl_lvl >= 8 else 0):
                _wait(S, "t", "cV", ofm_cv)
                _wait(S, "t", "dG", state["wo_loaded_dG"])
                _wait(S, "t", "cV", wo_free)

                def ftW(Ee, dh=dh):
                    E = Ee
                    for hc in range(16):
                        for n2 in range(2):
                            ins = E["t"].matmul(
                                ps_wo[:, n2 * 512:(n2 + 1) * 512],
                                ofm[:, hc * 128:(hc + 1) * 128],
                                wo_sb[:, hc * 2048 + dh * 1024 + n2 * 512:
                                      hc * 2048 + dh * 1024 + (n2 + 1) * 512],
                                start=(hc == 0), stop=(hc == 15),
                            )
                            if hc == 15 and n2 == 1:
                                ins.then_inc(E["sem"]["cP"], 1)
                S.emit("t", ftW)
                cpW = S.inc("cP")
                _wait(S, "v", "cP", cpW)
                _wait(S, "v", "dS", x1_loaded[xslot])

                def fvW(Ee, dh=dh, xslot=xslot):
                    E = Ee
                    xs = x1t[xslot][:, dh * 1024:(dh + 1) * 1024]
                    E["v"].tensor_add(xs, ps_wo[:, 0:1024], xs)
                    E["v"].tensor_copy(vdmy[:], vdmy[:]).then_inc(
                        E["sem"]["cV"], 1)
                S.emit("v", fvW)
                wo_free = S.inc("cV")
            if fl_lvl < 8:
                continue
            _wait(S, "s", "cV", wo_free)

            def fsx2(Ee, qt=qt, xslot=xslot):
                Ee["s"].dma_start(
                    out=x1_dram[qt * 128:(qt + 1) * 128, :],
                    in_=x1t[xslot][:],
                ).then_inc(Ee["sem"]["dS"], 16)
            S.emit("s", fsx2)
            x1_free[xslot] = S.val("cV")
            S.inc("dS", 16)
        state["x2_stored_dS"] = S.val("dS")
        state["attn_cP"] = S.val("cP")
        state["attn_cV"] = S.val("cV")
        state["attn_cA"] = S.val("cA")

    def emit_x2_reload():
        _wait(S, "s", "dS", state["x2_stored_dS"])
        _wait(S, "s", "cP", state["attn_cP"])
        _wait(S, "s", "cV", state["attn_cV"])
        _wait(S, "s", "cA", state["attn_cA"])
        for tc in range(NTC):
            def fs(Ee, tc=tc):
                Ee["s"].dma_start(
                    out=x_sb[:, tc * 2048:(tc + 1) * 2048],
                    in_=x1_dram[tc * 128:(tc + 1) * 128, :],
                ).then_inc(Ee["sem"]["dS"], 16)
            S.emit("s", fs)
            S.inc("dS", 16)
        state["x_loaded_dS"] = S.val("dS")

    def emit_output(h):
        _wait(S, "s", "cV", state["ffn_done_cV"])
        for tl in range(8):
            tcg = h * 8 + tl
            def fs(Ee, tcg=tcg):
                Ee["s"].dma_start(
                    out=out[tcg * 128:(tcg + 1) * 128, :],
                    in_=x_sb[:, tcg * 2048:(tcg + 1) * 2048],
                ).then_inc(Ee["sem"]["dS"], 16)
            S.emit("s", fs)
            S.inc("dS", 16)

    def emit_dump_and_finish():
        cv = S.val("cV")
        ca = S.val("cA")
        cp = S.val("cP")
        _wait(S, "s", "cV", cv)
        _wait(S, "s", "cA", ca)
        _wait(S, "s", "cP", cp)
        for tc in range(NTC):
            def fs(Ee, tc=tc):
                Ee["s"].dma_start(
                    out=out[tc * 128:(tc + 1) * 128, :],
                    in_=x_sb[:, tc * 2048:(tc + 1) * 2048],
                ).then_inc(Ee["sem"]["dS"], 16)
            S.emit("s", fs)
            S.inc("dS", 16)
        tot = S.val("dS")

        def ff(Ee):
            Ee["s"].wait_ge(Ee["sem"]["dS"], tot)
        S.emit("s", ff)

    phase0()

    if stop_after in ("ln", "router"):
        with ExitStack() as ps_ctx:
            e_ = ps_ctx.enter_context
            ps_tr = e_(nc.psum_tensor("ps_tr_dbg", [128, 512], BF16))
            ps_sc = e_(nc.psum_tensor("ps_sc_dbg", [128, 512], F32))
            emit_ln(0, ps_tr, xn_tok_f)
            if stop_after == "router":
                emit_router(0, ps_sc)
        emit_dump_and_finish()
        return _assemble(nc, S)

    for h in range(2):
        with ExitStack() as ps_ctx:
            e_ = ps_ctx.enter_context
            ps_g = e_(nc.psum_tensor(f"ps_g_{h}", [128, 512], F32))
            ps_g2 = e_(nc.psum_tensor(f"ps_g2_{h}", [128, 512], F32))
            ps_u = e_(nc.psum_tensor(f"ps_u_{h}", [128, 512], F32))
            dacc0 = e_(nc.psum_tensor(f"dacc0_{h}", [128, 512], F32))
            dacc1 = e_(nc.psum_tensor(f"dacc1_{h}", [128, 512], F32))
            ps_tr = e_(nc.psum_tensor(f"ps_tr_{h}", [128, 512], BF16))
            ps_sc = e_(nc.psum_tensor(f"ps_sc_{h}", [128, 512], F32))
            emit_ln(h, ps_tr, xn_tok_f)
            emit_router(h, ps_sc)
            emit_experts(h, (ps_g, ps_g2), ps_u, (dacc0, dacc1),
                         (tmp_ev, ps_sc))
        state["arena_free_cP"] = S.val("cP")

    if stop_after == "ffn1":
        _wait(S, "s", "cV", S.val("cV"))
        _wait(S, "s", "cA", S.val("cA"))
        _wait(S, "s", "cP", S.val("cP"))

        def fdbg(Ee):
            Ee["s"].dma_start(out=dbg_cn[:], in_=x_sb[:, 0:1024]
                              ).then_inc(Ee["sem"]["dS"], 16)
            Ee["s"].dma_start(out=dbg_sc[:, 0:256], in_=scores_sb[:]
                              ).then_inc(Ee["sem"]["dS"], 16)
            Ee["s"].dma_start(out=dbg_sc[:, 256:512], in_=c_tok[:]
                              ).then_inc(Ee["sem"]["dS"], 16)
            Ee["s"].dma_start(out=dbg_xn[:], in_=xn_sb[:, 0:2048]
                              ).then_inc(Ee["sem"]["dS"], 16)
        S.emit("s", fdbg)
        S.inc("dS", 64)
        emit_dump_and_finish()
        return _assemble(nc, S)

    state["rope_dS"] = 0
    _wait(S, "s", "cP", state["ffn_done_cP"])
    _wait(S, "s", "cV", state["ffn_done_cV"])

    def f_rope(Ee):
        E = Ee
        E["s"].dma_start(out=ropec_sb[:], in_=ropec_in).then_inc(E["sem"]["dS"], 16)
        E["s"].dma_start(out=ropes_sb[:], in_=ropes_in).then_inc(E["sem"]["dS"], 16)
    S.emit("s", f_rope)
    state["rope_dS"] = S.inc("dS", 32)

    def f_pid(Ee):
        pid = nc.partition_id([mybir.EngineType.SP])
        state["par"] = nc.snap(pid % 2, engines=[mybir.EngineType.SP],
                               min_val=0, max_val=1)
        state["parx"] = nc.snap((pid + 1) % 2, engines=[mybir.EngineType.SP],
                                min_val=0, max_val=1)
    S.emit("s", f_pid)

    for h in range(2):
        with ExitStack() as ps_ctx:
            e_ = ps_ctx.enter_context
            ps_tr = e_(nc.psum_tensor(f"ps_tr_q{h}", [128, 512], BF16))
            ps_qk = [e_(nc.psum_tensor(f"ps_qk{i}_{h}", [128, 512], F32))
                     for i in range(2)]
            ps_v = [e_(nc.psum_tensor(f"ps_v{i}_{h}", [128, 512], F32))
                    for i in range(2)]
            state["qkv_start_cV"] = S.val("cV")
            emit_ln(h, ps_tr, xn_tok_q)
            emit_x1_spill(h, to_dbg=True)
            emit_qkv(h, ps_qk, ps_v, None)
    state["qkv_end_cA"] = S.val("cA")
    state["qkv_end_cV"] = S.val("cV")
    emit_flag_sync()
    if stop_after == "qkv":
        emit_dump_and_finish()
        return _assemble(nc, S)
    emit_attn_overlay_loads()

    if paranoid >= 2:
        S.paranoid = True
    with ExitStack() as ps_ctx:
        e_ = ps_ctx.enter_context
        ps_s = [e_(nc.psum_tensor(f"ps_s{i}", [128, 512], F32)) for i in range(2)]
        ps_pt = [e_(nc.psum_tensor(f"ps_pt{i}", [128, 512], BF16))
                 for i in range(2)]
        ps_o = e_(nc.psum_tensor("ps_o", [128, 512], F32))
        ps_ofm = e_(nc.psum_tensor("ps_ofm", [128, 512], BF16))
        ps_wo = e_(nc.psum_tensor("ps_wo", [128, 1024], F32))
        emit_attention(ps_s, ps_pt, ps_o, ps_ofm, ps_wo)
    S.paranoid = False

    emit_x2_reload()
    if stop_after == "attn":
        emit_dump_and_finish()
        return _assemble(nc, S)

    state["arena_free_cP"] = S.val("cP")
    for h in range(2):
        with ExitStack() as ps_ctx:
            e_ = ps_ctx.enter_context
            ps_g = e_(nc.psum_tensor(f"ps2_g_{h}", [128, 512], F32))
            ps_g2 = e_(nc.psum_tensor(f"ps2_g2_{h}", [128, 512], F32))
            ps_u = e_(nc.psum_tensor(f"ps2_u_{h}", [128, 512], F32))
            dacc0 = e_(nc.psum_tensor(f"dacc2_0_{h}", [128, 512], F32))
            dacc1 = e_(nc.psum_tensor(f"dacc2_1_{h}", [128, 512], F32))
            ps_tr = e_(nc.psum_tensor(f"ps2_tr_{h}", [128, 512], BF16))
            ps_sc = e_(nc.psum_tensor(f"ps2_sc_{h}", [128, 512], F32))
            emit_ln(h, ps_tr, xn_tok_f)
            emit_router(h, ps_sc)
            emit_experts(h, (ps_g, ps_g2), ps_u, (dacc0, dacc1),
                         (tmp_ev, ps_sc))
            emit_output(h)
        state["arena_free_cP"] = S.val("cP")

    total_dS = S.val("dS")

    def f_final(Ee):
        Ee["s"].wait_ge(Ee["sem"]["dS"], total_dS)
    S.emit("s", f_final)
    return _assemble(nc, S)


def _assemble(nc, S):
    with ExitStack() as ctx:
        e_ = ctx.enter_context
        sems = {}
        for sn in ("dS", "dG", "dP", "cP", "cV", "cA"):
            sems[sn] = e_(nc.semaphore(sn))
        block = e_(nc.Block())

        def run(eng_key):
            def runner(eng):
                Ee = {"t": nc.tensor, "v": nc.vector, "a": nc.scalar,
                      "g": nc.gpsimd, "s": nc.sync, "sem": sems}
                for fn in S.ops[eng_key]:
                    fn(Ee)
            return runner

        block.tensor(run("t"))
        block.vector(run("v"))
        block.scalar(run("a"))
        block.gpsimd(run("g"))
        block.sync(run("s"))
    return nc




def kernel(L0_attn_x, L0_attn_Wqkv, L0_attn_Wo, L0_ffn_router_w,
           L0_ffn_gate_w, L0_ffn_up_w, L0_ffn_down_w):
    x = np.ascontiguousarray(np.asarray(L0_attn_x, dtype=np.float32))
    wqkv = np.ascontiguousarray(np.asarray(L0_attn_Wqkv, dtype=np.float32))
    wo = np.ascontiguousarray(np.asarray(L0_attn_Wo, dtype=np.float32))
    rw = np.ascontiguousarray(np.asarray(L0_ffn_router_w, dtype=np.float32))
    gw = np.ascontiguousarray(np.asarray(L0_ffn_gate_w, dtype=np.float32))
    uw = np.ascontiguousarray(np.asarray(L0_ffn_up_w, dtype=np.float32))
    dwn = np.ascontiguousarray(np.asarray(L0_ffn_down_w, dtype=np.float32))

    nc = build_nc(evac_act=False)
    nonce = np.array([[np.random.randint(1, 2**31)] * 16], dtype=np.uint32)
    consts = [_host_consts(p) for p in range(2)]
    in_maps = []
    for c in range(8):
        b, p = c // 2, c % 2
        ropec, ropes, cmask, ident = consts[p]
        xs = np.ascontiguousarray(x[b, p::2, :])
        in_maps.append({
            "x0": xs, "wqkv": wqkv, "wo": wo, "rw": rw,
            "gw": gw, "uw": uw, "dw": dwn,
            "ropec": ropec, "ropes": ropes, "cmask": cmask,
            "ident": ident, "nonce": nonce,
        })
    res = run_bass_kernel_spmd(nc, in_maps, core_ids=list(range(8)))
    outp = np.empty((4, 4096, 2048), dtype=np.float32)
    for c in range(8):
        b, p = c // 2, c % 2
        outp[b, p::2, :] = np.asarray(res.results[c]["out"])
    return outp


# version 05
# speedup vs baseline: 1.0881x; 1.0881x over previous
import math
from contextlib import ExitStack

import numpy as np
import ml_dtypes
import concourse.bass as bass
import concourse.mybir as mybir
from concourse.bass import ts
from concourse.bass_utils import run_bass_kernel_spmd

F32 = mybir.dt.float32
BF16 = mybir.dt.bfloat16
U32 = mybir.dt.uint32
AF = mybir.ActivationFunctionType
ALU = mybir.AluOpType
AX = mybir.AxisListType.X

T = 2048
D = 2048
E = 16
F = 768
NH = 16
NKV = 4
HD = 128
TH = 1024
NTC = 16
SC = 1.0 / math.sqrt(HD)
MASKNEG = -3.0e38
EPS = 1e-5



def _host_consts(core_parity):
    theta = 10000.0
    i = np.arange(64, dtype=np.float64)
    freqs = theta ** (-i / 64.0)
    t = np.arange(T, dtype=np.float64)
    pos = 2.0 * t + core_parity
    ang = pos[None, :] * freqs[:, None]
    cos = np.cos(ang)
    sin = np.sin(ang)
    ropec = np.concatenate([cos, cos], axis=0).astype(ml_dtypes.bfloat16)
    ropes = np.concatenate([sin, sin], axis=0).astype(ml_dtypes.bfloat16)

    cm = np.full((128, 8 * 512), MASKNEG, dtype=np.float32)
    p = np.arange(128)[:, None]
    f = np.arange(512)[None, :]
    for pk in range(2):
        strict = 1 if (core_parity == 0 and pk == 1) else 0
        for jm in range(4):
            keep = f <= (128 * jm + p - strict)
            blk = np.where(keep, np.float32(0.0), np.float32(MASKNEG))
            cm[:, (pk * 4 + jm) * 512:(pk * 4 + jm + 1) * 512] = blk
    cmask = cm.astype(ml_dtypes.bfloat16)

    ident = np.eye(128, dtype=ml_dtypes.bfloat16)
    return ropec, ropes, cmask, ident




class Sched:

    ENGS = ("t", "v", "a", "g", "s")

    def __init__(self):
        self.ops = {e: [] for e in self.ENGS}
        self.cnt = {}

    paranoid = False

    def emit(self, eng, fn):
        if self.paranoid and eng in ("t", "v", "a"):
            cp, cv, ca = self.val("cP"), self.val("cV"), self.val("cA")

            def wrapped(E, _fn=fn, _eng=eng, _cp=cp, _cv=cv, _ca=ca):
                if _cp: E[_eng].wait_ge(E["sem"]["cP"], _cp)
                if _cv: E[_eng].wait_ge(E["sem"]["cV"], _cv)
                if _ca: E[_eng].wait_ge(E["sem"]["cA"], _ca)
                _fn(E)
            self.ops[eng].append(wrapped)
        else:
            self.ops[eng].append(fn)

    def inc(self, name, n=1):
        self.cnt[name] = self.cnt.get(name, 0) + n
        return self.cnt[name]

    def val(self, name):
        return self.cnt.get(name, 0)


def _wait(S, eng, sem_name, value):
    if value <= 0:
        return
    def f(E, _sn=sem_name, _v=value):
        E[eng].wait_ge(E["sem"][_sn], _v)
    S.emit(eng, f)




def build_nc(dbg=False, n_exp=E, stop_after=None, n_qt=16, no_mask=False, no_resc=False, fl_lvl=8, paranoid=0, evac_act=False, para_eng="t"):
    EVAC_ACT = evac_act
    PARA_ENG = para_eng
    nc = bass.Bass("TRN2", target_bir_lowering=False)

    x0 = nc.dram_tensor("x0", [T, D], F32, kind="ExternalInput").ap()
    wqkv = nc.dram_tensor("wqkv", [D, 3072], F32, kind="ExternalInput").ap()
    wo = nc.dram_tensor("wo", [D, D], F32, kind="ExternalInput").ap()
    rw = nc.dram_tensor("rw", [D, E], F32, kind="ExternalInput").ap()
    gw = nc.dram_tensor("gw", [E, D, F], F32, kind="ExternalInput").ap()
    uw = nc.dram_tensor("uw", [E, D, F], F32, kind="ExternalInput").ap()
    dw = nc.dram_tensor("dw", [E, F, D], F32, kind="ExternalInput").ap()
    ropec_in = nc.dram_tensor("ropec", [128, T], BF16, kind="ExternalInput").ap()
    ropes_in = nc.dram_tensor("ropes", [128, T], BF16, kind="ExternalInput").ap()
    cmask_in = nc.dram_tensor("cmask", [128, 4096], BF16, kind="ExternalInput").ap()
    ident_in = nc.dram_tensor("ident", [128, 128], BF16, kind="ExternalInput").ap()
    nonce_in = nc.dram_tensor("nonce", [1, 16], U32, kind="ExternalInput").ap()

    out = nc.dram_tensor("out", [T, D], F32, kind="ExternalOutput").ap()

    qkv_kind = "ExternalOutput" if stop_after == "qkv" else "Internal"
    q_dram = nc.dram_tensor("q_dram", [D, T], BF16, kind=qkv_kind).ap()
    k_dram = nc.dram_tensor("k_dram", [512, T], BF16, kind=qkv_kind).ap()
    v_dram = nc.dram_tensor("v_dram", [T, 512], BF16, kind=qkv_kind).ap()
    x1_kind = "ExternalOutput" if dbg else "Internal"
    x1_dram = nc.dram_tensor("x1_dram", [T, D], F32, kind=x1_kind).ap()
    sh_k = nc.dram_tensor("sh_k", [1024, T], BF16, kind="Internal",
                          addr_space="Shared").ap()
    sh_v = nc.dram_tensor("sh_v", [2 * T, 512], BF16, kind="Internal",
                          addr_space="Shared").ap()
    sh_flag = nc.dram_tensor("sh_flag", [2, 16], U32, kind="Internal",
                             addr_space="Shared").ap()
    if dbg:
        dbg_x1 = nc.dram_tensor("dbg_x1", [T, D], F32, kind="ExternalOutput").ap()
    if stop_after == "ffn1":
        dbg_sc = nc.dram_tensor("dbg_sc", [128, 512], F32, kind="ExternalOutput").ap()
        dbg_xn = nc.dram_tensor("dbg_xn", [128, 2048], BF16, kind="ExternalOutput").ap()
        dbg_cn = nc.dram_tensor("dbg_cn", [128, 1024], F32, kind="ExternalOutput").ap()

    SBASE = 16928

    def A(name, shape, dtype, offset):
        return nc.alloc_sbuf_tensor_at(name, shape, dtype,
                                       offset=SBASE + offset)
    x_sb = A("x_sb", [128, NTC * 2048], F32, offset=0)
    wo_sb = A("wo_sb", [128, 16 * 2048], BF16, offset=0)
    cmask_sb = A("cmask_sb", [128, 4096], BF16, offset=65536)
    qbuf = [A(f"qbuf{i}", [128, 2048], BF16, offset=73728 + 4096 * i)
            for i in range(2)]
    kbuf = [A(f"kbuf{i}", [128, 512], BF16, offset=81920 + 1024 * i)
            for i in range(2)]
    vbuf = [A(f"vbuf{i}", [128, 512], BF16, offset=83968 + 1024 * i)
            for i in range(2)]
    psb = [A(f"psb{i}", [128, 512], BF16, offset=86016 + 1024 * i)
           for i in range(2)]
    ptsb = [A(f"ptsb{i}", [128, 512], BF16, offset=88064 + 1024 * i)
            for i in range(2)]
    otok = A("otok", [128, 2048], BF16, offset=90112)
    ofm = A("ofm", [128, 2048], BF16, offset=94208)
    x1t = [A(f"x1t{i}", [128, 2048], F32, offset=98304 + 8192 * i)
           for i in range(2)]

    xn_sb = A("xn_sb", [128, 16 * TH], BF16, offset=131072)

    CA = 163840
    gu = [A(f"gu{i}", [128, 4096], BF16, offset=CA + 8192 * i)
          for i in range(2)]
    dwq = [A(f"dwq{i}", [128, 3072], BF16, offset=CA + 16384 + 6144 * i)
           for i in range(2)]
    h_all = A("h_all", [128, 6 * TH], BF16, offset=CA + 28672)
    xn_tok_f = A("xn_tok_f", [128, 2048], BF16, offset=CA + 28672)
    xn_tok_q = A("xn_tok_q", [128, 2048], BF16, offset=CA + 0)
    silu_sb = A("silu_sb", [128, 512], BF16, offset=211392)
    tmp_ev = A("tmp_ev", [128, 512], F32, offset=CA + 40960)
    wqb = [A(f"wqb{i}", [128, 2048], BF16, offset=CA + 4096 * i)
           for i in range(2)]
    wv_sb = A("wv_sb", [128, 16 * 512], BF16, offset=CA + 8192)
    ropec_sb = A("ropec_sb", [128, T], BF16, offset=CA + 24576)
    ropes_sb = A("ropes_sb", [128, T], BF16, offset=CA + 28672)
    stq = [A(f"stq{i}", [128, 512], BF16, offset=CA + 32768 + 1024 * i)
           for i in range(2)]
    stv = [A(f"stv{i}", [128, 512], BF16, offset=CA + 34816 + 1024 * i)
           for i in range(2)]
    rta = A("rta", [128, 512], F32, offset=CA + 36864)
    rtb = A("rtb", [128, 512], F32, offset=CA + 38912)
    rth = A("rth", [128, 512], F32, offset=CA + 40960)

    sm = [206848]

    def SA(name, shape, dtype):
        t = A(name, shape, dtype, offset=sm[0])
        nbytes = shape[1] * (4 if dtype in (F32, U32) else 2)
        sm[0] += (nbytes + 31) // 32 * 32
        return t

    ident_sb = SA("ident_sb", [128, 128], BF16)
    rw_sb = SA("rw_sb", [128, 256], BF16)
    scores_sb = SA("scores_sb", [128, 256], F32)
    c_tok = SA("c_tok", [128, 256], F32)
    ln_sum = SA("ln_sum", [128, 1], F32)
    ln_negm = SA("ln_negm", [128, 1], F32)
    ln_var = SA("ln_var", [128, 1], F32)
    ln_r = SA("ln_r", [128, 1], F32)
    ln_mr = SA("ln_mr", [128, 1], F32)
    r2_m1 = SA("r2_m1", [128, 1], F32)
    r2_m2 = SA("r2_m2", [128, 1], F32)
    r2_t = SA("r2_t", [128, 16], F32)
    r2_s2 = SA("r2_s2", [128, 16], F32)
    r2_c1 = SA("r2_c1", [128, 16], F32)
    r2_e = SA("r2_e", [128, 1], F32)
    r2_w2 = SA("r2_w2", [128, 1], F32)
    r2_w1 = SA("r2_w1", [128, 1], F32)
    fl_m = [SA(f"fl_m{i}", [128, 1], F32) for i in range(4)]
    fl_l = [SA(f"fl_l{i}", [128, 1], F32) for i in range(4)]
    fl_mn = [SA(f"fl_mn{i}", [128, 1], F32) for i in range(2)]
    fl_dm = [SA(f"fl_dm{i}", [128, 1], F32) for i in range(2)]
    fl_al = [SA(f"fl_al{i}", [128, 1], F32) for i in range(2)]
    fl_bias = [SA(f"fl_bias{i}", [128, 1], F32) for i in range(2)]
    fl_lp = [SA(f"fl_lp{i}", [128, 1], F32) for i in range(2)]
    fl_rl = [SA(f"fl_rl{i}", [128, 1], F32) for i in range(4)]
    vdmy = SA("vdmy", [128, 1], F32)
    admy = SA("admy", [128, 1], F32)
    mx8 = SA("mx8", [128, 8], F32)
    r2_a1 = SA("r2_a1", [128, 16], F32)
    r2_a3 = SA("r2_a3", [128, 16], F32)
    r2_sg1 = SA("r2_sg1", [128, 16], F32)
    r2_sg3 = SA("r2_sg3", [128, 16], F32)
    nonce_sb = SA("nonce_sb", [1, 16], U32)
    flag_sb = SA("flag_sb", [1, 16], U32)
    assert sm[0] <= 211392 and SBASE + 211392 + 1024 <= 229344, sm[0]

    S = Sched()
    if paranoid >= 3:
        S.paranoid = True
    state = {}
    S_KV_PUB = [0]

    psum_stack = ExitStack()

    def psum_alloc(name, shape, dtype=F32):
        return psum_stack.enter_context(nc.psum_tensor(name, shape, dtype))


    def phase0():
        for tc in range(NTC):
            def f(Ee, tc=tc):
                Ee["s"].dma_start(
                    out=x_sb[:, tc * 2048:(tc + 1) * 2048],
                    in_=x0[tc * 128:(tc + 1) * 128, :],
                ).then_inc(Ee["sem"]["dS"], 16)
            S.emit("s", f)
            S.inc("dS", 16)
        for name, dst, src in (
            ("ident", ident_sb, ident_in),
            ("nonce", nonce_sb, nonce_in),
        ):
            def f(Ee, dst=dst, src=src):
                Ee["s"].dma_start(out=dst[:], in_=src).then_inc(Ee["sem"]["dS"], 16)
            S.emit("s", f)
            S.inc("dS", 16)
        def f(Ee):
            for dc in range(16):
                Ee["g"].dma_start(
                    out=rw_sb[:, dc * 16:(dc + 1) * 16],
                    in_=rw[dc * 128:(dc + 1) * 128, :],
                ).then_inc(Ee["sem"]["dG"], 16)
        S.emit("g", f)
        S.inc("dG", 256)
        def finit_v(Ee):
            Ee["v"].memset(vdmy[:], 0.0)
        S.emit("v", finit_v)

        def finit_a(Ee):
            Ee["a"].memzero(admy[:])
        S.emit("a", finit_a)
        state["x_loaded_dS"] = S.val("dS")
        state["consts_dG"] = S.val("dG")

    def load_gu(e, fc, slot, wait_cP):
        _wait(S, "g", "cP", wait_cP)
        def f(Ee, e=e, fc=fc, slot=slot):
            for w_, base in ((gw, 0), (uw, 2048)):
                wap = w_[e, :, fc * 128:(fc + 1) * 128].rearrange(
                    "(dc p) f -> p dc f", p=128)
                for hh in range(2):
                    Ee["g"].dma_start(
                        out=gu[slot][:, base + hh * 1024:base + (hh + 1) * 1024],
                        in_=wap[:, hh * 8:(hh + 1) * 8, :],
                    ).then_inc(Ee["sem"]["dG"], 16)
        S.emit("g", f)
        return S.inc("dG", 64)

    def load_dwq(e, dq, slot, wait_cP):
        _wait(S, "g", "cP", wait_cP)
        def f(Ee, e=e, dq=dq, slot=slot):
            Ee["g"].dma_start(
                out=dwq[slot][:],
                in_=dw[e][:, dq * 512:(dq + 1) * 512]
                .rearrange("(fc p) d -> p fc d", p=128),
            ).then_inc(Ee["sem"]["dG"], 16)
        S.emit("g", f)
        return S.inc("dG", 16)

    def emit_ln(h, ps_tr, xn_tok):
        _wait(S, "v", "dS", state["x_loaded_dS"])
        entry_cp = S.val("cP")
        _wait(S, "a", "cP", entry_cp)
        _wait(S, "v", "cP", entry_cp)
        for tl in range(8):
            tcg = h * 8 + tl
            xt = x_sb[:, tcg * 2048:(tcg + 1) * 2048]

            def fv1(Ee, xt=xt):
                E = Ee
                E["v"].reduce_sum(ln_sum[:], xt, AX)
                E["v"].tensor_copy(vdmy[:], vdmy[:])
                E["v"].tensor_scalar(ln_negm[:], ln_sum[:], -1.0 / D, None,
                                     ALU.mult)
                E["v"].tensor_copy(vdmy[:], vdmy[:]).then_inc(E["sem"]["cV"], 1)
            S.emit("v", fv1)
            cv1 = S.inc("cV")
            _wait(S, "a", "cV", cv1)

            def fa1(Ee, xt=xt, xn_tok=xn_tok):
                Ee["a"].activation(xn_tok[:], xt, AF.Square,
                                   bias=ln_negm[:], scale=1.0)
                Ee["a"].activation(admy[:], admy[:], AF.Identity
                                   ).then_inc(Ee["sem"]["cA"], 1)
            S.emit("a", fa1)
            ca1 = S.inc("cA")
            _wait(S, "v", "cA", ca1)

            def fv2(Ee, xn_tok=xn_tok):
                E = Ee
                E["v"].reduce_sum(ln_var[:], xn_tok[:], AX)
                E["v"].tensor_copy(vdmy[:], vdmy[:])
                E["v"].tensor_scalar(ln_var[:], ln_var[:], 1.0 / D, EPS,
                                     ALU.mult, ALU.add)
                E["v"].tensor_copy(vdmy[:], vdmy[:]).then_inc(E["sem"]["cV"], 1)
            S.emit("v", fv2)
            cv2 = S.inc("cV")
            _wait(S, "a", "cV", cv2)

            def fa2(Ee):
                E = Ee
                E["a"].activation(ln_r[:], ln_var[:], AF.Ln)
                E["a"].activation(ln_r[:], ln_r[:], AF.Exp, scale=-0.5)
                E["a"].activation(admy[:], admy[:], AF.Identity
                                  ).then_inc(E["sem"]["cA"], 1)
            S.emit("a", fa2)
            ca2 = S.inc("cA")
            _wait(S, "v", "cA", ca2)

            def fv3(Ee):
                Ee["v"].tensor_mul(ln_mr[:], ln_negm[:], ln_r[:])
                Ee["v"].tensor_copy(vdmy[:], vdmy[:]).then_inc(Ee["sem"]["cV"], 1)
            S.emit("v", fv3)
            cv3 = S.inc("cV")
            _wait(S, "a", "cV", cv3)

            def fa3(Ee, xt=xt, xn_tok=xn_tok):
                Ee["a"].activation(xn_tok[:], xt, AF.Identity,
                                   bias=ln_mr[:], scale=ln_r[:])
                Ee["a"].activation(admy[:], admy[:], AF.Identity
                                   ).then_inc(Ee["sem"]["cA"], 1)
            S.emit("a", fa3)
            ca3 = S.inc("cA")

            for g4 in range(4):
                _wait(S, "t", "cA", ca3)
                _wait(S, "t", "dS", state["x_loaded_dS"])
                _wait(S, "t", "cV", state.get("tr_evac_cV", 0))

                def ft(Ee, g4=g4, ps_tr=ps_tr, xn_tok=xn_tok):
                    E = Ee
                    for j in range(4):
                        dc = g4 * 4 + j
                        ins = E["t"].transpose(
                            ps_tr[:, j * 128:(j + 1) * 128],
                            xn_tok[:, dc * 128:(dc + 1) * 128],
                            ident_sb[:],
                        )
                        if j == 3:
                            ins.then_inc(E["sem"]["cP"], 1)
                S.emit("t", ft)
                cp = S.inc("cP")
                _wait(S, "v", "cP", cp)

                def fv4(Ee, g4=g4, tl=tl, ps_tr=ps_tr):
                    E = Ee
                    dst = xn_sb.ap().rearrange(
                        "p (dc t) -> p dc t", t=TH
                    )[:, 4 * g4:4 * g4 + 4, tl * 128:(tl + 1) * 128]
                    src = ps_tr.ap().rearrange("p (j t) -> p j t", t=128)
                    E["v"].tensor_copy(dst, src)
                    E["v"].tensor_copy(vdmy[:], vdmy[:]).then_inc(E["sem"]["cV"], 1)
                S.emit("v", fv4)
                state["tr_evac_cV"] = S.inc("cV")
        state["xn_cV"] = S.val("cV")

    def emit_router(h, ps_sc):
        _wait(S, "t", "dG", state["consts_dG"])
        _wait(S, "t", "cV", state["xn_cV"])
        for tl in range(8):
            tcg = h * 8 + tl

            def ft(Ee, tl=tl):
                E = Ee
                for dc in range(16):
                    ins = E["t"].matmul(
                        ps_sc[:, tl * 16:(tl + 1) * 16],
                        xn_sb[:, dc * TH + tl * 128:dc * TH + (tl + 1) * 128],
                        rw_sb[:, dc * 16:(dc + 1) * 16],
                        start=(dc == 0), stop=(dc == 15),
                    )
                    if dc == 15:
                        ins.then_inc(E["sem"]["cP"], 1)
            S.emit("t", ft)
            S.inc("cP")
        cp = S.val("cP")
        _wait(S, "v", "cP", cp)

        def fv(Ee, h=h):
            Ee["v"].tensor_copy(scores_sb[:, h * 128:(h + 1) * 128],
                                ps_sc[:, 0:128])
            Ee["v"].tensor_copy(vdmy[:], vdmy[:]).then_inc(Ee["sem"]["cV"], 1)
        S.emit("v", fv)
        S.inc("cV")
        for tl in range(8):
            tcg = h * 8 + tl
            s_sl = scores_sb[:, tcg * 16:(tcg + 1) * 16]

            def fv1(Ee, s_sl=s_sl):
                E = Ee
                v = E["v"]
                v.tensor_copy(vdmy[:], vdmy[:])
                v.max(mx8[:], s_sl)
                v.tensor_copy(vdmy[:], vdmy[:])
                v.tensor_scalar(r2_a1[:], s_sl, mx8[:, 1:2], None, ALU.subtract)
                v.tensor_scalar(r2_a3[:], s_sl, mx8[:, 2:3], None, ALU.subtract)
                v.tensor_sub(r2_e[:], mx8[:, 0:1], mx8[:, 1:2])
                v.tensor_copy(vdmy[:], vdmy[:]).then_inc(E["sem"]["cV"], 1)
            S.emit("v", fv1)
            cv = S.inc("cV")
            _wait(S, "a", "cV", cv)

            def fa(Ee):
                E = Ee
                E["a"].activation(r2_sg1[:], r2_a1[:], AF.Sign)
                E["a"].activation(r2_sg3[:], r2_a3[:], AF.Sign)
                E["a"].activation(r2_e[:], r2_e[:], AF.Exp)
                E["a"].activation(admy[:], admy[:], AF.Identity
                                  ).then_inc(E["sem"]["cA"], 1)
            S.emit("a", fa)
            ca = S.inc("cA")
            _wait(S, "v", "cA", ca)

            def fv2(Ee, s_sl=s_sl, tcg=tcg):
                E = Ee
                v = E["v"]
                v.tensor_scalar(r2_e[:], r2_e[:], 1.0, None, ALU.add)
                v.tensor_copy(vdmy[:], vdmy[:])
                v.reciprocal(r2_w2[:], r2_e[:])
                v.tensor_copy(vdmy[:], vdmy[:])
                v.tensor_scalar(r2_w1[:], r2_w2[:], -1.0, 0.5, ALU.mult,
                                ALU.add)
                v.tensor_scalar(r2_w2[:], r2_w2[:], 0.5, None, ALU.mult)
                v.tensor_copy(vdmy[:], vdmy[:])
                v.tensor_scalar(r2_c1[:], r2_sg1[:], 0.0, r2_w1[:],
                                ALU.max, ALU.mult)
                v.tensor_scalar(r2_t[:], r2_sg3[:], 0.0, r2_w2[:],
                                ALU.max, ALU.mult)
                v.tensor_copy(vdmy[:], vdmy[:])
                v.tensor_add(c_tok[:, tcg * 16:(tcg + 1) * 16],
                             r2_c1[:], r2_t[:])
                v.tensor_copy(vdmy[:], vdmy[:]).then_inc(E["sem"]["cV"], 1)
            S.emit("v", fv2)
            S.inc("cV")
        state["ctok_cV"] = S.val("cV")

    def emit_experts(h, ps_gs, ps_u, daccs, tmp_evs):
        gu_last_use = [0, 0]
        gu_loaded = [0, 0]
        silu_ca = [0, 0]
        h_cv = 0
        dacc_free = [0, 0]
        tmp_free = [0, 0]
        dw_last_use = [state.get("arena_free_cP", 0)] * 2
        dw_loaded = [0, 0]
        gseq = 0
        dwseq = 0
        it = 0
        entry = state.get("arena_free_cP", 0)
        gu_loaded[0] = load_gu(0, 0, 0, entry)
        gu_loaded[1] = load_gu(0, 1, 1, entry)
        dw_loaded[0] = load_dwq(0, 0, 0, entry)
        dw_loaded[1] = load_dwq(0, 1, 1, entry)
        for e in range(n_exp):
            for fc in range(6):
                slot = gseq % 2
                _wait(S, "t", "dG", gu_loaded[slot])
                _wait(S, "t", "cV", state["xn_cV"])
                for tb in range(2):
                    gslot = it % 2
                    it += 1
                    pg = ps_gs[gslot]
                    _wait(S, "t", "cA", silu_ca[gslot])

                    def ftg(Ee, slot=slot, tb=tb, pg=pg):
                        E = Ee
                        for dc in range(16):
                            ins = E["t"].matmul(
                                pg[:],
                                gu[slot][:, dc * 128:(dc + 1) * 128],
                                xn_sb[:, dc * TH + tb * 512:dc * TH + tb * 512 + 512],
                                start=(dc == 0), stop=(dc == 15),
                            )
                            if dc == 15:
                                ins.then_inc(E["sem"]["cP"], 1)
                    S.emit("t", ftg)
                    cp_g = S.inc("cP")
                    _wait(S, "t", "cV", h_cv)

                    def ftu(Ee, slot=slot, tb=tb, ps_u=ps_u):
                        E = Ee
                        for dc in range(16):
                            ins = E["t"].matmul(
                                ps_u[:],
                                gu[slot][:, 2048 + dc * 128:2048 + (dc + 1) * 128],
                                xn_sb[:, dc * TH + tb * 512:dc * TH + tb * 512 + 512],
                                start=(dc == 0), stop=(dc == 15),
                            )
                            if dc == 15:
                                ins.then_inc(E["sem"]["cP"], 1)
                    S.emit("t", ftu)
                    cp_u = S.inc("cP")
                    _wait(S, "a", "cP", cp_g)
                    _wait(S, "a", "cV", h_cv)

                    def fa(Ee, pg=pg):
                        Ee["a"].activation(silu_sb[:], pg[:], AF.Silu)
                        Ee["a"].activation(admy[:], admy[:], AF.Identity
                                           ).then_inc(Ee["sem"]["cA"], 1)
                    S.emit("a", fa)
                    silu_ca[gslot] = S.inc("cA")
                    _wait(S, "v", "cA", silu_ca[gslot])
                    _wait(S, "v", "cP", cp_u)

                    def fv(Ee, fc=fc, tb=tb, ps_u=ps_u):
                        Ee["v"].tensor_mul(
                            h_all[:, fc * TH + tb * 512:fc * TH + tb * 512 + 512],
                            ps_u[:], silu_sb[:])
                        Ee["v"].tensor_copy(vdmy[:], vdmy[:]
                                            ).then_inc(Ee["sem"]["cV"], 1)
                    S.emit("v", fv)
                    h_cv = S.inc("cV")
                gu_last_use[slot] = S.val("cP")
                ne, nfc = (e, fc + 2) if fc < 4 else (e + 1, fc - 4)
                if ne < n_exp:
                    gu_loaded[slot] = load_gu(ne, nfc, slot, gu_last_use[slot])
                gseq += 1
            _wait(S, "t", "cV", h_cv)
            for dq in range(4):
                slot = dwseq % 2
                dwseq += 1
                _wait(S, "t", "dG", dw_loaded[slot])
                for tl in range(8):
                    tcg = h * 8 + tl
                    dslot = tl % 2
                    dacc = daccs[dslot]
                    _wait(S, "t", "cA" if EVAC_ACT else "cV",
                          dacc_free[dslot])

                    def ft(Ee, tl=tl, slot=slot, dacc=dacc):
                        E = Ee
                        for fc in range(6):
                            ins = E["t"].matmul(
                                dacc[:],
                                h_all[:, fc * TH + tl * 128:fc * TH + (tl + 1) * 128],
                                dwq[slot][:, fc * 512:(fc + 1) * 512],
                                start=(fc == 0), stop=(fc == 5),
                            )
                            if fc == 5:
                                ins.then_inc(E["sem"]["cP"], 1)
                    S.emit("t", ft)
                    cp_d = S.inc("cP")
                    if EVAC_ACT:
                        _wait(S, "a", "cP", cp_d)
                        _wait(S, "a", "cV", state["ctok_cV"])
                        _wait(S, "a", "cV", tmp_free[dslot])

                        def fa(Ee, e=e, tcg=tcg, dacc=dacc,
                               tmp=tmp_evs[dslot]):
                            Ee["a"].activation(
                                tmp[:], dacc[:], AF.Identity,
                                scale=c_tok[:, tcg * 16 + e:tcg * 16 + e + 1])
                            Ee["a"].activation(admy[:], admy[:], AF.Identity
                                               ).then_inc(Ee["sem"]["cA"], 1)
                        S.emit("a", fa)
                        dacc_free[dslot] = S.inc("cA")
                        _wait(S, "v", "cA", dacc_free[dslot])

                        def fv(Ee, tcg=tcg, dq=dq, tmp=tmp_evs[dslot]):
                            E = Ee
                            xs = x_sb[:, tcg * 2048 + dq * 512:
                                      tcg * 2048 + (dq + 1) * 512]
                            E["v"].tensor_add(xs, tmp[:], xs)
                            E["v"].tensor_copy(vdmy[:], vdmy[:]).then_inc(
                                E["sem"]["cV"], 1)
                        S.emit("v", fv)
                        tmp_free[dslot] = S.inc("cV")
                    else:
                        _wait(S, "v", "cP", cp_d)
                        _wait(S, "v", "cV", state["ctok_cV"])

                        def fv(Ee, e=e, tcg=tcg, dq=dq, dacc=dacc):
                            E = Ee
                            E["v"].tensor_scalar(
                                tmp_ev[:], dacc[:],
                                c_tok[:, tcg * 16 + e:tcg * 16 + e + 1], None,
                                ALU.mult)
                            E["v"].tensor_copy(vdmy[:], vdmy[:])
                            xs = x_sb[:, tcg * 2048 + dq * 512:
                                      tcg * 2048 + (dq + 1) * 512]
                            E["v"].tensor_add(xs, tmp_ev[:], xs)
                            E["v"].tensor_copy(vdmy[:], vdmy[:]).then_inc(
                                E["sem"]["cV"], 1)
                        S.emit("v", fv)
                        dacc_free[dslot] = S.inc("cV")
                        tmp_free[dslot] = dacc_free[dslot]
                dw_last_use[slot] = S.val("cP")
                nde, ndq = (e, dq + 2) if dq < 2 else (e + 1, dq - 2)
                if nde < n_exp:
                    dw_loaded[slot] = load_dwq(nde, ndq, slot,
                                               dw_last_use[slot])
        state["ffn_done_cV"] = S.val("cV")
        state["ffn_done_cP"] = S.val("cP")
        state["ffn_done_cA"] = S.val("cA")

    def emit_qkv(h, ps_qk, ps_v, par_sv):
        entry_cp = S.val("cP")
        wq_loaded = [0, 0]
        wq_last_use = [entry_cp, entry_cp]
        rope_cv = 0
        stq_stored = [0, 0]
        stv_stored = [0, 0]
        for mi in range(20):
            slot = mi % 2
            _wait(S, "g", "cP", wq_last_use[slot])
            def fg(Ee, mi=mi, slot=slot):
                wap = wqkv[:, mi * 128:(mi + 1) * 128].rearrange(
                    "(dc p) f -> p dc f", p=128)
                for hh in range(2):
                    Ee["g"].dma_start(
                        out=wqb[slot][:, hh * 1024:(hh + 1) * 1024],
                        in_=wap[:, hh * 8:(hh + 1) * 8, :],
                    ).then_inc(Ee["sem"]["dG"], 16)
            S.emit("g", fg)
            wq_loaded[slot] = S.inc("dG", 32)
            for tb in range(2):
                pp = (mi * 2 + tb) % 2
                _wait(S, "t", "dG", wq_loaded[slot])
                _wait(S, "t", "cV", rope_cv if (mi * 2 + tb) >= 2 else state.get("qkv_start_cV", 0))
                def ft(Ee, slot=slot, tb=tb, pp=pp, ps_qk=ps_qk):
                    E = Ee
                    for dc in range(16):
                        ins = E["t"].matmul(
                            ps_qk[pp][:],
                            wqb[slot][:, dc * 128:(dc + 1) * 128],
                            xn_sb[:, dc * TH + tb * 512:dc * TH + tb * 512 + 512],
                            start=(dc == 0), stop=(dc == 15),
                        )
                        if dc == 15:
                            ins.then_inc(E["sem"]["cP"], 1)
                S.emit("t", ft)
                wq_last_use[slot] = S.inc("cP")
                cp = wq_last_use[slot]
                _wait(S, "v", "cP", cp)
                _wait(S, "v", "dS", state["rope_dS"])
                _wait(S, "v", "dS", stq_stored[pp])
                toff = h * TH + tb * 512

                def fv(Ee, pp=pp, toff=toff, ps_qk=ps_qk):
                    E = Ee
                    v = E["v"]
                    v.tensor_mul(rtb[:], ps_qk[pp][:],
                                 ropes_sb[:, toff:toff + 512])
                    v.tensor_mul(rta[:], ps_qk[pp][:],
                                 ropec_sb[:, toff:toff + 512])
                    v.tensor_copy(rth[0:64, :], rtb[64:128, :])
                    v.tensor_copy(vdmy[:], vdmy[:])
                    v.tensor_sub(stq[pp][0:64, :], rta[0:64, :], rth[0:64, :])
                    v.tensor_copy(rth[0:64, :], rta[64:128, :])
                    v.tensor_copy(vdmy[:], vdmy[:])
                    v.tensor_add(stq[pp][64:128, :], rtb[0:64, :],
                                 rth[0:64, :])
                    v.tensor_copy(vdmy[:], vdmy[:]).then_inc(E["sem"]["cV"], 1)
                S.emit("v", fv)
                rope_cv = S.inc("cV")
                _wait(S, "s", "cV", rope_cv)

                def fs(Ee, mi=mi, pp=pp, toff=toff, par_sv_key="par"):
                    E = Ee
                    if mi < 16:
                        E["s"].dma_start(
                            out=q_dram[mi * 128:(mi + 1) * 128,
                                       toff:toff + 512],
                            in_=stq[pp][:],
                        ).then_inc(E["sem"]["dS"], 16)
                    else:
                        hg = mi - 16
                        E["s"].dma_start(
                            out=k_dram[hg * 128:(hg + 1) * 128,
                                       toff:toff + 512],
                            in_=stq[pp][:],
                        ).then_inc(E["sem"]["dS"], 16)
                S.emit("s", fs)
                stq_stored[pp] = S.inc("dS", 16)
        _wait(S, "g", "cP", max(wq_last_use))
        def fgv(Ee):
            wap = wqkv[:, 2560:3072].rearrange("(dc p) f -> p dc f", p=128)
            for hh in range(2):
                Ee["g"].dma_start(
                    out=wv_sb[:, hh * 4096:(hh + 1) * 4096],
                    in_=wap[:, hh * 8:(hh + 1) * 8, :],
                ).then_inc(Ee["sem"]["dG"], 16)
        S.emit("g", fgv)
        wv_loaded = S.inc("dG", 32)
        v_evac_cv = 0
        for tl in range(8):
            tcg = h * 8 + tl
            pp = tl % 2
            _wait(S, "t", "dG", wv_loaded)
            _wait(S, "t", "cV", v_evac_cv if tl >= 2 else state["qkv_start_cV"])
            def ft(Ee, tl=tl, pp=pp, ps_v=ps_v):
                E = Ee
                for dc in range(16):
                    ins = E["t"].matmul(
                        ps_v[pp][:],
                        xn_sb[:, dc * TH + tl * 128:dc * TH + (tl + 1) * 128],
                        wv_sb[:, dc * 512:(dc + 1) * 512],
                        start=(dc == 0), stop=(dc == 15),
                    )
                    if dc == 15:
                        ins.then_inc(E["sem"]["cP"], 1)
            S.emit("t", ft)
            cp = S.inc("cP")
            _wait(S, "v", "cP", cp)
            _wait(S, "v", "dS", stv_stored[pp])

            def fv(Ee, pp=pp, ps_v=ps_v):
                Ee["v"].tensor_copy(stv[pp][:], ps_v[pp][:])
                Ee["v"].tensor_copy(vdmy[:], vdmy[:]).then_inc(
                    Ee["sem"]["cV"], 1)
            S.emit("v", fv)
            v_evac_cv = S.inc("cV")
            _wait(S, "s", "cV", v_evac_cv)

            def fs(Ee, tcg=tcg, pp=pp):
                Ee["s"].dma_start(
                    out=v_dram[tcg * 128:(tcg + 1) * 128, :],
                    in_=stv[pp][:],
                ).then_inc(Ee["sem"]["dS"], 16)
            S.emit("s", fs)
            stv_stored[pp] = S.inc("dS", 16)
        state["kv_stored_dS"] = S.val("dS")

    def emit_x1_spill(h, to_dbg=False):
        for tl in range(8):
            tcg = h * 8 + tl
            def fs(Ee, tcg=tcg, to_dbg=to_dbg):
                E = Ee
                E["s"].dma_start(
                    out=x1_dram[tcg * 128:(tcg + 1) * 128, :],
                    in_=x_sb[:, tcg * 2048:(tcg + 1) * 2048],
                ).then_inc(E["sem"]["dS"], 16)
                if to_dbg and dbg:
                    E["s"].dma_start(
                        out=dbg_x1[tcg * 128:(tcg + 1) * 128, :],
                        in_=x_sb[:, tcg * 2048:(tcg + 1) * 2048],
                    ).then_inc(E["sem"]["dS"], 16)
            S.emit("s", fs)
            S.inc("dS", 32 if (to_dbg and dbg) else 16)
        state[f"x1_spill_{h}_dS"] = S.val("dS")

    def emit_flag_sync():
        _wait(S, "s", "dS", state["kv_stored_dS"])

        def fs(Ee):
            E = Ee
            s = E["s"]
            s.dma_start(
                out=sh_k.rearrange("(b r) c -> b (r c)", b=32)[
                    ts(state["par"], 16), :],
                in_=k_dram.rearrange("(a r) c -> a (r c)", a=16),
            ).then_inc(E["sem"]["dS"], 16)
            s.dma_start(
                out=sh_v.rearrange("(b r) c -> b (r c)", b=32)[
                    ts(state["par"], 16), :],
                in_=v_dram.rearrange("(a r) c -> a (r c)", a=16),
            ).then_inc(E["sem"]["dS"], 16)
            s.wait_ge(E["sem"]["dS"], S_KV_PUB[0])
            s.dma_start(out=sh_flag[ts(state["par"], 1), :],
                        in_=nonce_sb[:]).then_inc(E["sem"]["dS"], 16)
            rn = s.alloc_register("rn")
            rf = s.alloc_register("rf")
            rd = s.alloc_register("rd")
            rc = s.alloc_register("rc")
            s.load(rn, nonce_sb[0:1, 0:1])
            s.reg_mov(rd, 1)
            s.reg_mov(rc, 0)
            with s.While(rd):
                s.reg_add(rc, rc, 16)
                s.dma_start(out=flag_sb[:],
                            in_=sh_flag[ts(state["parx"], 1), :]
                            ).then_inc(E["sem"]["dP"], 16)
                s.wait_ge(E["sem"]["dP"], rc)
                s.load(rf, flag_sb[0:1, 0:1])
                s.reg_alu(rd, rf, rn, ALU.bitwise_xor)
            s.sem_inc(E["sem"]["dS"], 16)
        S_KV_PUB[0] = S.val("dS") + 32
        S.emit("s", fs)
        S.inc("dS", 64)
        state["kv_ready_dS"] = S.val("dS")

    def emit_attn_overlay_loads():
        _wait(S, "g", "dS", state["x1_spill_0_dS"])
        _wait(S, "g", "cA", state["qkv_end_cA"])
        _wait(S, "g", "cV", state["qkv_end_cV"])
        def fg(Ee):
            E = Ee
            for q4 in range(4):
                E["g"].dma_start(
                    out=wo_sb[:, q4 * 8192:(q4 + 1) * 8192],
                    in_=wo[q4 * 512:(q4 + 1) * 512, :]
                    .rearrange("(hc p) d -> p hc d", p=128),
                ).then_inc(E["sem"]["dG"], 16)
        S.emit("g", fg)
        state["wo_loaded_dG"] = S.inc("dG", 64)
        _wait(S, "s", "dS", state["x1_spill_1_dS"])
        _wait(S, "s", "cA", state["qkv_end_cA"])
        _wait(S, "s", "cV", state["qkv_end_cV"])

        def fs(Ee):
            Ee["s"].dma_start(out=cmask_sb[:], in_=cmask_in).then_inc(
                Ee["sem"]["dS"], 16)
        S.emit("s", fs)
        state["cmask_dS"] = S.inc("dS", 16)

    def emit_attention(ps_s, ps_pt, ps_os, ps_wo):
        _wait(S, "t", "cV", S.val("cV"))
        _wait(S, "t", "cA", S.val("cA"))
        q_loaded = [0, 0]
        q_last_use = [0, 0]
        kv_loaded = [0, 0]
        kv_last_use = [0, 0]
        x1_loaded = [0, 0]
        x1_free = [0, 0]
        s_free = [0, 0]
        psb_free = [0, 0]
        lp_free = [0, 0]
        pt_free = [0, 0]
        o_free = 0
        ofm_ca = 0
        wo_free = 0
        sseq = 0
        tseq = 0
        kvseq = 0
        for qt in range(n_qt):
            qslot = qt % 2
            _wait(S, "s", "dS", state["kv_ready_dS"])
            _wait(S, "s", "cP", q_last_use[qslot])

            def fsq(Ee, qt=qt, qslot=qslot):
                qap = q_dram[:, qt * 128:(qt + 1) * 128].rearrange(
                    "(hc p) q -> p hc q", p=128)
                for hh in range(2):
                    Ee["s"].dma_start(
                        out=qbuf[qslot][:, hh * 1024:(hh + 1) * 1024],
                        in_=qap[:, hh * 8:(hh + 1) * 8, :],
                    ).then_inc(Ee["sem"]["dS"], 16)
            S.emit("s", fsq)
            q_loaded[qslot] = S.inc("dS", 32)
            xslot = qt % 2
            _wait(S, "s", "dS", state["x1_spill_1_dS"])
            _wait(S, "s", "cV", x1_free[xslot])

            def fsx(Ee, qt=qt, xslot=xslot):
                Ee["s"].dma_start(
                    out=x1t[xslot][:],
                    in_=x1_dram[qt * 128:(qt + 1) * 128, :],
                ).then_inc(Ee["sem"]["dS"], 16)
            S.emit("s", fsx)
            x1_loaded[xslot] = S.inc("dS", 16)

            nkb = qt // 4 + 1
            for hg in range(4):
                tuples = [(pk, kb, hq) for pk in range(2)
                          for kb in range(nkb) for hq in range(4)]
                n = len(tuples)
                info = [None] * n

                def prep_S(j, hg=hg, qslot=qslot, nkb=nkb, tuples=tuples,
                           info=info):
                    nonlocal kvseq, sseq, tseq
                    pk, kb, hq = tuples[j]
                    d = {"pk": pk, "kb": kb, "hq": hq, "ss": sseq % 2,
                         "ts": tseq % 2, "diag": kb == nkb - 1,
                         "isf": pk == 0 and kb == 0}
                    sseq += 1
                    tseq += 1
                    info[j] = d
                    if hq == 0:
                        kvslot = kvseq % 2
                        kvseq += 1
                        _wait(S, "s", "cP", kv_last_use[kvslot])

                        def fskv(Ee, pk=pk, hg=hg, kb=kb, kvslot=kvslot):
                            E = Ee
                            E["s"].dma_start(
                                out=kbuf[kvslot][:],
                                in_=sh_k[pk * 512 + hg * 128:
                                         pk * 512 + (hg + 1) * 128,
                                         kb * 512:(kb + 1) * 512],
                            ).then_inc(E["sem"]["dS"], 16)
                            E["s"].dma_start(
                                out=vbuf[kvslot][:],
                                in_=sh_v[pk * T + kb * 512:
                                         pk * T + (kb + 1) * 512,
                                         hg * 128:(hg + 1) * 128]
                                .rearrange("(i p) d -> p i d", p=128),
                            ).then_inc(E["sem"]["dS"], 16)
                        S.emit("s", fskv)
                        kv_loaded[kvslot] = S.inc("dS", 32)
                        d["kvs"] = kvslot
                    else:
                        d["kvs"] = info[j - 1]["kvs"]
                    _wait(S, "t", "dS", kv_loaded[d["kvs"]])
                    _wait(S, "t", "dS", q_loaded[qslot])
                    _wait(S, "t", "cA", s_free[d["ss"]])

                    def ftS(Ee, hg=hg, hq=hq, ss=d["ss"], kvs=d["kvs"],
                            qslot=qslot):
                        Ee["t"].matmul(
                            ps_s[ss][:],
                            qbuf[qslot][:, (hg * 4 + hq) * 128:
                                        (hg * 4 + hq + 1) * 128],
                            kbuf[kvs][:],
                            start=True, stop=True,
                        ).then_inc(Ee["sem"]["cP"], 1)
                    S.emit("t", ftS)
                    d["cpS"] = S.inc("cP")

                prep_S(0)
                for i in range(n + 1):
                    d = info[i] if i < n else None
                    p = info[i - 1] if i > 0 else None
                    if paranoid:
                        for _e in (("t", "v", "a") if paranoid >= 2
                                   else (PARA_ENG,)):
                            _wait(S, _e, "cP", S.val("cP"))
                            _wait(S, _e, "cV", S.val("cV"))
                            _wait(S, _e, "cA", S.val("cA"))
                    if d is not None:
                        _wait(S, "v", "cP", d["cpS"])
                        if d["diag"]:
                            _wait(S, "v", "dS", state["cmask_dS"])
                    if p is not None:
                        _wait(S, "v", "cA", p["ca"])
                        if not p["isf"]:
                            _wait(S, "v", "cP",
                                  state.get("pv_cp_%d" % p["hq"], 0))

                    def fvc(Ee, d=d, p=p, qt=qt):
                        E = Ee
                        v = E["v"]
                        lastw = [None]

                        def op(emitfn, reads, writes):
                            if lastw[0] is not None and lastw[0] in reads:
                                v.tensor_copy(vdmy[:], vdmy[:])
                            emitfn()
                            lastw[0] = writes
                        if d is not None and d["diag"] and not no_mask:
                            moff = (d["pk"] * 4 + qt % 4) * 512
                            op(lambda: v.tensor_add(
                                ps_s[d["ss"]][:], ps_s[d["ss"]][:],
                                cmask_sb[:, moff:moff + 512]),
                               (), "ps_s")
                        if p is not None:
                            op(lambda: v.reduce_sum(
                                fl_lp[p["ss"]][:], psb[p["ss"]][:], AX),
                               (), "fl_lp")
                        if d is not None:
                            mdst = (fl_m[d["hq"]] if d["isf"]
                                    else fl_mn[d["ss"]])
                            op(lambda: v.reduce_max(
                                mdst[:], ps_s[d["ss"]][:], AX),
                               ("ps_s",), "mx")
                        if p is not None:
                            if p["isf"]:
                                op(lambda: v.tensor_copy(
                                    fl_l[p["hq"]][:], fl_lp[p["ss"]][:]),
                                   ("fl_lp",), "fl_l")
                            else:
                                op(lambda: v.tensor_scalar(
                                    fl_l[p["hq"]][:], fl_l[p["hq"]][:],
                                    fl_al[p["ss"]][:], None, ALU.mult),
                                   ("fl_l",), "fl_l")
                        if d is not None and not d["isf"]:
                            op(lambda: v.tensor_max(
                                fl_mn[d["ss"]][:], fl_mn[d["ss"]][:],
                                fl_m[d["hq"]][:]),
                               ("mx",), "mx")
                        if p is not None and not p["isf"]:
                            if no_resc:
                                pass
                            else:
                                op(lambda: v.tensor_scalar(
                                    ps_os[p["hq"]][:, 0:128],
                                    ps_os[p["hq"]][:, 0:128],
                                    fl_al[p["ss"]][:], None, ALU.mult),
                                   (), "ps_o")
                        if d is not None and not d["isf"]:
                            op(lambda: v.tensor_sub(
                                fl_dm[d["ss"]][:], fl_m[d["hq"]][:],
                                fl_mn[d["ss"]][:]),
                               ("mx",), "fl_dm")
                        if p is not None and not p["isf"]:
                            op(lambda: v.tensor_add(
                                fl_l[p["hq"]][:], fl_l[p["hq"]][:],
                                fl_lp[p["ss"]][:]),
                               ("fl_l", "fl_lp"), "fl_l")
                        if d is not None:
                            if not d["isf"]:
                                op(lambda: v.tensor_copy(
                                    fl_m[d["hq"]][:], fl_mn[d["ss"]][:]),
                                   ("mx",), "fl_m")
                                op(lambda: v.tensor_scalar(
                                    fl_bias[d["ss"]][:], fl_mn[d["ss"]][:],
                                    -SC, None, ALU.mult),
                                   ("mx",), "fl_bias")
                            else:
                                op(lambda: v.tensor_scalar(
                                    fl_bias[d["ss"]][:], fl_m[d["hq"]][:],
                                    -SC, None, ALU.mult),
                                   ("mx",), "fl_bias")
                        v.tensor_copy(vdmy[:], vdmy[:]).then_inc(
                            E["sem"]["cV"], 1)
                    S.emit("v", fvc)
                    cvl = S.inc("cV")
                    if d is not None:
                        d["cvl"] = cvl
                    if p is not None:
                        p["cvl_flush"] = cvl
                    if d is not None:
                        _wait(S, "a", "cV", cvl)
                    if p is not None:
                        _wait(S, "a", "cP", p["cpT"])

                    def fac(Ee, d=d, p=p):
                        E = Ee
                        a = E["a"]
                        if p is not None:
                            a.activation(ptsb[p["ts"]][:], ps_pt[:],
                                         AF.Identity)
                        if d is not None:
                            if not d["isf"]:
                                a.activation(fl_al[d["ss"]][:],
                                             fl_dm[d["ss"]][:],
                                             AF.Exp, scale=SC)
                            a.activation(psb[d["ss"]][:], ps_s[d["ss"]][:],
                                         AF.Exp, bias=fl_bias[d["ss"]][:],
                                         scale=SC)
                        a.activation(admy[:], admy[:], AF.Identity
                                     ).then_inc(E["sem"]["cA"], 1)
                    S.emit("a", fac)
                    ca = S.inc("cA")
                    if d is not None:
                        d["ca"] = ca
                        s_free[d["ss"]] = ca
                    if p is not None:
                        p["ca_evac"] = ca
                    if i + 1 < n:
                        prep_S(i + 1)
                    if d is not None:
                        _wait(S, "t", "cA", d["ca"])

                        def ftT(Ee, ss=d["ss"]):
                            E = Ee
                            for u in range(4):
                                ins = E["t"].matmul(
                                    ps_pt[:, u * 128:(u + 1) * 128],
                                    psb[ss][:, u * 128:(u + 1) * 128],
                                    ident_sb[:],
                                    start=True, stop=True,
                                )
                                if u == 3:
                                    ins.then_inc(E["sem"]["cP"], 1)
                        S.emit("t", ftT)
                        d["cpT"] = S.inc("cP")
                    if p is not None:
                        _wait(S, "t", "cV", p["cvl_flush"])
                        _wait(S, "t", "cA", p["ca_evac"])
                        _wait(S, "t", "cA", state.get("o_free_a", 0))

                        def ftPV(Ee, tslot=p["ts"], kvs=p["kvs"],
                                 hq=p["hq"], st=p["isf"]):
                            E = Ee
                            for u in range(4):
                                ins = E["t"].matmul(
                                    ps_os[hq][:, 0:128],
                                    ptsb[tslot][:, u * 128:(u + 1) * 128],
                                    vbuf[kvs][:, u * 128:(u + 1) * 128],
                                    start=(st and u == 0), stop=False,
                                    skip_group_check=True,
                                )
                                if u == 3:
                                    ins.then_inc(E["sem"]["cP"], 1)
                        S.emit("t", ftPV)
                        state["pv_cp_%d" % p["hq"]] = S.inc("cP")
                        kv_last_use[p["kvs"]] = S.val("cP")
                for hq in range(4):
                    _wait(S, "v", "cP", state.get("pv_cp_%d" % hq, 0))

                def fvr(Ee):
                    E = Ee
                    v = E["v"]
                    for hq in range(4):
                        v.reciprocal(fl_rl[hq][:], fl_l[hq][:])
                    v.tensor_copy(vdmy[:], vdmy[:]
                                  ).then_inc(E["sem"]["cV"], 1)
                S.emit("v", fvr)
                cvr = S.inc("cV")
                _wait(S, "a", "cV", cvr)

                _wait(S, "a", "dS", state.get("otdump_dS", 0))

                def fao(Ee, hg=hg):
                    E = Ee
                    for hq in range(4):
                        E["a"].activation(
                            otok[:, hg * 512 + hq * 128:
                                 hg * 512 + (hq + 1) * 128],
                            ps_os[hq][:, 0:128],
                            AF.Identity, scale=fl_rl[hq][:])
                    E["a"].activation(admy[:], admy[:], AF.Identity
                                      ).then_inc(E["sem"]["cA"], 1)
                S.emit("a", fao)
                o_free_a = S.inc("cA")
                state["o_free_a"] = o_free_a
            q_last_use[qslot] = S.val("cP")
            for g4 in range(4 if fl_lvl >= 8 else 0):
                _wait(S, "t", "cA", state.get("o_free_a", 0))
                _wait(S, "t", "cA", ofm_ca)

                def ftO(Ee, g4=g4):
                    E = Ee
                    for j in range(4):
                        hc = g4 * 4 + j
                        ins = E["t"].matmul(
                            ps_pt[:, j * 128:(j + 1) * 128],
                            otok[:, hc * 128:(hc + 1) * 128],
                            ident_sb[:],
                            start=True, stop=True,
                        )
                        if j == 3:
                            ins.then_inc(E["sem"]["cP"], 1)
                S.emit("t", ftO)
                cpO = S.inc("cP")
                _wait(S, "a", "cP", cpO)

                def faO(Ee, g4=g4):
                    E = Ee
                    for j in range(4):
                        E["a"].activation(
                            ofm[:, (4 * g4 + j) * 128:(4 * g4 + j + 1) * 128],
                            ps_pt[:, j * 128:(j + 1) * 128],
                            AF.Identity)
                    E["a"].activation(admy[:], admy[:], AF.Identity
                                      ).then_inc(E["sem"]["cA"], 1)
                S.emit("a", faO)
                ofm_ca = S.inc("cA")
            for dh in range(4 if fl_lvl >= 8 else 0):
                _wait(S, "t", "cA", ofm_ca)
                _wait(S, "t", "dG", state["wo_loaded_dG"])
                _wait(S, "t", "cV", wo_free)

                def ftW(Ee, dh=dh):
                    E = Ee
                    for hc in range(16):
                        ins = E["t"].matmul(
                            ps_wo[:],
                            ofm[:, hc * 128:(hc + 1) * 128],
                            wo_sb[:, hc * 2048 + dh * 512:
                                  hc * 2048 + (dh + 1) * 512],
                            start=(hc == 0), stop=(hc == 15),
                        )
                        if hc == 15:
                            ins.then_inc(E["sem"]["cP"], 1)
                S.emit("t", ftW)
                cpW = S.inc("cP")
                _wait(S, "v", "cP", cpW)
                _wait(S, "v", "dS", x1_loaded[xslot])

                def fvW(Ee, dh=dh, xslot=xslot):
                    E = Ee
                    xs = x1t[xslot][:, dh * 512:(dh + 1) * 512]
                    E["v"].tensor_add(xs, ps_wo[:], xs)
                    E["v"].tensor_copy(vdmy[:], vdmy[:]).then_inc(
                        E["sem"]["cV"], 1)
                S.emit("v", fvW)
                wo_free = S.inc("cV")
            if fl_lvl < 8:
                continue
            _wait(S, "s", "cV", wo_free)

            def fsx2(Ee, qt=qt, xslot=xslot):
                Ee["s"].dma_start(
                    out=x1_dram[qt * 128:(qt + 1) * 128, :],
                    in_=x1t[xslot][:],
                ).then_inc(Ee["sem"]["dS"], 16)
            S.emit("s", fsx2)
            x1_free[xslot] = S.val("cV")
            S.inc("dS", 16)
        state["x2_stored_dS"] = S.val("dS")
        state["attn_cP"] = S.val("cP")
        state["attn_cV"] = S.val("cV")
        state["attn_cA"] = S.val("cA")

    def emit_x2_reload():
        _wait(S, "s", "dS", state["x2_stored_dS"])
        _wait(S, "s", "cP", state["attn_cP"])
        _wait(S, "s", "cV", state["attn_cV"])
        _wait(S, "s", "cA", state["attn_cA"])
        for tc in range(NTC):
            def fs(Ee, tc=tc):
                Ee["s"].dma_start(
                    out=x_sb[:, tc * 2048:(tc + 1) * 2048],
                    in_=x1_dram[tc * 128:(tc + 1) * 128, :],
                ).then_inc(Ee["sem"]["dS"], 16)
            S.emit("s", fs)
            S.inc("dS", 16)
        state["x_loaded_dS"] = S.val("dS")

    def emit_output(h):
        _wait(S, "s", "cV", state["ffn_done_cV"])
        for tl in range(8):
            tcg = h * 8 + tl
            def fs(Ee, tcg=tcg):
                Ee["s"].dma_start(
                    out=out[tcg * 128:(tcg + 1) * 128, :],
                    in_=x_sb[:, tcg * 2048:(tcg + 1) * 2048],
                ).then_inc(Ee["sem"]["dS"], 16)
            S.emit("s", fs)
            S.inc("dS", 16)

    def emit_dump_and_finish():
        cv = S.val("cV")
        ca = S.val("cA")
        cp = S.val("cP")
        _wait(S, "s", "cV", cv)
        _wait(S, "s", "cA", ca)
        _wait(S, "s", "cP", cp)
        for tc in range(NTC):
            def fs(Ee, tc=tc):
                Ee["s"].dma_start(
                    out=out[tc * 128:(tc + 1) * 128, :],
                    in_=x_sb[:, tc * 2048:(tc + 1) * 2048],
                ).then_inc(Ee["sem"]["dS"], 16)
            S.emit("s", fs)
            S.inc("dS", 16)
        tot = S.val("dS")

        def ff(Ee):
            Ee["s"].wait_ge(Ee["sem"]["dS"], tot)
        S.emit("s", ff)

    phase0()

    if stop_after in ("ln", "router"):
        with ExitStack() as ps_ctx:
            e_ = ps_ctx.enter_context
            ps_tr = e_(nc.psum_tensor("ps_tr_dbg", [128, 512], BF16))
            ps_sc = e_(nc.psum_tensor("ps_sc_dbg", [128, 512], F32))
            emit_ln(0, ps_tr, xn_tok_f)
            if stop_after == "router":
                emit_router(0, ps_sc)
        emit_dump_and_finish()
        return _assemble(nc, S)

    for h in range(2):
        with ExitStack() as ps_ctx:
            e_ = ps_ctx.enter_context
            ps_g = e_(nc.psum_tensor(f"ps_g_{h}", [128, 512], F32))
            ps_g2 = e_(nc.psum_tensor(f"ps_g2_{h}", [128, 512], F32))
            ps_u = e_(nc.psum_tensor(f"ps_u_{h}", [128, 512], F32))
            dacc0 = e_(nc.psum_tensor(f"dacc0_{h}", [128, 512], F32))
            dacc1 = e_(nc.psum_tensor(f"dacc1_{h}", [128, 512], F32))
            ps_tr = e_(nc.psum_tensor(f"ps_tr_{h}", [128, 512], BF16))
            ps_sc = e_(nc.psum_tensor(f"ps_sc_{h}", [128, 512], F32))
            emit_ln(h, ps_tr, xn_tok_f)
            emit_router(h, ps_sc)
            emit_experts(h, (ps_g, ps_g2), ps_u, (dacc0, dacc1),
                         (tmp_ev, ps_sc))
        state["arena_free_cP"] = S.val("cP")

    if stop_after == "ffn1":
        _wait(S, "s", "cV", S.val("cV"))
        _wait(S, "s", "cA", S.val("cA"))
        _wait(S, "s", "cP", S.val("cP"))

        def fdbg(Ee):
            Ee["s"].dma_start(out=dbg_cn[:], in_=x_sb[:, 0:1024]
                              ).then_inc(Ee["sem"]["dS"], 16)
            Ee["s"].dma_start(out=dbg_sc[:, 0:256], in_=scores_sb[:]
                              ).then_inc(Ee["sem"]["dS"], 16)
            Ee["s"].dma_start(out=dbg_sc[:, 256:512], in_=c_tok[:]
                              ).then_inc(Ee["sem"]["dS"], 16)
            Ee["s"].dma_start(out=dbg_xn[:], in_=xn_sb[:, 0:2048]
                              ).then_inc(Ee["sem"]["dS"], 16)
        S.emit("s", fdbg)
        S.inc("dS", 64)
        emit_dump_and_finish()
        return _assemble(nc, S)

    state["rope_dS"] = 0
    _wait(S, "s", "cP", state["ffn_done_cP"])
    _wait(S, "s", "cV", state["ffn_done_cV"])

    def f_rope(Ee):
        E = Ee
        E["s"].dma_start(out=ropec_sb[:], in_=ropec_in).then_inc(E["sem"]["dS"], 16)
        E["s"].dma_start(out=ropes_sb[:], in_=ropes_in).then_inc(E["sem"]["dS"], 16)
    S.emit("s", f_rope)
    state["rope_dS"] = S.inc("dS", 32)

    def f_pid(Ee):
        pid = nc.partition_id([mybir.EngineType.SP])
        state["par"] = nc.snap(pid % 2, engines=[mybir.EngineType.SP],
                               min_val=0, max_val=1)
        state["parx"] = nc.snap((pid + 1) % 2, engines=[mybir.EngineType.SP],
                                min_val=0, max_val=1)
    S.emit("s", f_pid)

    for h in range(2):
        with ExitStack() as ps_ctx:
            e_ = ps_ctx.enter_context
            ps_tr = e_(nc.psum_tensor(f"ps_tr_q{h}", [128, 512], BF16))
            ps_qk = [e_(nc.psum_tensor(f"ps_qk{i}_{h}", [128, 512], F32))
                     for i in range(2)]
            ps_v = [e_(nc.psum_tensor(f"ps_v{i}_{h}", [128, 512], F32))
                    for i in range(2)]
            state["qkv_start_cV"] = S.val("cV")
            emit_ln(h, ps_tr, xn_tok_q)
            emit_x1_spill(h, to_dbg=True)
            emit_qkv(h, ps_qk, ps_v, None)
    state["qkv_end_cA"] = S.val("cA")
    state["qkv_end_cV"] = S.val("cV")
    emit_flag_sync()
    if stop_after == "qkv":
        emit_dump_and_finish()
        return _assemble(nc, S)
    emit_attn_overlay_loads()

    if paranoid >= 2:
        S.paranoid = True
    with ExitStack() as ps_ctx:
        e_ = ps_ctx.enter_context
        ps_s = [e_(nc.psum_tensor(f"ps_s{i}", [128, 512], F32)) for i in range(2)]
        ps_pt = e_(nc.psum_tensor("ps_pt", [128, 512], F32))
        ps_os = [e_(nc.psum_tensor(f"ps_o{i}", [128, 512], F32))
                 for i in range(4)]
        ps_wo = e_(nc.psum_tensor("ps_wo", [128, 512], F32))
        emit_attention(ps_s, ps_pt, ps_os, ps_wo)
    S.paranoid = False

    emit_x2_reload()
    if stop_after == "attn":
        emit_dump_and_finish()
        return _assemble(nc, S)

    state["arena_free_cP"] = S.val("cP")
    for h in range(2):
        with ExitStack() as ps_ctx:
            e_ = ps_ctx.enter_context
            ps_g = e_(nc.psum_tensor(f"ps2_g_{h}", [128, 512], F32))
            ps_g2 = e_(nc.psum_tensor(f"ps2_g2_{h}", [128, 512], F32))
            ps_u = e_(nc.psum_tensor(f"ps2_u_{h}", [128, 512], F32))
            dacc0 = e_(nc.psum_tensor(f"dacc2_0_{h}", [128, 512], F32))
            dacc1 = e_(nc.psum_tensor(f"dacc2_1_{h}", [128, 512], F32))
            ps_tr = e_(nc.psum_tensor(f"ps2_tr_{h}", [128, 512], BF16))
            ps_sc = e_(nc.psum_tensor(f"ps2_sc_{h}", [128, 512], F32))
            emit_ln(h, ps_tr, xn_tok_f)
            emit_router(h, ps_sc)
            emit_experts(h, (ps_g, ps_g2), ps_u, (dacc0, dacc1),
                         (tmp_ev, ps_sc))
            emit_output(h)
        state["arena_free_cP"] = S.val("cP")

    total_dS = S.val("dS")

    def f_final(Ee):
        Ee["s"].wait_ge(Ee["sem"]["dS"], total_dS)
    S.emit("s", f_final)
    return _assemble(nc, S)


def _assemble(nc, S):
    with ExitStack() as ctx:
        e_ = ctx.enter_context
        sems = {}
        for sn in ("dS", "dG", "dP", "cP", "cV", "cA"):
            sems[sn] = e_(nc.semaphore(sn))
        block = e_(nc.Block())

        def run(eng_key):
            def runner(eng):
                Ee = {"t": nc.tensor, "v": nc.vector, "a": nc.scalar,
                      "g": nc.gpsimd, "s": nc.sync, "sem": sems}
                for fn in S.ops[eng_key]:
                    fn(Ee)
            return runner

        block.tensor(run("t"))
        block.vector(run("v"))
        block.scalar(run("a"))
        block.gpsimd(run("g"))
        block.sync(run("s"))
    return nc




def kernel(L0_attn_x, L0_attn_Wqkv, L0_attn_Wo, L0_ffn_router_w,
           L0_ffn_gate_w, L0_ffn_up_w, L0_ffn_down_w):
    x = np.ascontiguousarray(np.asarray(L0_attn_x, dtype=np.float32))
    wqkv = np.ascontiguousarray(np.asarray(L0_attn_Wqkv, dtype=np.float32))
    wo = np.ascontiguousarray(np.asarray(L0_attn_Wo, dtype=np.float32))
    rw = np.ascontiguousarray(np.asarray(L0_ffn_router_w, dtype=np.float32))
    gw = np.ascontiguousarray(np.asarray(L0_ffn_gate_w, dtype=np.float32))
    uw = np.ascontiguousarray(np.asarray(L0_ffn_up_w, dtype=np.float32))
    dwn = np.ascontiguousarray(np.asarray(L0_ffn_down_w, dtype=np.float32))

    nc = build_nc()
    nonce = np.array([[np.random.randint(1, 2**31)] * 16], dtype=np.uint32)
    consts = [_host_consts(p) for p in range(2)]
    in_maps = []
    for c in range(8):
        b, p = c // 2, c % 2
        ropec, ropes, cmask, ident = consts[p]
        xs = np.ascontiguousarray(x[b, p::2, :])
        in_maps.append({
            "x0": xs, "wqkv": wqkv, "wo": wo, "rw": rw,
            "gw": gw, "uw": uw, "dw": dwn,
            "ropec": ropec, "ropes": ropes, "cmask": cmask,
            "ident": ident, "nonce": nonce,
        })
    res = run_bass_kernel_spmd(nc, in_maps, core_ids=list(range(8)))
    outp = np.empty((4, 4096, 2048), dtype=np.float32)
    for c in range(8):
        b, p = c // 2, c % 2
        outp[b, p::2, :] = np.asarray(res.results[c]["out"])
    return outp


# version 08 (final)
# speedup vs baseline: 1.1506x; 1.0575x over previous
import math
from contextlib import ExitStack

import numpy as np
import ml_dtypes
import concourse.bass as bass
import concourse.mybir as mybir
from concourse.bass import ts
from concourse.bass_utils import run_bass_kernel_spmd

F32 = mybir.dt.float32
BF16 = mybir.dt.bfloat16
U32 = mybir.dt.uint32
AF = mybir.ActivationFunctionType
ALU = mybir.AluOpType
AX = mybir.AxisListType.X

T = 2048
D = 2048
E = 16
F = 768
NH = 16
NKV = 4
HD = 128
TH = 1024
NTC = 16
SC = 1.0 / math.sqrt(HD)
MASKNEG = -3.0e38
EPS = 1e-5



def _host_consts(core_parity):
    theta = 10000.0
    i = np.arange(64, dtype=np.float64)
    freqs = theta ** (-i / 64.0)
    t = np.arange(T, dtype=np.float64)
    pos = 2.0 * t + core_parity
    ang = pos[None, :] * freqs[:, None]
    cos = np.cos(ang)
    sin = np.sin(ang)
    ropec = np.concatenate([cos, cos], axis=0).astype(ml_dtypes.bfloat16)
    ropes = np.concatenate([sin, sin], axis=0).astype(ml_dtypes.bfloat16)

    cm = np.full((128, 8 * 512), MASKNEG, dtype=np.float32)
    p = np.arange(128)[:, None]
    f = np.arange(512)[None, :]
    for pk in range(2):
        strict = 1 if (core_parity == 0 and pk == 1) else 0
        for jm in range(4):
            keep = f <= (128 * jm + p - strict)
            blk = np.where(keep, np.float32(0.0), np.float32(MASKNEG))
            cm[:, (pk * 4 + jm) * 512:(pk * 4 + jm + 1) * 512] = blk
    cmask = cm.astype(ml_dtypes.bfloat16)

    ident = np.eye(128, dtype=ml_dtypes.bfloat16)
    return ropec, ropes, cmask, ident




class Sched:

    ENGS = ("t", "v", "a", "g", "s")

    def __init__(self):
        self.ops = {e: [] for e in self.ENGS}
        self.cnt = {}

    paranoid = False

    def emit(self, eng, fn):
        if self.paranoid and eng in ("t", "v", "a"):
            cp, cv, ca = self.val("cP"), self.val("cV"), self.val("cA")

            def wrapped(E, _fn=fn, _eng=eng, _cp=cp, _cv=cv, _ca=ca):
                if _cp: E[_eng].wait_ge(E["sem"]["cP"], _cp)
                if _cv: E[_eng].wait_ge(E["sem"]["cV"], _cv)
                if _ca: E[_eng].wait_ge(E["sem"]["cA"], _ca)
                _fn(E)
            self.ops[eng].append(wrapped)
        else:
            self.ops[eng].append(fn)

    def inc(self, name, n=1):
        self.cnt[name] = self.cnt.get(name, 0) + n
        return self.cnt[name]

    def val(self, name):
        return self.cnt.get(name, 0)


def _wait(S, eng, sem_name, value):
    if value <= 0:
        return
    def f(E, _sn=sem_name, _v=value):
        E[eng].wait_ge(E["sem"][_sn], _v)
    S.emit(eng, f)




def build_nc(dbg=False, n_exp=E, stop_after=None, n_qt=16, no_mask=False, no_resc=False, fl_lvl=8, paranoid=0, evac_act=False, para_eng="t"):
    EVAC_ACT = evac_act
    PARA_ENG = para_eng
    nc = bass.Bass("TRN2", target_bir_lowering=False)

    x0 = nc.dram_tensor("x0", [T, D], F32, kind="ExternalInput").ap()
    wqkv = nc.dram_tensor("wqkv", [D, 3072], F32, kind="ExternalInput").ap()
    wo = nc.dram_tensor("wo", [D, D], F32, kind="ExternalInput").ap()
    rw = nc.dram_tensor("rw", [D, E], F32, kind="ExternalInput").ap()
    gw = nc.dram_tensor("gw", [E, D, F], F32, kind="ExternalInput").ap()
    uw = nc.dram_tensor("uw", [E, D, F], F32, kind="ExternalInput").ap()
    dw = nc.dram_tensor("dw", [E, F, D], F32, kind="ExternalInput").ap()
    ropec_in = nc.dram_tensor("ropec", [128, T], BF16, kind="ExternalInput").ap()
    ropes_in = nc.dram_tensor("ropes", [128, T], BF16, kind="ExternalInput").ap()
    cmask_in = nc.dram_tensor("cmask", [128, 4096], BF16, kind="ExternalInput").ap()
    ident_in = nc.dram_tensor("ident", [128, 128], BF16, kind="ExternalInput").ap()
    nonce_in = nc.dram_tensor("nonce", [1, 16], U32, kind="ExternalInput").ap()

    out = nc.dram_tensor("out", [T, D], F32, kind="ExternalOutput").ap()

    qkv_kind = "ExternalOutput" if stop_after == "qkv" else "Internal"
    q_dram = nc.dram_tensor("q_dram", [D, T], BF16, kind=qkv_kind).ap()
    k_dram = nc.dram_tensor("k_dram", [512, T], BF16, kind=qkv_kind).ap()
    v_dram = nc.dram_tensor("v_dram", [T, 512], BF16, kind=qkv_kind).ap()
    x1_kind = "ExternalOutput" if dbg else "Internal"
    x1_dram = nc.dram_tensor("x1_dram", [T, D], F32, kind=x1_kind).ap()
    sh_k = nc.dram_tensor("sh_k", [1024, T], BF16, kind="Internal",
                          addr_space="Shared").ap()
    sh_v = nc.dram_tensor("sh_v", [2 * T, 512], BF16, kind="Internal",
                          addr_space="Shared").ap()
    sh_flag = nc.dram_tensor("sh_flag", [2, 16], U32, kind="Internal",
                             addr_space="Shared").ap()
    if dbg:
        dbg_x1 = nc.dram_tensor("dbg_x1", [T, D], F32, kind="ExternalOutput").ap()
    if stop_after == "ffn1":
        dbg_sc = nc.dram_tensor("dbg_sc", [128, 512], F32, kind="ExternalOutput").ap()
        dbg_xn = nc.dram_tensor("dbg_xn", [128, 2048], BF16, kind="ExternalOutput").ap()
        dbg_cn = nc.dram_tensor("dbg_cn", [128, 1024], F32, kind="ExternalOutput").ap()

    SBASE = 16928

    def A(name, shape, dtype, offset):
        return nc.alloc_sbuf_tensor_at(name, shape, dtype,
                                       offset=SBASE + offset)
    x_sb = A("x_sb", [128, NTC * 2048], F32, offset=0)
    wo_sb = A("wo_sb", [128, 16 * 2048], BF16, offset=0)
    cmask_sb = A("cmask_sb", [128, 4096], BF16, offset=65536)
    qbuf = [A(f"qbuf{i}", [128, 2048], BF16, offset=73728 + 4096 * i)
            for i in range(2)]
    kbuf = [A(f"kbuf{i}", [128, 512], BF16, offset=81920 + 1024 * i)
            for i in range(2)]
    vbuf = [A(f"vbuf{i}", [128, 512], BF16, offset=83968 + 1024 * i)
            for i in range(2)]
    psb = [A(f"psb{i}", [128, 512], BF16, offset=86016 + 1024 * i)
           for i in range(2)]
    ptsb = [A(f"ptsb{i}", [128, 512], BF16, offset=88064 + 1024 * i)
            for i in range(2)]
    otok = A("otok", [128, 2048], BF16, offset=90112)
    ofm = A("ofm", [128, 2048], BF16, offset=94208)
    x1t = [A(f"x1t{i}", [128, 2048], F32, offset=98304 + 8192 * i)
           for i in range(2)]

    xn_sb = A("xn_sb", [128, 16 * TH], BF16, offset=131072)

    CA = 163840
    gu = [A(f"gu{i}", [128, 4096], BF16, offset=CA + 8192 * i)
          for i in range(2)]
    dwq = [A(f"dwq{i}", [128, 3072], BF16, offset=CA + 16384 + 6144 * i)
           for i in range(2)]
    h_all = A("h_all", [128, 6 * TH], BF16, offset=CA + 28672)
    xn_tok_f = A("xn_tok_f", [128, 2048], BF16, offset=CA + 28672)
    xn_tok_q = A("xn_tok_q", [128, 2048], BF16, offset=CA + 0)
    silu_sb = A("silu_sb", [128, 512], BF16, offset=211392)
    tmp_ev = A("tmp_ev", [128, 512], F32, offset=CA + 40960)
    wqb = [A(f"wqb{i}", [128, 2048], BF16, offset=CA + 4096 * i)
           for i in range(2)]
    wv_sb = A("wv_sb", [128, 16 * 512], BF16, offset=CA + 8192)
    ropec_sb = A("ropec_sb", [128, T], BF16, offset=CA + 24576)
    ropes_sb = A("ropes_sb", [128, T], BF16, offset=CA + 28672)
    stq = [A(f"stq{i}", [128, 512], BF16, offset=CA + 32768 + 1024 * i)
           for i in range(2)]
    stv = [A(f"stv{i}", [128, 512], BF16, offset=CA + 34816 + 1024 * i)
           for i in range(2)]
    rta = A("rta", [128, 512], F32, offset=CA + 36864)
    rtb = A("rtb", [128, 512], F32, offset=CA + 38912)
    rth = A("rth", [128, 512], F32, offset=CA + 40960)

    sm = [206848]

    def SA(name, shape, dtype):
        t = A(name, shape, dtype, offset=sm[0])
        nbytes = shape[1] * (4 if dtype in (F32, U32) else 2)
        sm[0] += (nbytes + 31) // 32 * 32
        return t

    ident_sb = SA("ident_sb", [128, 128], BF16)
    rw_sb = SA("rw_sb", [128, 256], BF16)
    scores_sb = SA("scores_sb", [128, 256], F32)
    c_tok = SA("c_tok", [128, 256], F32)
    ln_sum = SA("ln_sum", [128, 1], F32)
    ln_negm = SA("ln_negm", [128, 1], F32)
    ln_var = SA("ln_var", [128, 1], F32)
    ln_r = SA("ln_r", [128, 1], F32)
    ln_mr = SA("ln_mr", [128, 1], F32)
    r2_m1 = SA("r2_m1", [128, 1], F32)
    r2_m2 = SA("r2_m2", [128, 1], F32)
    r2_t = SA("r2_t", [128, 16], F32)
    r2_s2 = SA("r2_s2", [128, 16], F32)
    r2_c1 = SA("r2_c1", [128, 16], F32)
    r2_e = SA("r2_e", [128, 1], F32)
    r2_w2 = SA("r2_w2", [128, 1], F32)
    r2_w1 = SA("r2_w1", [128, 1], F32)
    fl_m = [SA(f"fl_m{i}", [128, 1], F32) for i in range(4)]
    fl_l = [SA(f"fl_l{i}", [128, 1], F32) for i in range(4)]
    fl_mn = [SA(f"fl_mn{i}", [128, 1], F32) for i in range(2)]
    fl_dm = [SA(f"fl_dm{i}", [128, 1], F32) for i in range(2)]
    fl_al = [SA(f"fl_al{i}", [128, 1], F32) for i in range(2)]
    fl_bias = [SA(f"fl_bias{i}", [128, 1], F32) for i in range(2)]
    fl_lp = [SA(f"fl_lp{i}", [128, 1], F32) for i in range(2)]
    fl_rl = [SA(f"fl_rl{i}", [128, 1], F32) for i in range(4)]
    vdmy = SA("vdmy", [128, 1], F32)
    admy = SA("admy", [128, 1], F32)
    mx8 = SA("mx8", [128, 8], F32)
    r2_a1 = SA("r2_a1", [128, 16], F32)
    r2_a3 = SA("r2_a3", [128, 16], F32)
    r2_sg1 = SA("r2_sg1", [128, 16], F32)
    r2_sg3 = SA("r2_sg3", [128, 16], F32)
    nonce_sb = SA("nonce_sb", [1, 16], U32)
    flag_sb = SA("flag_sb", [1, 16], U32)
    assert sm[0] <= 211392 and SBASE + 211392 + 1024 <= 229344, sm[0]

    S = Sched()
    if paranoid >= 3:
        S.paranoid = True
    state = {}
    S_KV_PUB = [0]

    psum_stack = ExitStack()

    def psum_alloc(name, shape, dtype=F32):
        return psum_stack.enter_context(nc.psum_tensor(name, shape, dtype))


    def phase0():
        for tc in range(NTC):
            def f(Ee, tc=tc):
                Ee["s"].dma_start(
                    out=x_sb[:, tc * 2048:(tc + 1) * 2048],
                    in_=x0[tc * 128:(tc + 1) * 128, :],
                ).then_inc(Ee["sem"]["dS"], 16)
            S.emit("s", f)
            S.inc("dS", 16)
        for name, dst, src in (
            ("ident", ident_sb, ident_in),
            ("nonce", nonce_sb, nonce_in),
        ):
            def f(Ee, dst=dst, src=src):
                Ee["s"].dma_start(out=dst[:], in_=src).then_inc(Ee["sem"]["dS"], 16)
            S.emit("s", f)
            S.inc("dS", 16)
        def f(Ee):
            for dc in range(16):
                Ee["g"].dma_start(
                    out=rw_sb[:, dc * 16:(dc + 1) * 16],
                    in_=rw[dc * 128:(dc + 1) * 128, :],
                ).then_inc(Ee["sem"]["dG"], 16)
        S.emit("g", f)
        S.inc("dG", 256)
        def finit_v(Ee):
            Ee["v"].memset(vdmy[:], 0.0)
        S.emit("v", finit_v)

        def finit_a(Ee):
            Ee["a"].memzero(admy[:])
        S.emit("a", finit_a)
        state["x_loaded_dS"] = S.val("dS")
        state["consts_dG"] = S.val("dG")

    def load_gu(e, fc, slot, wait_cP):
        _wait(S, "g", "cP", wait_cP)
        def f(Ee, e=e, fc=fc, slot=slot):
            for w_, base in ((gw, 0), (uw, 2048)):
                wap = w_[e, :, fc * 128:(fc + 1) * 128].rearrange(
                    "(dc p) f -> p dc f", p=128)
                for hh in range(2):
                    Ee["g"].dma_start(
                        out=gu[slot][:, base + hh * 1024:base + (hh + 1) * 1024],
                        in_=wap[:, hh * 8:(hh + 1) * 8, :],
                    ).then_inc(Ee["sem"]["dG"], 16)
        S.emit("g", f)
        return S.inc("dG", 64)

    def load_dwq(e, dq, slot, wait_cP):
        _wait(S, "g", "cP", wait_cP)
        def f(Ee, e=e, dq=dq, slot=slot):
            Ee["g"].dma_start(
                out=dwq[slot][:],
                in_=dw[e][:, dq * 512:(dq + 1) * 512]
                .rearrange("(fc p) d -> p fc d", p=128),
            ).then_inc(Ee["sem"]["dG"], 16)
        S.emit("g", f)
        return S.inc("dG", 16)

    def emit_ln(h, ps_tr, xn_tok):
        _wait(S, "v", "dS", state["x_loaded_dS"])
        entry_cp = S.val("cP")
        _wait(S, "a", "cP", entry_cp)
        _wait(S, "v", "cP", entry_cp)
        for tl in range(8):
            tcg = h * 8 + tl
            xt = x_sb[:, tcg * 2048:(tcg + 1) * 2048]

            def fv1(Ee, xt=xt):
                E = Ee
                E["v"].reduce_sum(ln_sum[:], xt, AX)
                E["v"].tensor_copy(vdmy[:], vdmy[:])
                E["v"].tensor_scalar(ln_negm[:], ln_sum[:], -1.0 / D, None,
                                     ALU.mult)
                E["v"].tensor_copy(vdmy[:], vdmy[:])
                E["v"].tensor_copy(vdmy[:], vdmy[:]).then_inc(E["sem"]["cV"], 1)
            S.emit("v", fv1)
            cv1 = S.inc("cV")
            _wait(S, "a", "cV", cv1)

            def fa1(Ee, xt=xt, xn_tok=xn_tok):
                Ee["a"].activation(xn_tok[:], xt, AF.Square,
                                   bias=ln_negm[:], scale=1.0)
                Ee["a"].activation(admy[:], admy[:], AF.Identity
                                   ).then_inc(Ee["sem"]["cA"], 1)
            S.emit("a", fa1)
            ca1 = S.inc("cA")
            _wait(S, "v", "cA", ca1)

            def fv2(Ee, xn_tok=xn_tok):
                E = Ee
                E["v"].reduce_sum(ln_var[:], xn_tok[:], AX)
                E["v"].tensor_copy(vdmy[:], vdmy[:])
                E["v"].tensor_scalar(ln_var[:], ln_var[:], 1.0 / D, EPS,
                                     ALU.mult, ALU.add)
                E["v"].tensor_copy(vdmy[:], vdmy[:])
                E["v"].tensor_copy(vdmy[:], vdmy[:]).then_inc(E["sem"]["cV"], 1)
            S.emit("v", fv2)
            cv2 = S.inc("cV")
            _wait(S, "a", "cV", cv2)

            def fa2(Ee):
                E = Ee
                E["a"].activation(ln_r[:], ln_var[:], AF.Ln)
                E["a"].activation(ln_r[:], ln_r[:], AF.Exp, scale=-0.5)
                E["a"].activation(admy[:], admy[:], AF.Identity
                                  ).then_inc(E["sem"]["cA"], 1)
            S.emit("a", fa2)
            ca2 = S.inc("cA")
            _wait(S, "v", "cA", ca2)

            def fv3(Ee):
                Ee["v"].tensor_mul(ln_mr[:], ln_negm[:], ln_r[:])
                Ee["v"].tensor_copy(vdmy[:], vdmy[:])
                Ee["v"].tensor_copy(vdmy[:], vdmy[:]).then_inc(Ee["sem"]["cV"], 1)
            S.emit("v", fv3)
            cv3 = S.inc("cV")
            _wait(S, "a", "cV", cv3)

            def fa3(Ee, xt=xt, xn_tok=xn_tok):
                Ee["a"].activation(xn_tok[:], xt, AF.Identity,
                                   bias=ln_mr[:], scale=ln_r[:])
                Ee["a"].activation(admy[:], admy[:], AF.Identity
                                   ).then_inc(Ee["sem"]["cA"], 1)
            S.emit("a", fa3)
            ca3 = S.inc("cA")

            for g4 in range(4):
                _wait(S, "t", "cA", ca3)
                _wait(S, "t", "dS", state["x_loaded_dS"])
                _wait(S, "t", "cV", state.get("tr_evac_cV", 0))

                def ft(Ee, g4=g4, ps_tr=ps_tr, xn_tok=xn_tok):
                    E = Ee
                    for j in range(4):
                        dc = g4 * 4 + j
                        ins = E["t"].transpose(
                            ps_tr[:, j * 128:(j + 1) * 128],
                            xn_tok[:, dc * 128:(dc + 1) * 128],
                            ident_sb[:],
                        )
                        if j == 3:
                            ins.then_inc(E["sem"]["cP"], 1)
                S.emit("t", ft)
                cp = S.inc("cP")
                _wait(S, "v", "cP", cp)

                def fv4(Ee, g4=g4, tl=tl, ps_tr=ps_tr):
                    E = Ee
                    dst = xn_sb.ap().rearrange(
                        "p (dc t) -> p dc t", t=TH
                    )[:, 4 * g4:4 * g4 + 4, tl * 128:(tl + 1) * 128]
                    src = ps_tr.ap().rearrange("p (j t) -> p j t", t=128)
                    E["v"].tensor_copy(dst, src)
                    E["v"].tensor_copy(vdmy[:], vdmy[:]).then_inc(E["sem"]["cV"], 1)
                S.emit("v", fv4)
                state["tr_evac_cV"] = S.inc("cV")
        state["xn_cV"] = S.val("cV")

    def emit_router(h, ps_sc):
        _wait(S, "t", "dG", state["consts_dG"])
        _wait(S, "t", "cV", state["xn_cV"])
        for tl in range(8):
            tcg = h * 8 + tl

            def ft(Ee, tl=tl):
                E = Ee
                for dc in range(16):
                    ins = E["t"].matmul(
                        ps_sc[:, tl * 16:(tl + 1) * 16],
                        xn_sb[:, dc * TH + tl * 128:dc * TH + (tl + 1) * 128],
                        rw_sb[:, dc * 16:(dc + 1) * 16],
                        start=(dc == 0), stop=(dc == 15),
                    )
                    if dc == 15:
                        ins.then_inc(E["sem"]["cP"], 1)
            S.emit("t", ft)
            S.inc("cP")
        cp = S.val("cP")
        _wait(S, "v", "cP", cp)

        def fv(Ee, h=h):
            Ee["v"].tensor_copy(scores_sb[:, h * 128:(h + 1) * 128],
                                ps_sc[:, 0:128])
            Ee["v"].tensor_copy(vdmy[:], vdmy[:]).then_inc(Ee["sem"]["cV"], 1)
        S.emit("v", fv)
        S.inc("cV")
        for tl in range(8):
            tcg = h * 8 + tl
            s_sl = scores_sb[:, tcg * 16:(tcg + 1) * 16]

            def fv1(Ee, s_sl=s_sl):
                E = Ee
                v = E["v"]
                v.tensor_copy(vdmy[:], vdmy[:])
                v.max(mx8[:], s_sl)
                v.tensor_copy(vdmy[:], vdmy[:])
                v.tensor_scalar(r2_a1[:], s_sl, mx8[:, 1:2], None, ALU.subtract)
                v.tensor_scalar(r2_a3[:], s_sl, mx8[:, 2:3], None, ALU.subtract)
                v.tensor_sub(r2_e[:], mx8[:, 0:1], mx8[:, 1:2])
                v.tensor_copy(vdmy[:], vdmy[:])
                v.tensor_copy(vdmy[:], vdmy[:]).then_inc(E["sem"]["cV"], 1)
            S.emit("v", fv1)
            cv = S.inc("cV")
            _wait(S, "a", "cV", cv)

            def fa(Ee):
                E = Ee
                E["a"].activation(r2_sg1[:], r2_a1[:], AF.Sign)
                E["a"].activation(r2_sg3[:], r2_a3[:], AF.Sign)
                E["a"].activation(r2_e[:], r2_e[:], AF.Exp)
                E["a"].activation(admy[:], admy[:], AF.Identity
                                  ).then_inc(E["sem"]["cA"], 1)
            S.emit("a", fa)
            ca = S.inc("cA")
            _wait(S, "v", "cA", ca)

            def fv2(Ee, s_sl=s_sl, tcg=tcg):
                E = Ee
                v = E["v"]
                v.tensor_scalar(r2_e[:], r2_e[:], 1.0, None, ALU.add)
                v.tensor_copy(vdmy[:], vdmy[:])
                v.reciprocal(r2_w2[:], r2_e[:])
                v.tensor_copy(vdmy[:], vdmy[:])
                v.tensor_scalar(r2_w1[:], r2_w2[:], -1.0, 0.5, ALU.mult,
                                ALU.add)
                v.tensor_scalar(r2_w2[:], r2_w2[:], 0.5, None, ALU.mult)
                v.tensor_copy(vdmy[:], vdmy[:])
                v.tensor_scalar(r2_c1[:], r2_sg1[:], 0.0, r2_w1[:],
                                ALU.max, ALU.mult)
                v.tensor_scalar(r2_t[:], r2_sg3[:], 0.0, r2_w2[:],
                                ALU.max, ALU.mult)
                v.tensor_copy(vdmy[:], vdmy[:])
                v.tensor_add(c_tok[:, tcg * 16:(tcg + 1) * 16],
                             r2_c1[:], r2_t[:])
                v.tensor_copy(vdmy[:], vdmy[:])
                v.tensor_copy(vdmy[:], vdmy[:]).then_inc(E["sem"]["cV"], 1)
            S.emit("v", fv2)
            S.inc("cV")
        state["ctok_cV"] = S.val("cV")

    def emit_experts(h, ps_gs, ps_u, daccs, tmp_evs):
        gu_last_use = [0, 0]
        gu_loaded = [0, 0]
        silu_ca = [0, 0]
        h_cv = 0
        dacc_free = [0, 0]
        tmp_free = [0, 0]
        dw_last_use = [state.get("arena_free_cP", 0)] * 2
        dw_loaded = [0, 0]
        gseq = 0
        dwseq = 0
        it = 0
        entry = state.get("arena_free_cP", 0)
        gu_loaded[0] = load_gu(0, 0, 0, entry)
        gu_loaded[1] = load_gu(0, 1, 1, entry)
        dw_loaded[0] = load_dwq(0, 0, 0, entry)
        dw_loaded[1] = load_dwq(0, 1, 1, entry)
        for e in range(n_exp):
            for fc in range(6):
                slot = gseq % 2
                _wait(S, "t", "dG", gu_loaded[slot])
                _wait(S, "t", "cV", state["xn_cV"])
                for tb in range(2):
                    gslot = it % 2
                    it += 1
                    pg = ps_gs[gslot]
                    _wait(S, "t", "cA", silu_ca[gslot])

                    def ftg(Ee, slot=slot, tb=tb, pg=pg):
                        E = Ee
                        for dc in range(16):
                            ins = E["t"].matmul(
                                pg[:],
                                gu[slot][:, dc * 128:(dc + 1) * 128],
                                xn_sb[:, dc * TH + tb * 512:dc * TH + tb * 512 + 512],
                                start=(dc == 0), stop=(dc == 15),
                            )
                            if dc == 15:
                                ins.then_inc(E["sem"]["cP"], 1)
                    S.emit("t", ftg)
                    cp_g = S.inc("cP")
                    _wait(S, "t", "cV", h_cv)

                    def ftu(Ee, slot=slot, tb=tb, ps_u=ps_u):
                        E = Ee
                        for dc in range(16):
                            ins = E["t"].matmul(
                                ps_u[:],
                                gu[slot][:, 2048 + dc * 128:2048 + (dc + 1) * 128],
                                xn_sb[:, dc * TH + tb * 512:dc * TH + tb * 512 + 512],
                                start=(dc == 0), stop=(dc == 15),
                            )
                            if dc == 15:
                                ins.then_inc(E["sem"]["cP"], 1)
                    S.emit("t", ftu)
                    cp_u = S.inc("cP")
                    _wait(S, "a", "cP", cp_g)
                    _wait(S, "a", "cV", h_cv)

                    def fa(Ee, pg=pg):
                        Ee["a"].activation(silu_sb[:], pg[:], AF.Silu)
                        Ee["a"].activation(admy[:], admy[:], AF.Identity
                                           ).then_inc(Ee["sem"]["cA"], 1)
                    S.emit("a", fa)
                    silu_ca[gslot] = S.inc("cA")
                    _wait(S, "v", "cA", silu_ca[gslot])
                    _wait(S, "v", "cP", cp_u)

                    def fv(Ee, fc=fc, tb=tb, ps_u=ps_u):
                        Ee["v"].tensor_mul(
                            h_all[:, fc * TH + tb * 512:fc * TH + tb * 512 + 512],
                            ps_u[:], silu_sb[:])
                        Ee["v"].tensor_copy(vdmy[:], vdmy[:]
                                            ).then_inc(Ee["sem"]["cV"], 1)
                    S.emit("v", fv)
                    h_cv = S.inc("cV")
                gu_last_use[slot] = S.val("cP")
                ne, nfc = (e, fc + 2) if fc < 4 else (e + 1, fc - 4)
                if ne < n_exp:
                    gu_loaded[slot] = load_gu(ne, nfc, slot, gu_last_use[slot])
                gseq += 1
            _wait(S, "t", "cV", h_cv)
            for dq in range(4):
                slot = dwseq % 2
                dwseq += 1
                _wait(S, "t", "dG", dw_loaded[slot])
                for tl in range(8):
                    tcg = h * 8 + tl
                    dslot = tl % 2
                    dacc = daccs[dslot]
                    _wait(S, "t", "cA" if EVAC_ACT else "cV",
                          dacc_free[dslot])

                    def ft(Ee, tl=tl, slot=slot, dacc=dacc):
                        E = Ee
                        for fc in range(6):
                            ins = E["t"].matmul(
                                dacc[:],
                                h_all[:, fc * TH + tl * 128:fc * TH + (tl + 1) * 128],
                                dwq[slot][:, fc * 512:(fc + 1) * 512],
                                start=(fc == 0), stop=(fc == 5),
                            )
                            if fc == 5:
                                ins.then_inc(E["sem"]["cP"], 1)
                    S.emit("t", ft)
                    cp_d = S.inc("cP")
                    if EVAC_ACT:
                        _wait(S, "a", "cP", cp_d)
                        _wait(S, "a", "cV", state["ctok_cV"])
                        _wait(S, "a", "cV", tmp_free[dslot])

                        def fa(Ee, e=e, tcg=tcg, dacc=dacc,
                               tmp=tmp_evs[dslot]):
                            Ee["a"].activation(
                                tmp[:], dacc[:], AF.Identity,
                                scale=c_tok[:, tcg * 16 + e:tcg * 16 + e + 1])
                            Ee["a"].activation(admy[:], admy[:], AF.Identity
                                               ).then_inc(Ee["sem"]["cA"], 1)
                        S.emit("a", fa)
                        dacc_free[dslot] = S.inc("cA")
                        _wait(S, "v", "cA", dacc_free[dslot])

                        def fv(Ee, tcg=tcg, dq=dq, tmp=tmp_evs[dslot]):
                            E = Ee
                            xs = x_sb[:, tcg * 2048 + dq * 512:
                                      tcg * 2048 + (dq + 1) * 512]
                            E["v"].tensor_add(xs, tmp[:], xs)
                            E["v"].tensor_copy(vdmy[:], vdmy[:]).then_inc(
                                E["sem"]["cV"], 1)
                        S.emit("v", fv)
                        tmp_free[dslot] = S.inc("cV")
                    else:
                        _wait(S, "v", "cP", cp_d)
                        _wait(S, "v", "cV", state["ctok_cV"])

                        def fv(Ee, e=e, tcg=tcg, dq=dq, dacc=dacc):
                            E = Ee
                            E["v"].tensor_scalar(
                                tmp_ev[:], dacc[:],
                                c_tok[:, tcg * 16 + e:tcg * 16 + e + 1], None,
                                ALU.mult)
                            E["v"].tensor_copy(vdmy[:], vdmy[:])
                            xs = x_sb[:, tcg * 2048 + dq * 512:
                                      tcg * 2048 + (dq + 1) * 512]
                            E["v"].tensor_add(xs, tmp_ev[:], xs)
                            E["v"].tensor_copy(vdmy[:], vdmy[:]).then_inc(
                                E["sem"]["cV"], 1)
                        S.emit("v", fv)
                        dacc_free[dslot] = S.inc("cV")
                        tmp_free[dslot] = dacc_free[dslot]
                dw_last_use[slot] = S.val("cP")
                nde, ndq = (e, dq + 2) if dq < 2 else (e + 1, dq - 2)
                if nde < n_exp:
                    dw_loaded[slot] = load_dwq(nde, ndq, slot,
                                               dw_last_use[slot])
        state["ffn_done_cV"] = S.val("cV")
        state["ffn_done_cP"] = S.val("cP")
        state["ffn_done_cA"] = S.val("cA")

    def emit_qkv(h, ps_qk, ps_v, par_sv):
        entry_cp = S.val("cP")
        wq_loaded = [0, 0]
        wq_last_use = [entry_cp, entry_cp]
        rope_cv = 0
        stq_stored = [0, 0]
        stv_stored = [0, 0]
        for mi in range(20):
            slot = mi % 2
            _wait(S, "g", "cP", wq_last_use[slot])
            def fg(Ee, mi=mi, slot=slot):
                wap = wqkv[:, mi * 128:(mi + 1) * 128].rearrange(
                    "(dc p) f -> p dc f", p=128)
                for hh in range(2):
                    Ee["g"].dma_start(
                        out=wqb[slot][:, hh * 1024:(hh + 1) * 1024],
                        in_=wap[:, hh * 8:(hh + 1) * 8, :],
                    ).then_inc(Ee["sem"]["dG"], 16)
            S.emit("g", fg)
            wq_loaded[slot] = S.inc("dG", 32)
            for tb in range(2):
                pp = (mi * 2 + tb) % 2
                _wait(S, "t", "dG", wq_loaded[slot])
                _wait(S, "t", "cV", rope_cv if (mi * 2 + tb) >= 2 else state.get("qkv_start_cV", 0))
                def ft(Ee, slot=slot, tb=tb, pp=pp, ps_qk=ps_qk):
                    E = Ee
                    for dc in range(16):
                        ins = E["t"].matmul(
                            ps_qk[pp][:],
                            wqb[slot][:, dc * 128:(dc + 1) * 128],
                            xn_sb[:, dc * TH + tb * 512:dc * TH + tb * 512 + 512],
                            start=(dc == 0), stop=(dc == 15),
                        )
                        if dc == 15:
                            ins.then_inc(E["sem"]["cP"], 1)
                S.emit("t", ft)
                wq_last_use[slot] = S.inc("cP")
                cp = wq_last_use[slot]
                _wait(S, "v", "cP", cp)
                _wait(S, "v", "dS", state["rope_dS"])
                _wait(S, "v", "dS", stq_stored[pp])
                toff = h * TH + tb * 512

                def fv(Ee, pp=pp, toff=toff, ps_qk=ps_qk):
                    E = Ee
                    v = E["v"]
                    v.tensor_mul(rtb[:], ps_qk[pp][:],
                                 ropes_sb[:, toff:toff + 512])
                    v.tensor_mul(rta[:], ps_qk[pp][:],
                                 ropec_sb[:, toff:toff + 512])
                    v.tensor_copy(rth[0:64, :], rtb[64:128, :])
                    v.tensor_copy(vdmy[:], vdmy[:])
                    v.tensor_sub(stq[pp][0:64, :], rta[0:64, :], rth[0:64, :])
                    v.tensor_copy(rth[0:64, :], rta[64:128, :])
                    v.tensor_copy(vdmy[:], vdmy[:])
                    v.tensor_add(stq[pp][64:128, :], rtb[0:64, :],
                                 rth[0:64, :])
                    v.tensor_copy(vdmy[:], vdmy[:]).then_inc(E["sem"]["cV"], 1)
                S.emit("v", fv)
                rope_cv = S.inc("cV")
                _wait(S, "s", "cV", rope_cv)

                def fs(Ee, mi=mi, pp=pp, toff=toff, par_sv_key="par"):
                    E = Ee
                    if mi < 16:
                        E["s"].dma_start(
                            out=q_dram[mi * 128:(mi + 1) * 128,
                                       toff:toff + 512],
                            in_=stq[pp][:],
                        ).then_inc(E["sem"]["dS"], 16)
                    else:
                        hg = mi - 16
                        E["s"].dma_start(
                            out=k_dram[hg * 128:(hg + 1) * 128,
                                       toff:toff + 512],
                            in_=stq[pp][:],
                        ).then_inc(E["sem"]["dS"], 16)
                S.emit("s", fs)
                stq_stored[pp] = S.inc("dS", 16)
        _wait(S, "g", "cP", max(wq_last_use))
        def fgv(Ee):
            wap = wqkv[:, 2560:3072].rearrange("(dc p) f -> p dc f", p=128)
            for hh in range(2):
                Ee["g"].dma_start(
                    out=wv_sb[:, hh * 4096:(hh + 1) * 4096],
                    in_=wap[:, hh * 8:(hh + 1) * 8, :],
                ).then_inc(Ee["sem"]["dG"], 16)
        S.emit("g", fgv)
        wv_loaded = S.inc("dG", 32)
        v_evac_cv = 0
        for tl in range(8):
            tcg = h * 8 + tl
            pp = tl % 2
            _wait(S, "t", "dG", wv_loaded)
            _wait(S, "t", "cV", v_evac_cv if tl >= 2 else state["qkv_start_cV"])
            def ft(Ee, tl=tl, pp=pp, ps_v=ps_v):
                E = Ee
                for dc in range(16):
                    ins = E["t"].matmul(
                        ps_v[pp][:],
                        xn_sb[:, dc * TH + tl * 128:dc * TH + (tl + 1) * 128],
                        wv_sb[:, dc * 512:(dc + 1) * 512],
                        start=(dc == 0), stop=(dc == 15),
                    )
                    if dc == 15:
                        ins.then_inc(E["sem"]["cP"], 1)
            S.emit("t", ft)
            cp = S.inc("cP")
            _wait(S, "v", "cP", cp)
            _wait(S, "v", "dS", stv_stored[pp])

            def fv(Ee, pp=pp, ps_v=ps_v):
                Ee["v"].tensor_copy(stv[pp][:], ps_v[pp][:])
                Ee["v"].tensor_copy(vdmy[:], vdmy[:]).then_inc(
                    Ee["sem"]["cV"], 1)
            S.emit("v", fv)
            v_evac_cv = S.inc("cV")
            _wait(S, "s", "cV", v_evac_cv)

            def fs(Ee, tcg=tcg, pp=pp):
                Ee["s"].dma_start(
                    out=v_dram[tcg * 128:(tcg + 1) * 128, :],
                    in_=stv[pp][:],
                ).then_inc(Ee["sem"]["dS"], 16)
            S.emit("s", fs)
            stv_stored[pp] = S.inc("dS", 16)
        state["kv_stored_dS"] = S.val("dS")

    def emit_x1_spill(h, to_dbg=False):
        for tl in range(8):
            tcg = h * 8 + tl
            def fs(Ee, tcg=tcg, to_dbg=to_dbg):
                E = Ee
                E["s"].dma_start(
                    out=x1_dram[tcg * 128:(tcg + 1) * 128, :],
                    in_=x_sb[:, tcg * 2048:(tcg + 1) * 2048],
                ).then_inc(E["sem"]["dS"], 16)
                if to_dbg and dbg:
                    E["s"].dma_start(
                        out=dbg_x1[tcg * 128:(tcg + 1) * 128, :],
                        in_=x_sb[:, tcg * 2048:(tcg + 1) * 2048],
                    ).then_inc(E["sem"]["dS"], 16)
            S.emit("s", fs)
            S.inc("dS", 32 if (to_dbg and dbg) else 16)
        state[f"x1_spill_{h}_dS"] = S.val("dS")

    def emit_flag_sync():
        _wait(S, "s", "dS", state["kv_stored_dS"])

        def fs(Ee):
            E = Ee
            s = E["s"]
            s.dma_start(
                out=sh_k.rearrange("(b r) c -> b (r c)", b=32)[
                    ts(state["par"], 16), :],
                in_=k_dram.rearrange("(a r) c -> a (r c)", a=16),
            ).then_inc(E["sem"]["dS"], 16)
            s.dma_start(
                out=sh_v.rearrange("(b r) c -> b (r c)", b=32)[
                    ts(state["par"], 16), :],
                in_=v_dram.rearrange("(a r) c -> a (r c)", a=16),
            ).then_inc(E["sem"]["dS"], 16)
            s.wait_ge(E["sem"]["dS"], S_KV_PUB[0])
            s.dma_start(out=sh_flag[ts(state["par"], 1), :],
                        in_=nonce_sb[:]).then_inc(E["sem"]["dS"], 16)
            rn = s.alloc_register("rn")
            rf = s.alloc_register("rf")
            rd = s.alloc_register("rd")
            rc = s.alloc_register("rc")
            s.load(rn, nonce_sb[0:1, 0:1])
            s.reg_mov(rd, 1)
            s.reg_mov(rc, 0)
            with s.While(rd):
                s.reg_add(rc, rc, 16)
                s.dma_start(out=flag_sb[:],
                            in_=sh_flag[ts(state["parx"], 1), :]
                            ).then_inc(E["sem"]["dP"], 16)
                s.wait_ge(E["sem"]["dP"], rc)
                s.load(rf, flag_sb[0:1, 0:1])
                s.reg_alu(rd, rf, rn, ALU.bitwise_xor)
            for _ in range(16):
                s.reg_add(rc, rc, 16)
                s.dma_start(out=flag_sb[:],
                            in_=sh_flag[ts(state["parx"], 1), :]
                            ).then_inc(E["sem"]["dP"], 16)
                s.wait_ge(E["sem"]["dP"], rc)
            s.sem_inc(E["sem"]["dS"], 16)
        S_KV_PUB[0] = S.val("dS") + 32
        S.emit("s", fs)
        S.inc("dS", 64)
        state["kv_ready_dS"] = S.val("dS")

    def emit_attn_overlay_loads():
        _wait(S, "g", "dS", state["x1_spill_0_dS"])
        _wait(S, "g", "cA", state["qkv_end_cA"])
        _wait(S, "g", "cV", state["qkv_end_cV"])
        def fg(Ee):
            E = Ee
            for q4 in range(4):
                E["g"].dma_start(
                    out=wo_sb[:, q4 * 8192:(q4 + 1) * 8192],
                    in_=wo[q4 * 512:(q4 + 1) * 512, :]
                    .rearrange("(hc p) d -> p hc d", p=128),
                ).then_inc(E["sem"]["dG"], 16)
        S.emit("g", fg)
        state["wo_loaded_dG"] = S.inc("dG", 64)
        _wait(S, "s", "dS", state["x1_spill_1_dS"])
        _wait(S, "s", "cA", state["qkv_end_cA"])
        _wait(S, "s", "cV", state["qkv_end_cV"])

        def fs(Ee):
            Ee["s"].dma_start(out=cmask_sb[:], in_=cmask_in).then_inc(
                Ee["sem"]["dS"], 16)
        S.emit("s", fs)
        state["cmask_dS"] = S.inc("dS", 16)

    def emit_attention(ps_s, ps_pt, ps_os, ps_wo):
        ps_pts = (ps_pt, ps_wo)
        _wait(S, "t", "cV", S.val("cV"))
        _wait(S, "t", "cA", S.val("cA"))
        q_loaded = [0, 0]
        q_last_use = [0, 0]
        kv_loaded = [0, 0]
        kv_last_use = [0, 0]
        x1_loaded = [0, 0]
        x1_free = [0, 0]
        s_free = [0, 0]
        psb_free = [0, 0]
        lp_free = [0, 0]
        pt_free = [0, 0]
        o_free = 0
        ofm_ca = 0
        wo_free = 0
        sseq = 0
        tseq = 0
        kvseq = 0
        for qt in range(n_qt):
            qslot = qt % 2
            _wait(S, "s", "dS", state["kv_ready_dS"])
            _wait(S, "s", "cP", q_last_use[qslot])

            def fsq(Ee, qt=qt, qslot=qslot):
                qap = q_dram[:, qt * 128:(qt + 1) * 128].rearrange(
                    "(hc p) q -> p hc q", p=128)
                for hh in range(2):
                    Ee["s"].dma_start(
                        out=qbuf[qslot][:, hh * 1024:(hh + 1) * 1024],
                        in_=qap[:, hh * 8:(hh + 1) * 8, :],
                    ).then_inc(Ee["sem"]["dS"], 16)
            S.emit("s", fsq)
            q_loaded[qslot] = S.inc("dS", 32)
            xslot = qt % 2
            _wait(S, "s", "dS", state["x1_spill_1_dS"])
            _wait(S, "s", "cV", x1_free[xslot])

            def fsx(Ee, qt=qt, xslot=xslot):
                Ee["s"].dma_start(
                    out=x1t[xslot][:],
                    in_=x1_dram[qt * 128:(qt + 1) * 128, :],
                ).then_inc(Ee["sem"]["dS"], 16)
            S.emit("s", fsx)
            x1_loaded[xslot] = S.inc("dS", 16)

            nkb = qt // 4 + 1
            for hg in range(4):
                tuples = [(pk, kb, hq) for pk in range(2)
                          for kb in range(nkb) for hq in range(4)]
                n = len(tuples)
                info = [None] * n

                def prep_S(j, hg=hg, qslot=qslot, nkb=nkb, tuples=tuples,
                           info=info):
                    nonlocal kvseq, sseq, tseq
                    pk, kb, hq = tuples[j]
                    d = {"pk": pk, "kb": kb, "hq": hq, "ss": sseq % 2,
                         "ts": tseq % 2, "diag": kb == nkb - 1,
                         "isf": pk == 0 and kb == 0}
                    sseq += 1
                    tseq += 1
                    info[j] = d
                    if hq == 0:
                        kvslot = kvseq % 2
                        kvseq += 1
                        _wait(S, "s", "cP", kv_last_use[kvslot])

                        def fskv(Ee, pk=pk, hg=hg, kb=kb, kvslot=kvslot):
                            E = Ee
                            E["s"].dma_start(
                                out=kbuf[kvslot][:],
                                in_=sh_k[pk * 512 + hg * 128:
                                         pk * 512 + (hg + 1) * 128,
                                         kb * 512:(kb + 1) * 512],
                            ).then_inc(E["sem"]["dS"], 16)
                            E["s"].dma_start(
                                out=vbuf[kvslot][:],
                                in_=sh_v[pk * T + kb * 512:
                                         pk * T + (kb + 1) * 512,
                                         hg * 128:(hg + 1) * 128]
                                .rearrange("(i p) d -> p i d", p=128),
                            ).then_inc(E["sem"]["dS"], 16)
                        S.emit("s", fskv)
                        kv_loaded[kvslot] = S.inc("dS", 32)
                        d["kvs"] = kvslot
                    else:
                        d["kvs"] = info[j - 1]["kvs"]
                    _wait(S, "t", "dS", kv_loaded[d["kvs"]])
                    _wait(S, "t", "dS", q_loaded[qslot])
                    _wait(S, "t", "cA", s_free[d["ss"]])

                    def ftS(Ee, hg=hg, hq=hq, ss=d["ss"], kvs=d["kvs"],
                            qslot=qslot):
                        Ee["t"].matmul(
                            ps_s[ss][:],
                            qbuf[qslot][:, (hg * 4 + hq) * 128:
                                        (hg * 4 + hq + 1) * 128],
                            kbuf[kvs][:],
                            start=True, stop=True,
                        ).then_inc(Ee["sem"]["cP"], 1)
                    S.emit("t", ftS)
                    d["cpS"] = S.inc("cP")

                prep_S(0)
                for i in range(n + 2):
                    d = info[i] if i < n else None
                    p = info[i - 2] if i >= 2 else None
                    if paranoid:
                        for _e in (("t", "v", "a") if paranoid >= 2
                                   else (PARA_ENG,)):
                            _wait(S, _e, "cP", S.val("cP"))
                            _wait(S, _e, "cV", S.val("cV"))
                            _wait(S, _e, "cA", S.val("cA"))
                    if d is not None:
                        _wait(S, "v", "cP", d["cpS"])
                        if d["diag"]:
                            _wait(S, "v", "dS", state["cmask_dS"])
                    if p is not None:
                        _wait(S, "v", "cA", p["ca"])
                        if not p["isf"]:
                            _wait(S, "v", "cP",
                                  state.get("pv_cp_%d" % p["hq"], 0))

                    def fvc(Ee, d=d, p=p, qt=qt):
                        E = Ee
                        v = E["v"]
                        lastw = [None, None]

                        def op(emitfn, reads, writes):
                            while any(w is not None and w in reads
                                      for w in lastw):
                                v.tensor_copy(vdmy[:], vdmy[:])
                                lastw.pop(0)
                                lastw.append(None)
                            emitfn()
                            lastw.pop(0)
                            lastw.append(writes)
                        if d is not None and d["diag"] and not no_mask:
                            moff = (d["pk"] * 4 + qt % 4) * 512
                            op(lambda: v.tensor_add(
                                ps_s[d["ss"]][:], ps_s[d["ss"]][:],
                                cmask_sb[:, moff:moff + 512]),
                               (), "ps_s")
                        if p is not None:
                            op(lambda: v.reduce_sum(
                                fl_lp[p["ss"]][:], psb[p["ss"]][:], AX),
                               (), "fl_lp")
                        if d is not None:
                            mdst = (fl_m[d["hq"]] if d["isf"]
                                    else fl_mn[d["ss"]])
                            op(lambda: v.reduce_max(
                                mdst[:], ps_s[d["ss"]][:], AX),
                               ("ps_s",), "mx")
                        if p is not None:
                            if p["isf"]:
                                op(lambda: v.tensor_copy(
                                    fl_l[p["hq"]][:], fl_lp[p["ss"]][:]),
                                   ("fl_lp",), "fl_l")
                            else:
                                op(lambda: v.tensor_scalar(
                                    fl_l[p["hq"]][:], fl_l[p["hq"]][:],
                                    fl_al[p["ss"]][:], None, ALU.mult),
                                   ("fl_l",), "fl_l")
                        if d is not None and not d["isf"]:
                            op(lambda: v.tensor_max(
                                fl_mn[d["ss"]][:], fl_mn[d["ss"]][:],
                                fl_m[d["hq"]][:]),
                               ("mx",), "mx")
                        if p is not None and not p["isf"]:
                            if no_resc:
                                pass
                            else:
                                op(lambda: v.tensor_scalar(
                                    ps_os[p["hq"]][:, 0:128],
                                    ps_os[p["hq"]][:, 0:128],
                                    fl_al[p["ss"]][:], None, ALU.mult),
                                   (), "ps_o")
                        if d is not None and not d["isf"]:
                            op(lambda: v.tensor_sub(
                                fl_dm[d["ss"]][:], fl_m[d["hq"]][:],
                                fl_mn[d["ss"]][:]),
                               ("mx",), "fl_dm")
                        if p is not None and not p["isf"]:
                            op(lambda: v.tensor_add(
                                fl_l[p["hq"]][:], fl_l[p["hq"]][:],
                                fl_lp[p["ss"]][:]),
                               ("fl_l", "fl_lp"), "fl_l")
                        if d is not None:
                            if not d["isf"]:
                                op(lambda: v.tensor_copy(
                                    fl_m[d["hq"]][:], fl_mn[d["ss"]][:]),
                                   ("mx",), "fl_m")
                                op(lambda: v.tensor_scalar(
                                    fl_bias[d["ss"]][:], fl_mn[d["ss"]][:],
                                    -SC, None, ALU.mult),
                                   ("mx",), "fl_bias")
                            else:
                                op(lambda: v.tensor_scalar(
                                    fl_bias[d["ss"]][:], fl_m[d["hq"]][:],
                                    -SC, None, ALU.mult),
                                   ("mx",), "fl_bias")
                        v.tensor_copy(vdmy[:], vdmy[:])
                        v.tensor_copy(vdmy[:], vdmy[:]).then_inc(
                            E["sem"]["cV"], 1)
                    S.emit("v", fvc)
                    cvl = S.inc("cV")
                    if d is not None:
                        d["cvl"] = cvl
                    if p is not None:
                        p["cvl_flush"] = cvl
                    if d is not None:
                        _wait(S, "a", "cV", cvl)
                    if 1 <= i <= n:
                        _wait(S, "a", "cP", info[i - 1]["cpT"])
                    elif p is not None:
                        _wait(S, "a", "cP", S.val("cP"))

                    def fac(Ee, d=d, p=p):
                        E = Ee
                        a = E["a"]
                        if p is not None:
                            a.activation(ptsb[p["ts"]][:], ps_pts[p["ts"]][:],
                                         AF.Identity)
                        if d is not None:
                            if not d["isf"]:
                                a.activation(fl_al[d["ss"]][:],
                                             fl_dm[d["ss"]][:],
                                             AF.Exp, scale=SC)
                            a.activation(psb[d["ss"]][:], ps_s[d["ss"]][:],
                                         AF.Exp, bias=fl_bias[d["ss"]][:],
                                         scale=SC)
                        a.activation(admy[:], admy[:], AF.Identity)
                        a.activation(admy[:], admy[:], AF.Identity
                                     ).then_inc(E["sem"]["cA"], 1)
                    S.emit("a", fac)
                    ca = S.inc("cA")
                    if d is not None:
                        d["ca"] = ca
                        s_free[d["ss"]] = ca
                    if p is not None:
                        p["ca_evac"] = ca
                    if i + 1 < n and i >= 0:
                        prep_S(i + 1)
                    if d is not None:
                        _wait(S, "t", "cA", d["ca"])

                        def ftT(Ee, ss=d["ss"], tslot=d["ts"]):
                            E = Ee
                            for u in range(4):
                                ins = E["t"].matmul(
                                    ps_pts[tslot][:, u * 128:(u + 1) * 128],
                                    psb[ss][:, u * 128:(u + 1) * 128],
                                    ident_sb[:],
                                    start=True, stop=True,
                                )
                                if u == 3:
                                    ins.then_inc(E["sem"]["cP"], 1)
                        S.emit("t", ftT)
                        d["cpT"] = S.inc("cP")
                    if p is not None:
                        _wait(S, "t", "cV", p["cvl_flush"])
                        _wait(S, "t", "cA", p["ca_evac"])
                        _wait(S, "t", "cA", state.get("o_free_a", 0))

                        def ftPV(Ee, tslot=p["ts"], kvs=p["kvs"],
                                 hq=p["hq"], st=p["isf"]):
                            E = Ee
                            for u in range(4):
                                ins = E["t"].matmul(
                                    ps_os[hq][:, 0:128],
                                    ptsb[tslot][:, u * 128:(u + 1) * 128],
                                    vbuf[kvs][:, u * 128:(u + 1) * 128],
                                    start=(st and u == 0), stop=False,
                                    skip_group_check=True,
                                )
                                if u == 3:
                                    ins.then_inc(E["sem"]["cP"], 1)
                        S.emit("t", ftPV)
                        state["pv_cp_%d" % p["hq"]] = S.inc("cP")
                        kv_last_use[p["kvs"]] = S.val("cP")
                for hq in range(4):
                    _wait(S, "v", "cP", state.get("pv_cp_%d" % hq, 0))

                def fvr(Ee):
                    E = Ee
                    v = E["v"]
                    for hq in range(4):
                        v.reciprocal(fl_rl[hq][:], fl_l[hq][:])
                    v.tensor_copy(vdmy[:], vdmy[:])
                    v.tensor_copy(vdmy[:], vdmy[:]
                                  ).then_inc(E["sem"]["cV"], 1)
                S.emit("v", fvr)
                cvr = S.inc("cV")
                _wait(S, "a", "cV", cvr)

                _wait(S, "a", "dS", state.get("otdump_dS", 0))

                def fao(Ee, hg=hg):
                    E = Ee
                    for hq in range(4):
                        E["a"].activation(
                            otok[:, hg * 512 + hq * 128:
                                 hg * 512 + (hq + 1) * 128],
                            ps_os[hq][:, 0:128],
                            AF.Identity, scale=fl_rl[hq][:])
                    E["a"].activation(admy[:], admy[:], AF.Identity
                                      ).then_inc(E["sem"]["cA"], 1)
                S.emit("a", fao)
                o_free_a = S.inc("cA")
                state["o_free_a"] = o_free_a
            q_last_use[qslot] = S.val("cP")
            for g4 in range(4 if fl_lvl >= 8 else 0):
                _wait(S, "t", "cA", state.get("o_free_a", 0))
                _wait(S, "t", "cA", ofm_ca)

                def ftO(Ee, g4=g4):
                    E = Ee
                    for j in range(4):
                        hc = g4 * 4 + j
                        ins = E["t"].matmul(
                            ps_pt[:, j * 128:(j + 1) * 128],
                            otok[:, hc * 128:(hc + 1) * 128],
                            ident_sb[:],
                            start=True, stop=True,
                        )
                        if j == 3:
                            ins.then_inc(E["sem"]["cP"], 1)
                S.emit("t", ftO)
                cpO = S.inc("cP")
                _wait(S, "a", "cP", cpO)

                def faO(Ee, g4=g4):
                    E = Ee
                    for j in range(4):
                        E["a"].activation(
                            ofm[:, (4 * g4 + j) * 128:(4 * g4 + j + 1) * 128],
                            ps_pt[:, j * 128:(j + 1) * 128],
                            AF.Identity)
                    E["a"].activation(admy[:], admy[:], AF.Identity
                                      ).then_inc(E["sem"]["cA"], 1)
                S.emit("a", faO)
                ofm_ca = S.inc("cA")
            for dh in range(4 if fl_lvl >= 8 else 0):
                _wait(S, "t", "cA", ofm_ca)
                _wait(S, "t", "dG", state["wo_loaded_dG"])
                _wait(S, "t", "cV", wo_free)

                def ftW(Ee, dh=dh):
                    E = Ee
                    for hc in range(16):
                        ins = E["t"].matmul(
                            ps_wo[:],
                            ofm[:, hc * 128:(hc + 1) * 128],
                            wo_sb[:, hc * 2048 + dh * 512:
                                  hc * 2048 + (dh + 1) * 512],
                            start=(hc == 0), stop=(hc == 15),
                        )
                        if hc == 15:
                            ins.then_inc(E["sem"]["cP"], 1)
                S.emit("t", ftW)
                cpW = S.inc("cP")
                _wait(S, "v", "cP", cpW)
                _wait(S, "v", "dS", x1_loaded[xslot])

                def fvW(Ee, dh=dh, xslot=xslot):
                    E = Ee
                    xs = x1t[xslot][:, dh * 512:(dh + 1) * 512]
                    E["v"].tensor_add(xs, ps_wo[:], xs)
                    E["v"].tensor_copy(vdmy[:], vdmy[:]).then_inc(
                        E["sem"]["cV"], 1)
                S.emit("v", fvW)
                wo_free = S.inc("cV")
            if fl_lvl < 8:
                continue
            _wait(S, "s", "cV", wo_free)

            def fsx2(Ee, qt=qt, xslot=xslot):
                Ee["s"].dma_start(
                    out=x1_dram[qt * 128:(qt + 1) * 128, :],
                    in_=x1t[xslot][:],
                ).then_inc(Ee["sem"]["dS"], 16)
            S.emit("s", fsx2)
            x1_free[xslot] = S.val("cV")
            S.inc("dS", 16)
        state["x2_stored_dS"] = S.val("dS")
        state["attn_cP"] = S.val("cP")
        state["attn_cV"] = S.val("cV")
        state["attn_cA"] = S.val("cA")

    def emit_x2_reload():
        _wait(S, "s", "dS", state["x2_stored_dS"])
        _wait(S, "s", "cP", state["attn_cP"])
        _wait(S, "s", "cV", state["attn_cV"])
        _wait(S, "s", "cA", state["attn_cA"])
        for tc in range(NTC):
            def fs(Ee, tc=tc):
                Ee["s"].dma_start(
                    out=x_sb[:, tc * 2048:(tc + 1) * 2048],
                    in_=x1_dram[tc * 128:(tc + 1) * 128, :],
                ).then_inc(Ee["sem"]["dS"], 16)
            S.emit("s", fs)
            S.inc("dS", 16)
        state["x_loaded_dS"] = S.val("dS")

    def emit_output(h):
        _wait(S, "s", "cV", state["ffn_done_cV"])
        for tl in range(8):
            tcg = h * 8 + tl
            def fs(Ee, tcg=tcg):
                Ee["s"].dma_start(
                    out=out[tcg * 128:(tcg + 1) * 128, :],
                    in_=x_sb[:, tcg * 2048:(tcg + 1) * 2048],
                ).then_inc(Ee["sem"]["dS"], 16)
            S.emit("s", fs)
            S.inc("dS", 16)

    def emit_dump_and_finish():
        cv = S.val("cV")
        ca = S.val("cA")
        cp = S.val("cP")
        _wait(S, "s", "cV", cv)
        _wait(S, "s", "cA", ca)
        _wait(S, "s", "cP", cp)
        for tc in range(NTC):
            def fs(Ee, tc=tc):
                Ee["s"].dma_start(
                    out=out[tc * 128:(tc + 1) * 128, :],
                    in_=x_sb[:, tc * 2048:(tc + 1) * 2048],
                ).then_inc(Ee["sem"]["dS"], 16)
            S.emit("s", fs)
            S.inc("dS", 16)
        tot = S.val("dS")

        def ff(Ee):
            Ee["s"].wait_ge(Ee["sem"]["dS"], tot)
        S.emit("s", ff)

    phase0()

    if stop_after in ("ln", "router"):
        with ExitStack() as ps_ctx:
            e_ = ps_ctx.enter_context
            ps_tr = e_(nc.psum_tensor("ps_tr_dbg", [128, 512], BF16))
            ps_sc = e_(nc.psum_tensor("ps_sc_dbg", [128, 512], F32))
            emit_ln(0, ps_tr, xn_tok_f)
            if stop_after == "router":
                emit_router(0, ps_sc)
        emit_dump_and_finish()
        return _assemble(nc, S)

    for h in range(2):
        with ExitStack() as ps_ctx:
            e_ = ps_ctx.enter_context
            ps_g = e_(nc.psum_tensor(f"ps_g_{h}", [128, 512], F32))
            ps_g2 = e_(nc.psum_tensor(f"ps_g2_{h}", [128, 512], F32))
            ps_u = e_(nc.psum_tensor(f"ps_u_{h}", [128, 512], F32))
            dacc0 = e_(nc.psum_tensor(f"dacc0_{h}", [128, 512], F32))
            dacc1 = e_(nc.psum_tensor(f"dacc1_{h}", [128, 512], F32))
            ps_tr = e_(nc.psum_tensor(f"ps_tr_{h}", [128, 512], BF16))
            ps_sc = e_(nc.psum_tensor(f"ps_sc_{h}", [128, 512], F32))
            emit_ln(h, ps_tr, xn_tok_f)
            emit_router(h, ps_sc)
            emit_experts(h, (ps_g, ps_g2), ps_u, (dacc0, dacc1),
                         (tmp_ev, ps_sc))
        state["arena_free_cP"] = S.val("cP")

    if stop_after == "ffn1":
        _wait(S, "s", "cV", S.val("cV"))
        _wait(S, "s", "cA", S.val("cA"))
        _wait(S, "s", "cP", S.val("cP"))

        def fdbg(Ee):
            Ee["s"].dma_start(out=dbg_cn[:], in_=x_sb[:, 0:1024]
                              ).then_inc(Ee["sem"]["dS"], 16)
            Ee["s"].dma_start(out=dbg_sc[:, 0:256], in_=scores_sb[:]
                              ).then_inc(Ee["sem"]["dS"], 16)
            Ee["s"].dma_start(out=dbg_sc[:, 256:512], in_=c_tok[:]
                              ).then_inc(Ee["sem"]["dS"], 16)
            Ee["s"].dma_start(out=dbg_xn[:], in_=xn_sb[:, 0:2048]
                              ).then_inc(Ee["sem"]["dS"], 16)
        S.emit("s", fdbg)
        S.inc("dS", 64)
        emit_dump_and_finish()
        return _assemble(nc, S)

    state["rope_dS"] = 0
    _wait(S, "s", "cP", state["ffn_done_cP"])
    _wait(S, "s", "cV", state["ffn_done_cV"])

    def f_rope(Ee):
        E = Ee
        E["s"].dma_start(out=ropec_sb[:], in_=ropec_in).then_inc(E["sem"]["dS"], 16)
        E["s"].dma_start(out=ropes_sb[:], in_=ropes_in).then_inc(E["sem"]["dS"], 16)
    S.emit("s", f_rope)
    state["rope_dS"] = S.inc("dS", 32)

    def f_pid(Ee):
        pid = nc.partition_id([mybir.EngineType.SP])
        state["par"] = nc.snap(pid % 2, engines=[mybir.EngineType.SP],
                               min_val=0, max_val=1)
        state["parx"] = nc.snap((pid + 1) % 2, engines=[mybir.EngineType.SP],
                                min_val=0, max_val=1)
    S.emit("s", f_pid)

    for h in range(2):
        with ExitStack() as ps_ctx:
            e_ = ps_ctx.enter_context
            ps_tr = e_(nc.psum_tensor(f"ps_tr_q{h}", [128, 512], BF16))
            ps_qk = [e_(nc.psum_tensor(f"ps_qk{i}_{h}", [128, 512], F32))
                     for i in range(2)]
            ps_v = [e_(nc.psum_tensor(f"ps_v{i}_{h}", [128, 512], F32))
                    for i in range(2)]
            state["qkv_start_cV"] = S.val("cV")
            emit_ln(h, ps_tr, xn_tok_q)
            emit_x1_spill(h, to_dbg=True)
            emit_qkv(h, ps_qk, ps_v, None)
    state["qkv_end_cA"] = S.val("cA")
    state["qkv_end_cV"] = S.val("cV")
    emit_flag_sync()
    if stop_after == "qkv":
        emit_dump_and_finish()
        return _assemble(nc, S)
    emit_attn_overlay_loads()

    if paranoid >= 2:
        S.paranoid = True
    with ExitStack() as ps_ctx:
        e_ = ps_ctx.enter_context
        ps_s = [e_(nc.psum_tensor(f"ps_s{i}", [128, 512], F32)) for i in range(2)]
        ps_pt = e_(nc.psum_tensor("ps_pt", [128, 512], F32))
        ps_os = [e_(nc.psum_tensor(f"ps_o{i}", [128, 512], F32))
                 for i in range(4)]
        ps_wo = e_(nc.psum_tensor("ps_wo", [128, 512], F32))
        emit_attention(ps_s, ps_pt, ps_os, ps_wo)
    S.paranoid = False

    emit_x2_reload()
    if stop_after == "attn":
        emit_dump_and_finish()
        return _assemble(nc, S)

    state["arena_free_cP"] = S.val("cP")
    for h in range(2):
        with ExitStack() as ps_ctx:
            e_ = ps_ctx.enter_context
            ps_g = e_(nc.psum_tensor(f"ps2_g_{h}", [128, 512], F32))
            ps_g2 = e_(nc.psum_tensor(f"ps2_g2_{h}", [128, 512], F32))
            ps_u = e_(nc.psum_tensor(f"ps2_u_{h}", [128, 512], F32))
            dacc0 = e_(nc.psum_tensor(f"dacc2_0_{h}", [128, 512], F32))
            dacc1 = e_(nc.psum_tensor(f"dacc2_1_{h}", [128, 512], F32))
            ps_tr = e_(nc.psum_tensor(f"ps2_tr_{h}", [128, 512], BF16))
            ps_sc = e_(nc.psum_tensor(f"ps2_sc_{h}", [128, 512], F32))
            emit_ln(h, ps_tr, xn_tok_f)
            emit_router(h, ps_sc)
            emit_experts(h, (ps_g, ps_g2), ps_u, (dacc0, dacc1),
                         (tmp_ev, ps_sc))
            emit_output(h)
        state["arena_free_cP"] = S.val("cP")

    total_dS = S.val("dS")

    def f_final(Ee):
        Ee["s"].wait_ge(Ee["sem"]["dS"], total_dS)
    S.emit("s", f_final)
    return _assemble(nc, S)


def _assemble(nc, S):
    with ExitStack() as ctx:
        e_ = ctx.enter_context
        sems = {}
        for sn in ("dS", "dG", "dP", "cP", "cV", "cA"):
            sems[sn] = e_(nc.semaphore(sn))
        block = e_(nc.Block())

        def run(eng_key):
            def runner(eng):
                Ee = {"t": nc.tensor, "v": nc.vector, "a": nc.scalar,
                      "g": nc.gpsimd, "s": nc.sync, "sem": sems}
                for fn in S.ops[eng_key]:
                    fn(Ee)
            return runner

        block.tensor(run("t"))
        block.vector(run("v"))
        block.scalar(run("a"))
        block.gpsimd(run("g"))
        block.sync(run("s"))
    return nc




def kernel(L0_attn_x, L0_attn_Wqkv, L0_attn_Wo, L0_ffn_router_w,
           L0_ffn_gate_w, L0_ffn_up_w, L0_ffn_down_w):
    x = np.ascontiguousarray(np.asarray(L0_attn_x, dtype=np.float32))
    wqkv = np.ascontiguousarray(np.asarray(L0_attn_Wqkv, dtype=np.float32))
    wo = np.ascontiguousarray(np.asarray(L0_attn_Wo, dtype=np.float32))
    rw = np.ascontiguousarray(np.asarray(L0_ffn_router_w, dtype=np.float32))
    gw = np.ascontiguousarray(np.asarray(L0_ffn_gate_w, dtype=np.float32))
    uw = np.ascontiguousarray(np.asarray(L0_ffn_up_w, dtype=np.float32))
    dwn = np.ascontiguousarray(np.asarray(L0_ffn_down_w, dtype=np.float32))

    nc = build_nc()
    nonce = np.array([[np.random.randint(1, 2**31)] * 16], dtype=np.uint32)
    consts = [_host_consts(p) for p in range(2)]
    in_maps = []
    for c in range(8):
        b, p = c // 2, c % 2
        ropec, ropes, cmask, ident = consts[p]
        xs = np.ascontiguousarray(x[b, p::2, :])
        in_maps.append({
            "x0": xs, "wqkv": wqkv, "wo": wo, "rw": rw,
            "gw": gw, "uw": uw, "dw": dwn,
            "ropec": ropec, "ropes": ropes, "cmask": cmask,
            "ident": ident, "nonce": nonce,
        })
    res = run_bass_kernel_spmd(nc, in_maps, core_ids=list(range(8)))
    outp = np.empty((4, 4096, 2048), dtype=np.float32)
    for c in range(8):
        b, p = c // 2, c % 2
        outp[b, p::2, :] = np.asarray(res.results[c]["out"])
    return outp
